# version 13
# baseline (speedup 1.0000x reference)
import contextlib
import numpy as np
import concourse.bass as bass
import concourse.mybir as mybir
from concourse.bass_core_rust import IndirectOffsetOnAxis
from concourse.bass_utils import run_bass_kernel_spmd

F32 = mybir.dt.float32
F16 = mybir.dt.float16
BF16 = mybir.dt.bfloat16
I32 = mybir.dt.int32
AF = mybir.ActivationFunctionType
ALU = mybir.AluOpType

NCORES = 8
B, T, C = 16, 2048, 1024
NSEQ = B // NCORES
FB = C // 128
HID = 4096
HB = HID // 128
NE = 8
MH = 2048
CH = 512
NCH = T // CH
NTOK = NSEQ * T

SNN_B = (0.24419428096993973, 0.4639691338428854, 0.6861859295255306,
         0.9059607823984763, 1.0839784132255623)

ALL_STAGES = ("f0", "rwkv", "f0", "moe", "attn", "moe")
BIG_N = 100 * 1024


class Ctx:
    def __init__(self, nc):
        self.nc = nc
        self.E = {"t": nc.tensor, "v": nc.vector, "s": nc.scalar,
                  "g": nc.gpsimd, "q": nc.sync}
        self.sem = {}
        self.cnt = {}
        self.waited = {}

    DMA_STREAMS = ("ini", "xld0", "xld1", "xst", "wt0", "wt1", "wA0", "wA1",
                   "wB0", "wB1", "wC0", "wC1", "ms0", "ms1", "ms2", "ms3",
                   "ms4", "cc0", "cc1", "pcw", "wD0", "wD1", "fin")

    def init_sems(self, stack):
        for name in ("t", "v", "s", "g", "q") + self.DMA_STREAMS:
            self.sem[name] = stack.enter_context(self.nc.semaphore("sem_" + name))
            self.cnt[name] = 0

    def tok(self, name):
        return (name, self.cnt[name])

    def mark(self, name, inst, n=1):
        self.cnt[name] += n
        inst.then_inc(self.sem[name], n)
        return (name, self.cnt[name])

    def wait(self, waiter, *tokens):
        for tk in tokens:
            if tk is None:
                continue
            sname, v = tk
            if v <= 0:
                continue
            key = (waiter, sname)
            if self.waited.get(key, 0) >= v:
                continue
            self.waited[key] = v
            self.E[waiter].wait_ge(self.sem[sname], v)

    def op(self, eng, emit, *deps):
        self.wait(eng, self.tok(eng), *deps)
        return self.mark(eng, emit())

    def dma(self, qname, stream, out, in_, deps=(), transpose=False,
            in_offset=None, out_offset=None, **kw):
        eng = self.E[qname]
        self.wait(qname, *deps)
        if transpose:
            inst = eng.dma_start_transpose(out, in_)
        elif in_offset is not None or out_offset is not None:
            inst = eng.indirect_dma_start(out=out, out_offset=out_offset,
                                          in_=in_, in_offset=in_offset, **kw)
        else:
            inst = eng.dma_start(out=out, in_=in_, **kw)
        return self.mark(stream, inst, 16)

    def barrier(self):
        toks = [self.tok(s) for s in self.DMA_STREAMS]
        for name in ("t", "v", "s", "g", "q"):
            self.wait(name, self.tok(name), *toks)
            toks.append(self.mark(name, self.E[name].nop()))
        for name in ("t", "v", "s", "g", "q"):
            self.wait(name, *toks)


class Arena:

    def __init__(self, big):
        self.big = big
        self.off = 0
        self._marks = []

    def push(self):
        self._marks.append(self.off)

    def pop(self):
        self.off = self._marks.pop()

    def bf(self, n):
        o = self._align(2)
        self.off = o + 2 * n
        assert self.off <= 2 * BIG_N, f"SBUF arena overflow: {self.off}"
        return self.big[:, o // 2: o // 2 + n]

    def f32(self, n):
        o = self._align(4)
        self.off = o + 4 * n
        assert self.off <= 2 * BIG_N, f"SBUF arena overflow: {self.off}"
        return self.big.bitcast(F32)[:, o // 4: o // 4 + n]

    def f16(self, n):
        o = self._align(4)
        self.off = o + 2 * n
        assert self.off <= 2 * BIG_N, f"SBUF arena overflow: {self.off}"
        return self.big.bitcast(F16)[:, o // 2: o // 2 + n]

    def i32(self, n):
        o = self._align(4)
        self.off = o + 4 * n
        assert self.off <= 2 * BIG_N, f"SBUF arena overflow: {self.off}"
        return self.big.bitcast(I32)[:, o // 4: o // 4 + n]

    def _align(self, a):
        return (self.off + a - 1) // a * a


def ln_chunk(ctx, seq, ch, xs, xts, xnts, gain=None, bias=None, src=None,
             xts_free=(), apply_deps=()):
    nc, pb = ctx.nc, ctx.pb
    t0 = ch * CH
    if src is None:
        lts = [None, None]
        for fb in range(FB):
            lts[fb % 2] = ctx.dma(
                "q", "xld" + str(fb % 2), xts[fb],
                xs.ap()[seq][fb * 128:(fb + 1) * 128, t0:t0 + CH],
                deps=xts_free if fb < 2 else ())
        ldtoks = [lts[fb % 2] for fb in range(FB)]
    else:
        ldtoks = [src] * FB
    sqt = [None, None]
    tsqs = []
    for fb in range(FB):
        sq = ctx.lnsq[:, (fb % 2) * CH:(fb % 2) * CH + CH]
        tsq = ctx.op("s", lambda sq=sq, fb=fb: nc.scalar.activation(
            sq, xts[fb], AF.Square), ldtoks[fb], sqt[fb % 2], ctx.lnsq_guard,
            *ctx.ln_prior)
        tsqs.append(tsq)
        ctx.wait("t", ctx.tok("t"), ldtoks[fb], tsq,
                 *(ctx.ln_prior if fb == 0 else ()))
        nc.tensor.matmul(pb[4][0:1, 0:CH], ctx.c_rsum, xts[fb],
                         start=(fb == 0), stop=(fb == FB - 1))
        i2 = nc.tensor.matmul(pb[5][0:1, 0:CH], ctx.c_rsum, sq,
                              start=(fb == 0), stop=(fb == FB - 1))
        sqt[fb % 2] = ctx.mark("t", i2)
    tstat = ctx.tok("t")
    mu2 = ctx.lnrows[0:1, 0:CH]
    rstd = ctx.lnrows[0:1, CH:2 * CH]
    mrs = ctx.lnrows[0:1, 2 * CH:3 * CH]
    ts0 = ctx.op("s", lambda: nc.scalar.activation(mu2, pb[4][0:1, 0:CH],
                                                   AF.Square), tstat, *ctx.ln_prior)
    tv = ctx.op("v", lambda: nc.vector.tensor_sub(mu2, pb[5][0:1, 0:CH], mu2),
                ts0, tstat)
    ts = ctx.op("s", lambda: nc.scalar.activation(rstd, mu2, AF.Ln,
                                                  bias=ctx.c_eps[0:1, :]), tv)
    tr = ctx.op("s", lambda: nc.scalar.activation(rstd, rstd, AF.Exp,
                                                  scale=-0.5), ts)
    tv2 = ctx.op("v", lambda: nc.vector.tensor_mul(mrs, pb[4][0:1, 0:CH], rstd), tr)
    ctx.wait("t", ctx.tok("t"), tv2, ts0)
    ib = None
    for j in range(0, CH, 512):
        w = min(512, CH - j)
        nc.tensor.matmul(pb[4][:, j:j + w], ctx.c_brow,
                         rstd[0:1, j:j + w], start=True, stop=True)
        ib = nc.tensor.matmul(pb[5][:, j:j + w], ctx.c_brow,
                              mrs[0:1, j:j + w], start=True, stop=True)
    tb = ctx.mark("t", ib)
    outs = []
    t1tok = [tsqs[-2], tsqs[-1]]
    for fb in range(FB):
        t1 = ctx.lnsq[:, 0:CH] if fb % 2 == 0 else ctx.lnsq[:, CH:2 * CH]
        tm1 = ctx.op("v", lambda t1=t1, fb=fb: nc.vector.tensor_mul(
            t1, xts[fb], pb[4][:, 0:CH]), tb, t1tok[fb % 2],
            *(apply_deps if fb == 0 else ()))
        if gain is None:
            iv = ctx.op("v", lambda t1=t1, fb=fb: nc.vector.tensor_sub(
                xnts[fb], t1, pb[5][:, 0:CH]), tm1)
            outs.append(iv)
            t1tok[fb % 2] = iv
        else:
            tv3 = ctx.op("v", lambda t1=t1: nc.vector.tensor_sub(
                t1, t1, pb[5][:, 0:CH]), tm1)
            si = ctx.op("v", lambda t1=t1, fb=fb: nc.vector.tensor_scalar(
                xnts[fb], t1, gain[:, fb:fb + 1], bias[:, fb:fb + 1],
                ALU.mult, ALU.add), tv3)
            outs.append(si)
            t1tok[fb % 2] = si
    ctx.ln_prior = [ctx.tok("v"), ctx.tok("s")]
    return outs, ldtoks


def emit_ffn(ctx, dt, xs):
    nc, pb = ctx.nc, ctx.pb
    ctx.barrier()
    A = ctx.arena
    A.push()
    wpr = A.bf(HB * C)
    wfcq = [A.bf(FB * 1024) for _ in range(2)]
    h = A.bf(HB * CH)
    xn = [A.bf(CH) for _ in range(FB)]
    rt = [A.bf(CH) for _ in range(2)]
    xts = [A.f32(CH) for _ in range(FB)]
    yt = A.f32(CH)
    acc = A.f32(CH)
    ctx.lnsq = A.f32(2 * CH)
    ctx.lnrows = A.f32(3 * CH)

    twpr = None
    for hb in range(HB):
        twpr = ctx.dma("g", "wt0", wpr[:, hb * C:(hb + 1) * C],
                       dt["wpr"].ap()[hb * 128:(hb + 1) * 128, :])
    ctx.wait("q", *ctx.stream_ready)

    nq_total = NSEQ * NCH * 4

    def load_wfcq(qg, dep):
        buf = wfcq[qg % 2]
        q = qg % 4
        tq = None
        for fb in range(FB):
            tq = ctx.dma("g", "wA" + str(qg % 2),
                         buf[:, fb * 1024:(fb + 1) * 1024],
                         dt["wfc"].ap()[fb * 128:(fb + 1) * 128,
                                        q * 1024:(q + 1) * 1024],
                         deps=(dep,) if fb == 0 else ())
        return tq

    wfct = {0: load_wfcq(0, None), 1: load_wfcq(1, None)}
    qcons = {}
    evt = {}
    ebt = {}
    ybt = {}
    rtt = [None, None]
    resid_tok = []
    for seq in range(NSEQ):
        for ch in range(NCH):
            cidx = seq * NCH + ch
            xn_toks, _ = ln_chunk(ctx, seq, ch, xs, xts, xn,
                                  xts_free=resid_tok[-1:])
            resid_tok = []
            for q in range(4):
                qg = cidx * 4 + q
                ctx.wait("t", wfct[qg], *(xn_toks if q == 0 else ()))
                for k8 in range(8):
                    hb = q * 8 + k8
                    bank = pb[hb % 4]
                    ctx.wait("t", ctx.tok("t"), ebt.get(hb % 4))
                    for fb in range(FB):
                        im = nc.tensor.matmul(
                            bank[:, 0:CH],
                            wfcq[qg % 2][:, fb * 1024 + k8 * 128:
                                         fb * 1024 + k8 * 128 + 128],
                            xn[fb], start=(fb == 0), stop=(fb == FB - 1))
                    tm = ctx.mark("t", im)
                    tmx = ctx.op("v", lambda p=hb % 2, bank=bank:
                                 nc.vector.tensor_scalar_max(
                                     rt[p], bank[:, 0:CH], 0.0),
                                 tm, rtt[hb % 2])
                    tie = ctx.op("s", lambda p=hb % 2, hb=hb:
                                 nc.scalar.activation(
                                     h[:, hb * CH:(hb + 1) * CH], rt[p],
                                     AF.Square), tmx, evt.get(hb))
                    evt[hb] = ebt[hb % 4] = rtt[hb % 2] = tie
                qcons[qg] = ctx.tok("t")
                if qg + 2 < nq_total:
                    wfct[qg + 2] = load_wfcq(qg + 2, qcons[qg])
            for cb in range(FB):
                ybank = pb[6 + cb % 2]
                ctx.wait("t", ctx.tok("t"), twpr, ybt.get(cb % 2))
                for hb in range(HB):
                    ctx.wait("t", evt[hb])
                    im = nc.tensor.matmul(
                        ybank[:, 0:CH],
                        wpr[:, hb * C + cb * 128: hb * C + cb * 128 + 128],
                        h[:, hb * CH:(hb + 1) * CH],
                        start=(hb == 0), stop=(hb == HB - 1))
                tm = ctx.mark("t", im)
                tv = ctx.op("v", lambda ybank=ybank: nc.vector.tensor_copy(
                    yt, ybank[:, 0:CH]), tm)
                ybt[cb % 2] = tv
                ta = ctx.op("v", lambda: nc.vector.tensor_scalar(
                    acc, yt, SNN_B[0], None, ALU.is_gt), tv)
                for i in (1, 2, 3, 4):
                    ta = ctx.op("v", lambda i=i: nc.vector.scalar_tensor_tensor(
                        acc, yt, SNN_B[i], acc, ALU.is_gt, ALU.add), ta)
                tvx = ctx.op("v", lambda cb=cb: nc.vector.scalar_tensor_tensor(
                    xts[cb], acc, 0.1, xts[cb], ALU.mult, ALU.add), ta)
                resid_tok.append(ctx.dma(
                    "q", "xst", xs.ap()[seq][cb * 128:(cb + 1) * 128,
                                             ch * CH:(ch + 1) * CH], xts[cb],
                    deps=(tvx,)))
    ctx.stream_ready = (ctx.tok("xst"), ctx.tok("ms3"))
    A.pop()


def emit_rwkv(ctx, dt, xs, xs2):
    nc, pb = ctx.nc, ctx.pb
    ctx.barrier()
    A = ctx.arena
    sm = ctx.smal
    drv = ctx.drv

    A.push()
    wk = A.bf(FB * C); wv = A.bf(FB * C); wr = A.bf(FB * C); wo = A.bf(FB * C)
    xts = [A.f32(CH) for _ in range(FB)]
    xp = xts
    xn1 = [A.bf(CH + 1) for _ in range(FB)]
    xk = [A.bf(CH) for _ in range(FB)]
    xv = [A.bf(CH) for _ in range(FB)]
    xr = [A.bf(CH) for _ in range(FB)]
    kt = [A.f32(CH) for _ in range(FB)]
    vt = [A.f32(CH) for _ in range(FB)]
    sr = [A.bf(CH) for _ in range(FB)]
    rw = [A.bf(CH) for _ in range(FB)]
    At = A.f32(CH + 1); Bt = A.f32(CH + 1)
    ab_st = A.f32(2 * FB)
    ctx.lnsq = A.f32(2 * CH)
    ctx.lnrows = A.f32(3 * CH)

    tw = None
    for w_ap, nm in ((wk, "wk"), (wv, "wv"), (wr, "wr"), (wo, "wo")):
        for fb in range(FB):
            tw = ctx.dma("g", "wt0", w_ap[:, fb * C:(fb + 1) * C],
                         dt[nm].ap()[fb * 128:(fb + 1) * 128, :])
    ctx.wait("q", *ctx.stream_ready)

    ebt = {}
    obt = {}
    scan_dep = []
    r1_resid = []
    for seq in range(NSEQ):
        tz = ctx.op("v", lambda: nc.vector.memset(ab_st[:, 0:2 * FB], 0.0),
                    *scan_dep)
        zcols = []
        for fb in range(FB):
            zcols.append(ctx.op("v", lambda fb=fb: nc.vector.memset(
                xn1[fb][:, 0:1], 0.0)))
        for ch in range(NCH):
            xp_toks, _ = ln_chunk(ctx, seq, ch, xs, xts, xp,
                                  xts_free=r1_resid[-1:])
            if ch > 0:
                for fb in range(FB):
                    ctx.op("v", lambda fb=fb: nc.vector.tensor_copy(
                        xn1[fb][:, 0:1], xn1[fb][:, CH:CH + 1]))
            xn_views = [xn1[fb][:, 1:1 + CH] for fb in range(FB)]
            t_src = xp_toks[-1]
            xn_toks, _ = ln_chunk(ctx, seq, ch, xs, xp, xn_views,
                                  gain=sm["ln1g"], bias=sm["ln1b"], src=t_src)
            mix_toks = []
            for fb in range(FB):
                td = ctx.op("v", lambda fb=fb: nc.vector.tensor_sub(
                    rw[fb], xn1[fb][:, 1:1 + CH], xn1[fb][:, 0:CH]), xn_toks[fb])
                ctx.op("v", lambda fb=fb: nc.vector.scalar_tensor_tensor(
                    xk[fb], rw[fb], sm["tmk"][:, fb:fb + 1], xn1[fb][:, 0:CH],
                    ALU.mult, ALU.add), td)
                ctx.op("v", lambda fb=fb: nc.vector.scalar_tensor_tensor(
                    xv[fb], rw[fb], sm["tmv"][:, fb:fb + 1], xn1[fb][:, 0:CH],
                    ALU.mult, ALU.add))
                tm3 = ctx.op("v", lambda fb=fb: nc.vector.scalar_tensor_tensor(
                    xr[fb], rw[fb], sm["tmr"][:, fb:fb + 1], xn1[fb][:, 0:CH],
                    ALU.mult, ALU.add))
                mix_toks.append(tm3)
            proj_ev = []
            for w_ap, xin, which in ((wk, xk, "k"), (wv, xv, "v"), (wr, xr, "r")):
                for ob in range(FB):
                    bank = pb[(ob % 4)]
                    ctx.wait("t", ctx.tok("t"), tw, ebt.get(ob % 4),
                             mix_toks[-1])
                    for fb in range(FB):
                        im = nc.tensor.matmul(
                            bank[:, 0:CH],
                            w_ap[:, fb * C + ob * 128: fb * C + ob * 128 + 128],
                            xin[fb], start=(fb == 0), stop=(fb == FB - 1))
                    tm = ctx.mark("t", im)
                    if which == "k":
                        ev = ctx.op("s", lambda ob=ob, bank=bank:
                                    nc.scalar.copy(kt[ob], bank[:, 0:CH]),
                                    tm)
                    elif which == "v":
                        ev = ctx.op("v", lambda ob=ob, bank=bank:
                                    nc.vector.tensor_copy(vt[ob], bank[:, 0:CH]),
                                    tm)
                    else:
                        ev = ctx.op("s", lambda ob=ob, bank=bank:
                                    nc.scalar.activation(sr[ob], bank[:, 0:CH],
                                                         AF.Sigmoid), tm)
                    ebt[ob % 4] = ev
                    proj_ev.append(ev)
            Et = ctx.lnsq[:, 0:CH]
            EVt = ctx.lnsq[:, CH:2 * CH]
            rw_toks = []
            for fb in range(FB):
                te = ctx.op("s", lambda fb=fb: nc.scalar.activation(
                    Et, kt[fb], AF.Exp), proj_ev[fb], ctx.ln_prior[0],
                    *( [rw_toks[-1]] if fb else []))
                tev = ctx.op("v", lambda fb=fb: nc.vector.tensor_mul(
                    EVt, Et, vt[fb]), te, proj_ev[FB + fb])
                ca = ctx.op("v", lambda fb=fb: nc.vector.tensor_copy(
                    At[:, 0:1], ab_st[:, fb:fb + 1]), tz)
                ta = ctx.op("v", lambda fb=fb: nc.vector.tensor_tensor_scan(
                    At[:, 1:1 + CH], drv["wdec"][:, fb:fb + 1].to_broadcast([128, CH]),
                    EVt, ab_st[:, fb:fb + 1], ALU.mult, ALU.add), tev, ca,
                    ctx.drv_tok)
                cb_ = ctx.op("v", lambda fb=fb: nc.vector.tensor_copy(
                    Bt[:, 0:1], ab_st[:, FB + fb:FB + fb + 1]))
                tb_ = ctx.op("v", lambda fb=fb: nc.vector.tensor_tensor_scan(
                    Bt[:, 1:1 + CH], drv["wdec"][:, fb:fb + 1].to_broadcast([128, CH]),
                    Et, ab_st[:, FB + fb:FB + fb + 1], ALU.mult, ALU.add), cb_)
                ctx.op("v", lambda fb=fb: nc.vector.tensor_copy(
                    ab_st[:, fb:fb + 1], At[:, CH:CH + 1]), ta)
                ctx.op("v", lambda fb=fb: nc.vector.tensor_copy(
                    ab_st[:, FB + fb:FB + fb + 1], Bt[:, CH:CH + 1]), tb_)
                tn = ctx.op("v", lambda fb=fb: nc.vector.scalar_tensor_tensor(
                    EVt, EVt, drv["eu"][:, fb:fb + 1], At[:, 0:CH],
                    ALU.mult, ALU.add), ta)
                tdn = ctx.op("v", lambda fb=fb: nc.vector.scalar_tensor_tensor(
                    Et, Et, drv["eu"][:, fb:fb + 1], Bt[:, 0:CH],
                    ALU.mult, ALU.add), tb_)
                trl = ctx.op("s", lambda: nc.scalar.activation(
                    Et, Et, AF.Ln), tdn)
                tr_ = ctx.op("s", lambda: nc.scalar.activation(
                    Et, Et, AF.Exp, scale=-1.0), trl)
                ty = ctx.op("v", lambda fb=fb: nc.vector.tensor_mul(
                    kt[fb], EVt, Et), tr_, tn)
                trw = ctx.op("v", lambda fb=fb: nc.vector.tensor_mul(
                    rw[fb], kt[fb], sr[fb]), ty, proj_ev[2 * FB + fb])
                rw_toks.append(trw)
            ctx.lnsq_guard = ctx.tok("v")
            for ob in range(FB):
                bank = pb[6 + ob % 2]
                ctx.wait("t", ctx.tok("t"), obt.get(ob % 2), rw_toks[-1])
                for fb in range(FB):
                    im = nc.tensor.matmul(
                        bank[:, 0:CH],
                        wo[:, fb * C + ob * 128: fb * C + ob * 128 + 128],
                        rw[fb], start=(fb == 0), stop=(fb == FB - 1))
                tm = ctx.mark("t", im)
                tv = ctx.op("v", lambda ob=ob, bank=bank: nc.vector.tensor_add(
                    xp[ob], xp[ob], bank[:, 0:CH]), tm)
                obt[ob % 2] = tv
                r1_resid.append(ctx.dma(
                    "q", "xst",
                    xs2.ap()[seq][ob * 128:(ob + 1) * 128,
                                  ch * CH:(ch + 1) * CH], xp[ob],
                    deps=(tv,)))
            scan_dep = [ctx.tok("xst")]
    A.pop()
    ctx.barrier()
    r1_done = (ctx.tok("xst"),)

    A.push()
    fwv = A.bf(16 * C * 2)
    fwkq = [A.bf(FB * 1024) for _ in range(2)]
    fwr = A.bf(FB * C)
    kk = A.bf(16 * CH * 2)
    xn2 = [A.bf(CH + 1) for _ in range(FB)]
    xk2 = [A.bf(CH) for _ in range(FB)]
    xr2 = [kk[:, fb * CH:(fb + 1) * CH] for fb in range(FB)]
    sr2 = [A.bf(CH) for _ in range(FB)]
    rt = [A.bf(CH) for _ in range(2)]
    xts2 = [A.f32(CH) for _ in range(FB)]
    ctx.lnsq = A.f32(2 * CH)
    ctx.lnrows = A.f32(3 * CH)
    tmp = ctx.lnsq[:, 0:CH]
    xo = ctx.lnsq[:, CH:2 * CH]

    tfv = None
    for hb in range(32):
        tfv = ctx.dma("g", "wt1", fwv[:, hb * C:(hb + 1) * C],
                      dt["fwv"].ap()[hb * 128:(hb + 1) * 128, :])
    for fb in range(FB):
        tfr = ctx.dma("g", "wt1", fwr[:, fb * C:(fb + 1) * C],
                      dt["fwr"].ap()[fb * 128:(fb + 1) * 128, :])
    tfr = ctx.tok("wt1")
    ctx.wait("q", *r1_done)

    nq_total = NSEQ * NCH * 4

    def load_fwkq(qg, dep):
        buf = fwkq[qg % 2]
        q = qg % 4
        tq = None
        for fb in range(FB):
            tq = ctx.dma("g", "wA" + str(qg % 2),
                         buf[:, fb * 1024:(fb + 1) * 1024],
                         dt["fwk"].ap()[fb * 128:(fb + 1) * 128,
                                        q * 1024:(q + 1) * 1024],
                         deps=(dep,) if fb == 0 else ())
        return tq

    fwkt = {0: load_fwkq(0, None), 1: load_fwkq(1, None)}
    ctx.xo_guard = None
    qcons = {}
    evt = {}
    ebt = {}
    ybt = {}
    rtt = [None, None]
    resid_tok = []
    for seq in range(NSEQ):
        zc = []
        for fb in range(FB):
            zc.append(ctx.op("v", lambda fb=fb: nc.vector.memset(
                xn2[fb][:, 0:1], 0.0), *resid_tok[-1:]))
        for ch in range(NCH):
            cidx = seq * NCH + ch
            xn_toks, _ = ln_chunk(ctx, seq, ch, xs2, xts2,
                                  [xn2[fb][:, 1:1 + CH] for fb in range(FB)],
                                  gain=sm["ln2g"], bias=sm["ln2b"],
                                  xts_free=resid_tok[-1:])
            mix_toks = []
            for fb in range(FB):
                td = ctx.op("v", lambda fb=fb: nc.vector.tensor_sub(
                    rt[0], xn2[fb][:, 1:1 + CH], xn2[fb][:, 0:CH]), xn_toks[fb],
                    rtt[0])
                ctx.op("v", lambda fb=fb: nc.vector.scalar_tensor_tensor(
                    xk2[fb], rt[0], sm["ftmk"][:, fb:fb + 1], xn2[fb][:, 0:CH],
                    ALU.mult, ALU.add), td)
                tm2 = ctx.op("v", lambda fb=fb: nc.vector.scalar_tensor_tensor(
                    xr2[fb], rt[0], sm["ftmr"][:, fb:fb + 1], xn2[fb][:, 0:CH],
                    ALU.mult, ALU.add))
                rtt[0] = tm2
                mix_toks.append(tm2)
            for fb in range(FB):
                ctx.op("v", lambda fb=fb: nc.vector.tensor_copy(
                    xn2[fb][:, 0:1], xn2[fb][:, CH:CH + 1]), mix_toks[-1])
            sh_tok = ctx.tok("v")
            sr_toks = []
            for ob in range(FB):
                bank = pb[6 + ob % 2]
                ctx.wait("t", ctx.tok("t"), tfr, ybt.get(ob % 2), mix_toks[-1])
                for fb in range(FB):
                    im = nc.tensor.matmul(
                        bank[:, 0:CH],
                        fwr[:, fb * C + ob * 128: fb * C + ob * 128 + 128],
                        xr2[fb], start=(fb == 0), stop=(fb == FB - 1))
                tm = ctx.mark("t", im)
                ts_ = ctx.op("s", lambda ob=ob, bank=bank: nc.scalar.activation(
                    sr2[ob], bank[:, 0:CH], AF.Sigmoid), tm)
                ybt[ob % 2] = ts_
                sr_toks.append(ts_)
            for q in range(4):
                qg = cidx * 4 + q
                ctx.wait("t", fwkt[qg])
                for k8 in range(8):
                    hb = q * 8 + k8
                    bank = pb[hb % 4]
                    ctx.wait("t", ctx.tok("t"), ebt.get(hb % 4))
                    for fb in range(FB):
                        im = nc.tensor.matmul(
                            bank[:, 0:CH],
                            fwkq[qg % 2][:, fb * 1024 + k8 * 128:
                                         fb * 1024 + k8 * 128 + 128],
                            xk2[fb], start=(fb == 0), stop=(fb == FB - 1))
                    tm = ctx.mark("t", im)
                    tmx = ctx.op("v", lambda p=hb % 2, bank=bank:
                                 nc.vector.tensor_scalar_max(
                                     rt[p], bank[:, 0:CH], 0.0),
                                 tm, rtt[hb % 2])
                    tie = ctx.op("s", lambda p=hb % 2, hb=hb:
                                 nc.scalar.activation(
                                     kk[:, hb * CH:(hb + 1) * CH], rt[p],
                                     AF.Square), tmx, evt.get(hb))
                    evt[hb] = ebt[hb % 4] = rtt[hb % 2] = tie
                qcons[qg] = ctx.tok("t")
                if qg + 2 < nq_total:
                    fwkt[qg + 2] = load_fwkq(qg + 2, qcons[qg])
            for cb in range(FB):
                ybank = pb[6 + cb % 2]
                tld = ctx.dma("q", "ms1", xo,
                              xs.ap()[seq][cb * 128:(cb + 1) * 128,
                                           ch * CH:(ch + 1) * CH],
                              deps=(ctx.xo_guard, *ctx.ln_prior))
                ctx.wait("t", ctx.tok("t"), tfv, ybt.get(cb % 2))
                for hb in range(32):
                    ctx.wait("t", evt[hb])
                    im = nc.tensor.matmul(
                        ybank[:, 0:CH],
                        fwv[:, hb * C + cb * 128: hb * C + cb * 128 + 128],
                        kk[:, hb * CH:(hb + 1) * CH],
                        start=(hb == 0), stop=(hb == 31))
                tm = ctx.mark("t", im)
                tv1 = ctx.op("v", lambda cb=cb, ybank=ybank: nc.vector.tensor_mul(
                    tmp, ybank[:, 0:CH], sr2[cb]), tm, sr_toks[cb])
                ybt[cb % 2] = tv1
                tv2 = ctx.op("v", lambda cb=cb: nc.vector.tensor_add(
                    xts2[cb], xts2[cb], tmp), tv1)
                tv3 = ctx.op("v", lambda cb=cb: nc.vector.tensor_add(
                    xts2[cb], xts2[cb], xo), tv2, tld)
                ctx.xo_guard = tv3
                ctx.lnsq_guard = tv3
                resid_tok.append(ctx.dma(
                    "q", "xst", xs.ap()[seq][cb * 128:(cb + 1) * 128,
                                             ch * CH:(ch + 1) * CH], xts2[cb],
                    deps=(tv3,)))
    ctx.stream_ready = (ctx.tok("xst"), ctx.tok("ms3"))
    A.pop()


def emit_attn(ctx, dt, xs):
    nc, pb = ctx.nc, ctx.pb
    ctx.barrier()
    A = ctx.arena
    A.push()
    wqkv = A.bf(FB * 3 * C)
    qa = A.bf(FB * T)
    ka = A.bf(FB * T)
    va = A.bf(16 * C)
    mask = A.bf(384)
    ident = A.bf(128)
    onesr = A.bf(128)
    vbias = A.bf(C)
    boutr = A.bf(C)

    tz1 = ctx.op("v", lambda: nc.vector.memset(mask[:, :], 0.0))
    tz2 = ctx.op("v", lambda: nc.vector.memset(ident[:, :], 1.0))
    tz3 = ctx.op("v", lambda: nc.vector.memset(onesr[0:1, :], 1.0))
    ti = ctx.op("g", lambda: nc.gpsimd.affine_select(
        ident[:, :], ident[:, :], pattern=[[-1, 128]], base=0,
        channel_multiplier=1, compare_op=ALU.is_equal, fill=0.0), tz2)
    tm1 = ctx.op("g", lambda: nc.gpsimd.affine_select(
        mask[:, :], mask[:, :], pattern=[[1, 384]], base=-64,
        channel_multiplier=-1, compare_op=ALU.is_ge, fill=-240000.0), tz1)
    tm2 = ctx.op("g", lambda: nc.gpsimd.affine_select(
        mask[:, :], mask[:, :], pattern=[[-1, 384]], base=192,
        channel_multiplier=1, compare_op=ALU.is_ge, fill=-240000.0), tm1)
    ctx.dma("g", "ms2", vbias[0:1, :], dt["bqkvr"].ap()[0:1, 2 * C:3 * C])
    tb1 = tb2 = ctx.dma("g", "ms2", boutr[0:1, :], dt["boutr"].ap())
    ctx.dbg = dict(qa=qa, ka=ka, va=va)
    ctx.dbg2 = {}
    tw = None
    for fb in range(FB):
        tw = ctx.dma("g", "wt0", wqkv[:, fb * 3 * C:(fb + 1) * 3 * C],
                     dt["wqkv"].ap()[fb * 128:(fb + 1) * 128, :])
    ctx.wait("q", *ctx.stream_ready)

    for seq in range(NSEQ):
        A.push()
        xn = [A.bf(CH) for _ in range(FB)]
        xts = [A.f32(CH) for _ in range(FB)]
        ctx.lnsq = A.f32(2 * CH)
        ctx.lnrows = A.f32(3 * CH)
        ebt = {}
        for ch in range(NCH):
            xn_toks, _ = ln_chunk(ctx, seq, ch, xs, xts, xn,
                                  xts_free=tuple(ctx.ln_prior))
            t0 = ch * CH
            for which, colo, dst in (("q", 0, qa), ("k", C, ka)):
                for ob in range(FB):
                    bank = pb[ob % 4]
                    ctx.wait("t", ctx.tok("t"), tw, ebt.get(ob % 4), xn_toks[-1])
                    for fb in range(FB):
                        im = nc.tensor.matmul(
                            bank[:, 0:CH],
                            wqkv[:, fb * 3 * C + colo + ob * 128:
                                 fb * 3 * C + colo + ob * 128 + 128],
                            xn[fb], start=(fb == 0), stop=(fb == FB - 1))
                    tm = ctx.mark("t", im)
                    if which == "q":
                        ev = ctx.op("v", lambda ob=ob, bank=bank, t0=t0:
                                    nc.vector.tensor_scalar(
                                        qa[:, ob * T + t0: ob * T + t0 + CH],
                                        bank[:, 0:CH],
                                        ctx.smal["bqkv"][:, ob:ob + 1], None,
                                        ALU.add), tm)
                    else:
                        ev = ctx.op("v", lambda ob=ob, bank=bank, t0=t0:
                                    nc.vector.tensor_scalar(
                                        ka[:, ob * T + t0: ob * T + t0 + CH],
                                        bank[:, 0:CH],
                                        ctx.smal["bqkv"][:, 8 + ob:9 + ob], None,
                                        ALU.add), tm)
                    ebt[ob % 4] = ev
            for st_ in range(4):
                tt = t0 + st_ * 128
                for nb in range(2):
                    bank = pb[6 + nb]
                    ctx.wait("t", ctx.tok("t"), ebt.get(("v", nb)), tb1, tz3)
                    for fb in range(FB):
                        nc.tensor.matmul(
                            bank[:, 0:CH],
                            xn[fb][:, st_ * 128:(st_ + 1) * 128],
                            wqkv[:, fb * 3 * C + 2 * C + nb * CH:
                                 fb * 3 * C + 2 * C + (nb + 1) * CH],
                            start=(fb == 0), stop=False)
                    im = nc.tensor.matmul(
                        bank[:, 0:CH], onesr[0:1, 0:128],
                        vbias[0:1, nb * CH:(nb + 1) * CH],
                        start=False, stop=True)
                    tm = ctx.mark("t", im)
                    ev = ctx.op("v", lambda tt=tt, nb=nb, bank=bank:
                                nc.vector.tensor_copy(
                                    va[:, (tt // 128) * C + nb * CH:
                                       (tt // 128) * C + (nb + 1) * CH],
                                    bank[:, 0:CH]), tm)
                    ebt[("v", nb)] = ev
        A.pop()
        ctx.barrier()
        A.push()
        wout = A.bf(FB * C)
        asb = A.bf(384)
        asb_b = A.bf(384)
        atb = A.bf(384)
        atb_b = A.bf(384)
        dz = A.bf(128)
        dz_b = A.bf(128)
        scb2 = [A.f32(384), A.f32(384)]
        avb2 = [A.bf(FB * 128) for _ in range(2)]
        zrow = A.f32(4)
        xrt2 = [A.f32(128) for _ in range(2)]
        two = None
        for fb in range(FB):
            two = ctx.dma("g", "wt1", wout[:, fb * C:(fb + 1) * C],
                          dt["wout"].ap()[fb * 128:(fb + 1) * 128, :],
                          deps=(ctx.tok("wt1"),) if fb == 0 else ())
        nih = T // 128
        ebt2 = {}
        ctx.xo_guards = {}
        asb2 = [asb, asb_b]
        atb2 = [atb, atb_b]
        dz2 = [dz, dz_b]
        def mk_outproj(ih, avt):
            def emit():
                i0 = ih * 128
                avb = avb2[ih % 2]
                for ob in range(FB):
                    bank = pb[6 + ob % 2]
                    xrt = xrt2[ob % 2]
                    tld = ctx.dma("q", "ms1", xrt,
                                  xs.ap()[seq][ob * 128:(ob + 1) * 128,
                                               i0:i0 + 128],
                                  deps=(ctx.xo_guards.get(ob % 2),))
                    ctx.wait("t", ctx.tok("t"), ebt2.get(("o", ob % 2)), avt,
                             two)
                    for fb in range(FB):
                        im = nc.tensor.matmul(
                            bank[:, 0:128],
                            wout[:, fb * C + ob * 128: fb * C + ob * 128 + 128],
                            avb[:, fb * 128:(fb + 1) * 128],
                            start=(fb == 0), stop=(fb == FB - 1))
                    tm = ctx.mark("t", im)
                    ta = ctx.op("v", lambda bank=bank, xrt=xrt, ob=ob:
                                nc.vector.scalar_tensor_tensor(
                                    xrt, bank[:, 0:128],
                                    ctx.smal["bout"][:, ob:ob + 1], xrt,
                                    ALU.add, ALU.add), tm, tld)
                    ebt2[("o", ob % 2)] = ta
                    tst = ctx.dma("q", "xst" if ob % 2 == 0 else "ms3",
                                  xs.ap()[seq][ob * 128:(ob + 1) * 128,
                                               i0:i0 + 128],
                                  xrt, deps=(ta,))
                    ctx.xo_guards[ob % 2] = tst
                ctx.avb_guard[ih % 2] = ctx.tok("t")
            return emit

        pending_op = None
        ctx.avb_guard = {}
        for ih in range(nih):
            i0 = ih * 128
            jb0 = max(0, ih - 1)
            jb1 = min(nih, ih + 2)
            kw = (jb1 - jb0) * 128
            moff = 128 * (1 - (ih - jb0))
            avt = None
            st = {}

            def stage_scores(h):
                p = h % 2
                fbq = h // 2
                po = (h % 2) * 64
                sb_ = pb[p]
                ctx.wait("t", ctx.tok("t"), ebt2.get(("s", p)), ti, tm2)
                for jj in range(jb0, jb1):
                    nc.tensor.matmul(
                        sb_[:, (jj - jb0) * 128:(jj - jb0 + 1) * 128],
                        qa[po:po + 64, fbq * T + i0: fbq * T + i0 + 128],
                        ka[po:po + 64, fbq * T + jj * 128: fbq * T + (jj + 1) * 128],
                        start=(jj == jb0), stop=False)
                im = nc.tensor.matmul(
                    sb_[:, 0:kw], ident[:, 0:128],
                    mask[:, moff:moff + kw], start=False, stop=True)
                return ctx.mark("t", im)

            def stage_soft(h, tsc):
                p = h % 2
                sb_ = pb[p]
                scb = scb2[p]
                zr = zrow[:, 2 * p:2 * p + 2]
                tev = ctx.op("s", lambda: nc.scalar.copy(
                    scb[:, 0:kw], sb_[:, 0:kw]), tsc, st.get(("sc", p)))
                ebt2[("s", p)] = tev
                tmx = ctx.op("v", lambda: nc.vector.tensor_reduce(
                    zr[:, 0:1], scb[:, 0:kw], mybir.AxisListType.X,
                    ALU.max), tev, st.get(("zr", p)))
                tsk = ctx.op("v", lambda: nc.vector.tensor_scalar(
                    zr[:, 0:1], zr[:, 0:1], -0.125, None, ALU.mult), tmx)
                tex = ctx.op("s", lambda: nc.scalar.activation(
                    asb2[p][:, 0:kw], scb[:, 0:kw], AF.Exp,
                    bias=zr[:, 0:1], scale=0.125,
                    accum_out=zr[:, 1:2]), tsk, st.get(("as", p)))
                st[("sc", p)] = tex
                tzi = ctx.op("v", lambda: nc.vector.reciprocal(
                    zr[:, 1:2], zr[:, 1:2]), tex)
                tdz = ctx.op("v", lambda: nc.vector.tensor_scalar(
                    dz2[p][:, :], ident[:, :], zr[:, 1:2], None, ALU.mult),
                    tzi, ti, st.get(("dz", p)))
                st[("zr", p)] = tdz
                return tex, tdz

            def stage_trans(h, tex, tdz):
                p = h % 2
                atp = pb[2 + p]
                ctx.wait("t", ctx.tok("t"), ebt2.get(("at", p)), tex, tdz)
                im = None
                for jj in range(jb0, jb1):
                    im = nc.tensor.matmul(
                        atp[:, (jj - jb0) * 128:(jj - jb0 + 1) * 128],
                        asb2[p][:, (jj - jb0) * 128:(jj - jb0 + 1) * 128],
                        dz2[p][:, :], start=(jj == jb0), stop=(jj == jb1 - 1))
                tat = ctx.mark("t", im)
                st[("as", p)] = st[("dz", p)] = tat
                tae = ctx.op("s", lambda: nc.scalar.activation(
                    atb2[p][:, 0:kw], atp[:, 0:kw], AF.Copy), tat,
                    st.get(("ab", p)))
                ebt2[("at", p)] = tae
                return tae

            def stage_av(h, tae):
                p = h % 2
                avp = pb[4 + p]
                ctx.wait("t", ctx.tok("t"), ebt2.get(("av", p)), tae)
                im = None
                for jj in range(jb0, jb1):
                    im = nc.tensor.matmul(
                        avp[0:64, 0:128],
                        va[:, jj * C + h * 64: jj * C + h * 64 + 64],
                        atb2[p][:, (jj - jb0) * 128:(jj - jb0 + 1) * 128],
                        start=(jj == jb0), stop=(jj == jb1 - 1))
                tv_ = ctx.mark("t", im)
                st[("ab", p)] = tv_
                tve = ctx.op("v", lambda h=h, avp=avp, avb=avb2[ih % 2]:
                             nc.vector.tensor_copy(
                                 avb[(h % 2) * 64:(h % 2) * 64 + 64,
                                     (h // 2) * 128:(h // 2) * 128 + 128],
                                 avp[0:64, 0:128]), tv_,
                             ctx.avb_guard.get(ih % 2) if h == 0 else None)
                ebt2[("av", p)] = tve
                return tve

            tscs = {0: stage_scores(0)}
            softs = {}
            taes = {}
            for h in range(16):
                if h + 1 < 16:
                    tscs[h + 1] = stage_scores(h + 1)
                if h == 2 and pending_op is not None:
                    pending_op()
                    pending_op = None
                if h >= 1:
                    taes[h - 1] = stage_trans(h - 1, *softs[h - 1])
                if h >= 2:
                    avt = stage_av(h - 2, taes[h - 2])
                softs[h] = stage_soft(h, tscs[h])
            taes[15] = stage_trans(15, *softs[15])
            avt = stage_av(14, taes[14])
            avt = stage_av(15, taes[15])
            pending_op = mk_outproj(ih, avt)
        if pending_op is not None:
            pending_op()
        A.pop()
        ctx.barrier()
    ctx.stream_ready = (ctx.tok("xst"), ctx.tok("ms3"))
    A.pop()


CAPG = 4096
CAPL = 768
CAPT = CAPL // 128
TROWS = 4096 + NE * CAPT + 16


def emit_moe(ctx, dt, xs, aux):
    nc, pb = ctx.nc, ctx.pb
    ctx.barrier()
    A = ctx.arena
    xnfm, cc_in, cc = aux["xnfm"], aux["cc_in"], aux["cc"]
    tab, dlt = aux["tab"], aux["dlt"]
    NIT = 32
    A.push()
    ctx.moe_lint = A.i32(NE * CAPT)
    MHB = MH // 128
    w1 = A.bf(FB * MH)
    w3 = A.bf(FB * MH)
    w2 = A.bf(MHB * C)
    w1b, w3b, w2b = aux["w1b"], aux["w3b"], aux["w2b"]
    tw1_tok = {}; tw3_tok = {}; tw2_tok = {}

    def load_w1(e, *deps):
        for fb in range(FB):
            ctx.dma("q", "wD" + str(e % 2), w1[:, fb * MH:(fb + 1) * MH],
                    w1b.ap()[e][fb * 128:(fb + 1) * 128, :],
                    deps=deps if fb == 0 else ())
        tw1_tok[e] = ctx.tok("wD" + str(e % 2))

    def load_w3(e, *deps):
        for fb in range(FB):
            ctx.dma("q", "wD" + str(e % 2), w3[:, fb * MH:(fb + 1) * MH],
                    w3b.ap()[e][fb * 128:(fb + 1) * 128, :],
                    deps=deps + (ctx.tok("wD" + str(e % 2)),)
                    if fb == 0 else ())
        tw3_tok[e] = ctx.tok("wD" + str(e % 2))

    def load_w2(e, *deps):
        for hb in range(MHB):
            ctx.dma("q", "wD" + str(e % 2), w2[:, hb * C:(hb + 1) * C],
                    w2b.ap()[e][hb * 128:(hb + 1) * 128, :],
                    deps=deps + (ctx.tok("wD" + str(e % 2)),)
                    if hb == 0 else ())
        tw2_tok[e] = ctx.tok("wD" + str(e % 2))

    identb = A.bf(128)
    slt = A.bf(128)
    jrow = A.f16(CAPL)
    ones16 = A.f16(2)
    Sacc = A.f16(CAPL)
    lrowf = A.f32(CAPL)
    sc32 = A.f32(32); m16 = A.f16(32); c16 = A.f16(32); rtb = A.bf(2)
    jkl = A.f32(16)
    lo = A.f32(NE)
    ctx.identb = identb
    tzi = ctx.op("v", lambda: nc.vector.memset(identb[:, :], 1.0))
    tidb = ctx.op("g", lambda: nc.gpsimd.affine_select(
        identb[:, :], identb[:, :], pattern=[[-1, 128]], base=0,
        channel_multiplier=1, compare_op=ALU.is_equal, fill=0.0), tzi)
    t1s = ctx.op("v", lambda: nc.vector.memset(slt[:, :], 1.0))
    tslt = ctx.op("g", lambda: nc.gpsimd.affine_select(
        slt[:, :], slt[:, :], pattern=[[1, 128]], base=-1,
        channel_multiplier=-1, compare_op=ALU.is_ge, fill=0.0), t1s)
    tjr = ctx.op("g", lambda: nc.gpsimd.iota(
        jrow[:, 0:CAPL], pattern=[[1, CAPL]], base=0, channel_multiplier=0,
        allow_small_or_imprecise_dtypes=True))
    to16 = ctx.op("v", lambda: nc.vector.memset(ones16[:, 0:1], 1.0))

    lb = {"grd": None, "base": None, "lrow": None, "sacc": None, "rtbT": None}

    def lb_part1(e, *deps):
        tsm = ctx.op("v", lambda: nc.vector.memset(Sacc[:, 0:CAPL], 0.0),
                     lb["sacc"], *deps)
        tl = ctx.dma("g", "ms4", sc32[:, :],
                     cc_in.ap()[e].rearrange("(p u) -> p u", p=128),
                     deps=(tsm,))
        tm_ = ctx.op("v", lambda e=e: nc.vector.tensor_scalar(
            m16[:, :], sc32[:, :], lo[:, e:e + 1], None, ALU.is_gt), tl)
        tsc = ctx.op("v", lambda: nc.vector.tensor_tensor_scan(
            c16[:, :], ones16[:, 0:1].to_broadcast([128, 32]), m16[:, :], 0.0,
            ALU.mult, ALU.add), tm_, to16)
        lb["p1"] = ctx.op("v", lambda: nc.vector.tensor_copy(
            rtb[:, 0:1], c16[:, 31:32]), tsc, lb["rtbT"])

    def lb_base(bank):
        ctx.wait("t", ctx.tok("t"), lb["p1"], tslt, lb["grd"])
        ib = nc.tensor.matmul(pb[bank][:, 6:7], slt[:, :], rtb[:, 0:1],
                              start=True, stop=True)
        lb["rtbT"] = lb["base"] = ctx.mark("t", ib)

    def lb_staccs(bank):
        tcf = ctx.op("v", lambda: nc.vector.tensor_scalar(
            c16[:, :], c16[:, :], pb[bank][:, 6:7], None, ALU.add),
            lb["base"])
        lb["grd"] = tcf
        tsa = tcf
        for u in range(32):
            tsa = ctx.op("v", lambda u=u: nc.vector.scalar_tensor_tensor(
                Sacc[:, 0:CAPL], jrow[:, 0:CAPL], c16[:, u:u + 1],
                Sacc[:, 0:CAPL], ALU.is_ge, ALU.add), tsa, tjr)
        lb["stacc"] = tsa

    def lb_lrow_a(bank):
        ctx.wait("t", ctx.tok("t"), lb["stacc"], lb["grd"])
        il = nc.tensor.matmul(pb[bank][0:1, 0:512], ctx.c_ones16,
                              Sacc[:, 0:512], start=True, stop=True)
        tlr = ctx.mark("t", il)
        lb["grd"] = ctx.op("v", lambda: nc.vector.tensor_copy(
            lrowf[0:1, 0:512], pb[bank][0:1, 0:512]), tlr)

    def lb_lrow_b(bank):
        ctx.wait("t", ctx.tok("t"), lb["grd"])
        il = nc.tensor.matmul(pb[bank][0:1, 0:256], ctx.c_ones16,
                              Sacc[:, 512:CAPL], start=True, stop=True)
        tlr = ctx.mark("t", il)
        lb["sacc"] = lb["grd"] = ctx.op(
            "v", lambda: nc.vector.tensor_copy(
                lrowf[0:1, 512:CAPL], pb[bank][0:1, 0:256]), tlr)

    def lb_cols(e, bank):
        Lint = ctx.moe_lint
        ctx.wait("t", ctx.tok("t"), lb["grd"])
        ic = None
        for tt in range(CAPT):
            ic = nc.tensor.matmul(pb[bank][:, tt:tt + 1],
                                  lrowf[0:1, tt * 128:(tt + 1) * 128],
                                  ctx.c_ones[0:1, 0:1], start=True, stop=True)
        tic = ctx.mark("t", ic)
        tov = ctx.op("v", lambda bank=bank: nc.vector.tensor_scalar(
            jkl[:, 0:CAPT], pb[bank][:, 0:CAPT], 4095.5, None, ALU.is_gt),
            tic)
        tof = None
        for tt in range(CAPT):
            tof = ctx.op("v", lambda tt=tt, e=e, bank=bank:
                         nc.vector.scalar_tensor_tensor(
                             jkl[:, 8 + tt:9 + tt], jkl[:, tt:tt + 1],
                             float(e * CAPT + tt), pb[bank][:, tt:tt + 1],
                             ALU.mult, ALU.add), tov)
        lb["grd"] = tof
        lg = ctx.op("v", lambda e=e: nc.vector.tensor_copy(
            Lint[:, e * CAPT:(e + 1) * CAPT], jkl[:, 8:8 + CAPT]), tof)
        ctx.moe_lg_e[e] = lg

    A.push()
    wg = A.bf(FB * NE)
    xn = [A.bf(CH) for _ in range(FB)]
    xts = [A.f32(CH) for _ in range(FB)]
    scrow = A.f32(CH)
    xtk = [A.bf(1024) for _ in range(2)]
    zt = A.bf(1040)
    ztf = A.f32(1024)
    ctx.lnsq = A.f32(2 * CH)
    ctx.lnrows = A.f32(3 * CH)
    twg = None
    for fb in range(FB):
        twg = ctx.dma("g", "ms2", wg[:, fb * NE:(fb + 1) * NE],
                      dt["wg"].ap()[fb * 128:(fb + 1) * 128, :],
                      deps=(ctx.tok("ms2"),) if fb == 0 else ())
    tz0 = ctx.op("v", lambda: nc.vector.memset(zt[:, :], 0.0))
    tz1 = ctx.op("v", lambda: nc.vector.memset(ztf[:, :], 0.0))
    ctx.dma("q", "ms1", tab.ap()[4096:4096 + 64, :], zt[0:64, :], deps=(tz0,))
    for r in range(0, TROWS, 128):
        n = min(128, TROWS - r)
        ctx.dma("q", "ms1", dlt.ap()[r:r + n, :], ztf[0:n, :], deps=(tz1,))
    sct = None
    xtkg = [None, None]
    tbg = {}
    for seq in range(NSEQ):
        for ch in range(NCH):
            xn_toks, _ = ln_chunk(ctx, seq, ch, xs, xts, xn,
                                  xts_free=tuple(ctx.ln_prior),
                                  apply_deps=(ctx.tok("xst"), ctx.tok("wC0"),
                                              ctx.tok("wC1")))
            for fb in range(FB):
                ctx.dma("q", "xst",
                        xnfm.ap()[seq][fb * 128:(fb + 1) * 128,
                                       ch * CH:(ch + 1) * CH],
                        xn[fb], deps=(xn_toks[fb],))
            for sub in range(4):
                p = sub % 2
                r0 = seq * T + ch * CH + sub * 128
                bA, bB = (1, 2) if p == 0 else (3, 6)
                pA = pb[bA].bitcast(BF16)
                pB = pb[bB].bitcast(BF16)
                ctx.wait("t", ctx.tok("t"), xn_toks[-1], tidb,
                         tbg.get(p))
                for fb in range(FB):
                    bank = pA if fb < 4 else pB
                    itp = nc.tensor.transpose(
                        bank[:, (fb % 4) * 128:(fb % 4) * 128 + 128],
                        xn[fb][:, sub * 128:(sub + 1) * 128],
                        ctx.identb[:, :])
                ttp = ctx.mark("t", itp)
                te1 = ctx.op("s", lambda p=p, pA=pA: nc.scalar.copy(
                    xtk[p][:, 0:512], pA[:, 0:512]), ttp, xtkg[p])
                te2 = ctx.op("s", lambda p=p, pB=pB: nc.scalar.copy(
                    xtk[p][:, 512:1024], pB[:, 0:512]))
                tbg[p] = te2
                xtkg[p] = ctx.dma("q", "wC" + str(p),
                                  tab.ap()[r0:r0 + 128, 0:1024],
                                  xtk[p][:, :], deps=(te2,))
            ctx.wait("t", ctx.tok("t"), twg, sct, xn_toks[-1])
            for fb in range(FB):
                im = nc.tensor.matmul(pb[0][0:NE, 0:CH],
                                      wg[:, fb * NE:(fb + 1) * NE], xn[fb],
                                      start=(fb == 0), stop=(fb == FB - 1))
            tm = ctx.mark("t", im)
            sct = ctx.op("v", lambda: nc.vector.tensor_copy(
                scrow[0:NE, :], pb[0][0:NE, 0:CH]), tm, ctx.tok("ms1"))
            ctx.dma("q", "ms1", cc_in.ap()[0:NE, seq * T + ch * CH:
                                           seq * T + (ch + 1) * CH],
                    scrow[0:NE, :], deps=(sct,))
    A.pop()
    ctx.barrier()

    load_w1(0, ctx.pcw_tok.get(0))
    load_w3(0, ctx.pcw_tok.get(0))
    load_w2(0, ctx.pcw_tok.get(0))
    A.push()
    sct_all = [A.f32(256) for _ in range(NE)]
    hi = A.f32(NE); mid = A.f32(NE)
    cnts = A.f32(NE); msk = A.i32(NE); msk2 = A.i32(NE); negth = A.f32(NE)
    e1t = A.f32(256); junk = A.f32(256)
    zpart = A.f32(NE)
    thcol = A.f32(1); nthcol = A.f32(1); zhcol = A.f32(1)
    sloc = A.f32(CH); gloc = A.f32(CH); e1loc = A.f32(CH)
    gtmb = A.bf(4 * NE)
    identf = A.f32(128)
    onesf = A.f32(128)
    Lint = ctx.moe_lint
    t2 = ctx.op("v", lambda: nc.vector.memset(identf[:, :], 1.0))
    tidf = ctx.op("g", lambda: nc.gpsimd.affine_select(
        identf[:, :], identf[:, :], pattern=[[-1, 128]], base=0,
        channel_multiplier=1, compare_op=ALU.is_equal, fill=0.0), t2)
    tonf = ctx.op("v", lambda: nc.vector.memset(onesf[:, :], 1.0))
    tcc = ctx.op("g", lambda: nc.gpsimd.collective_compute(
        "AllGather", ALU.bypass, replica_groups=[list(range(NCORES))],
        ins=[cc_in.ap()], outs=[cc.ap()]), ctx.tok("ms1"))
    tld = None
    for e in range(NE):
        for sl in range(NCORES):
            tld = ctx.dma("g", "ms0", sct_all[e][:, sl * 32:(sl + 1) * 32],
                          cc.ap()[sl][e].rearrange("(p u) -> p u", p=128),
                          deps=(tcc,))
    ctx.op("v", lambda: nc.vector.memset(lo, -1000.0))
    t0v = ctx.op("v", lambda: nc.vector.memset(hi, 1000.0))
    pbg = {}
    tsel = {0: t0v, 4: t0v}
    tcg = {}

    def bis_counts(e0):
        ctx.op("v", lambda: nc.vector.tensor_add(
            mid[:, e0:e0 + 4], lo[:, e0:e0 + 4], hi[:, e0:e0 + 4]), tsel[e0])
        tmid = ctx.op("v", lambda: nc.vector.tensor_scalar(
            mid[:, e0:e0 + 4], mid[:, e0:e0 + 4], 0.5, None, ALU.mult))
        tc = None
        for e in range(e0, e0 + 4):
            tc = ctx.op("v", lambda e=e: nc.vector.tensor_scalar(
                junk, sct_all[e][:, :], mid[:, e:e + 1], 0.0, ALU.is_gt,
                ALU.add, accum_out=cnts[:, e:e + 1]), tmid, tld)
        tcg[e0] = tc

    def bis_update(e0, bank):
        ctx.wait("t", ctx.tok("t"), tcg[e0], tonf, pbg.get(bank))
        i3 = nc.tensor.matmul(pb[bank][:, 0:4], onesf, cnts[:, e0:e0 + 4],
                              start=True, stop=True)
        t3 = ctx.mark("t", i3)
        tmk = ctx.op("v", lambda: nc.vector.tensor_scalar(
            msk[:, e0:e0 + 4], pb[bank][:, 0:4], float(CAPG), None,
            ALU.is_ge), t3)
        tmk2 = ctx.op("v", lambda: nc.vector.tensor_scalar(
            msk2[:, e0:e0 + 4], pb[bank][:, 0:4], float(CAPG), None,
            ALU.is_lt), t3)
        pbg[bank] = tmk2
        ctx.op("v", lambda: nc.vector.select(
            lo[:, e0:e0 + 4], msk[:, e0:e0 + 4], mid[:, e0:e0 + 4],
            lo[:, e0:e0 + 4], add_drain=True), tmk)
        tsel[e0] = ctx.op("v", lambda: nc.vector.select(
            hi[:, e0:e0 + 4], msk2[:, e0:e0 + 4], mid[:, e0:e0 + 4],
            hi[:, e0:e0 + 4], add_drain=True), tmk2)

    bis_counts(0)
    bis_counts(4)
    for it in range(NIT):
        bis_update(0, 2)
        if it + 1 < NIT:
            bis_counts(0)
        bis_update(4, 3)
        if it + 1 < NIT:
            bis_counts(4)
    t0v = tsel[4]

    tng = ctx.op("v", lambda: nc.vector.tensor_scalar(
        negth, lo, -1.0, None, ALU.mult), t0v)
    tzp = None
    for e in range(NE):
        te1 = ctx.op("s", lambda e=e: nc.scalar.activation(
            e1t, sct_all[e][:, :], AF.Exp, bias=negth[:, e:e + 1]), tng, tzp)
        tzp = ctx.op("v", lambda e=e: nc.vector.scalar_tensor_tensor(
            junk, sct_all[e][:, :], lo[:, e:e + 1], e1t, ALU.is_gt, ALU.mult,
            accum_out=zpart[:, e:e + 1]), te1)
    ctx.wait("t", ctx.tok("t"), tzp, pbg.get(1))
    i1 = nc.tensor.matmul(pb[1][0:1, 0:NE], ctx.c_ones, zpart,
                          start=True, stop=True)
    t1m = ctx.mark("t", i1)
    tzv = ctx.op("v", lambda: nc.vector.tensor_copy(
        zpart[0:1, 0:NE], pb[1][0:1, 0:NE]), t1m)
    tzr = ctx.op("v", lambda: nc.vector.reciprocal(
        zpart[0:1, 0:NE], zpart[0:1, 0:NE]), tzv)
    tzh = ctx.op("v", lambda: nc.vector.tensor_scalar(
        zpart[0:1, 0:NE], zpart[0:1, 0:NE], 0.5, None, ALU.mult), tzr)
    ctx.wait("t", ctx.tok("t"), tzh, tng, pbg.get(3))
    nc.tensor.matmul(pb[3][0:NE, 0:1], lo, ctx.c_o128, start=True, stop=True)
    ic2 = nc.tensor.matmul(pb[3][0:NE, 1:2], negth, ctx.c_o128,
                           start=True, stop=True)
    tcm = ctx.mark("t", ic2)
    ctx.wait("t", ctx.tok("t"), tzh)
    ib = nc.tensor.matmul(pb[2][:, 0:NE], ctx.c_brow, zpart[0:1, 0:NE],
                          start=True, stop=True)
    tbz = ctx.mark("t", ib)
    tcpy = ctx.op("v", lambda: nc.vector.tensor_copy(
        junk[:, 0:NE], pb[2][:, 0:NE]), tbz)
    pbg[2] = tcpy
    ctx.wait("t", ctx.tok("t"), tcpy)
    ic3 = nc.tensor.matmul(pb[3][0:NE, 2:3], junk[:, 0:NE], ctx.c_o128,
                           start=True, stop=True)
    tcm2 = ctx.mark("t", ic3)
    ctx.op("v", lambda: nc.vector.tensor_copy(thcol[0:NE, :],
                                              pb[3][0:NE, 0:1]), tcm, tcm2)
    ctx.op("v", lambda: nc.vector.tensor_copy(nthcol[0:NE, :],
                                              pb[3][0:NE, 1:2]))
    tcols = ctx.op("v", lambda: nc.vector.tensor_copy(zhcol[0:NE, :],
                                                      pb[3][0:NE, 2:3]))
    pbg[3] = tcols
    tgl = None
    gldg = {}
    ctx.moe_ttg = None
    for seq in range(NSEQ):
        for ch in range(NCH):
            o = seq * T + ch * CH
            tl = ctx.dma("q", "ms1", sloc[0:NE, :], cc_in.ap()[0:NE, o:o + CH],
                         deps=(tgl,))
            te1 = ctx.op("s", lambda: nc.scalar.activation(
                e1loc[0:NE, :], sloc[0:NE, :], AF.Exp,
                bias=nthcol[0:NE, :]), tl, tcols, tgl)
            tg1 = ctx.op("v", lambda: nc.vector.scalar_tensor_tensor(
                gloc[0:NE, :], sloc[0:NE, :], thcol[0:NE, :], e1loc[0:NE, :],
                ALU.is_gt, ALU.mult), te1, ctx.moe_ttg)
            tg2 = ctx.op("v", lambda: nc.vector.tensor_scalar(
                gloc[0:NE, :], gloc[0:NE, :], zhcol[0:NE, :], None, ALU.mult),
                tg1)
            tgl = tg2
            for sub in range(4):
                p = sub % 2
                ctx.wait("t", ctx.tok("t"), tg2, pbg.get(4), tidf)
                it_ = nc.tensor.transpose(
                    pb[4][0:128, 0:NE],
                    gloc[0:NE, sub * 128:(sub + 1) * 128],
                    identf[0:NE, 0:NE])
                tt_ = ctx.mark("t", it_)
                ctx.moe_ttg = tt_
                ghl = gtmb[:, p * 2 * NE:(p + 1) * 2 * NE]
                ctx.op("v", lambda ghl=ghl: nc.vector.tensor_copy(
                    ghl[:, 0:NE], pb[4][:, 0:NE]), tt_, gldg.get(p))
                tgv = ctx.op("v", lambda ghl=ghl: nc.vector.tensor_sub(
                    ghl[:, NE:2 * NE], pb[4][:, 0:NE], ghl[:, 0:NE]))
                pbg[4] = tgv
                gldg[p] = ctx.dma(
                    "q", "wC" + str(p),
                    tab.ap()[o + sub * 128: o + sub * 128 + 128, 1024:1040],
                    ghl, deps=(tgv,))
    ctx.moe_lg_e = {}
    lb["grd"] = pbg.get(4)
    for e in (0, 1):
        lb_part1(e, tcols)
        lb_base(4 + e)
        lb_staccs(4 + e)
        lb_lrow_a(4 + e)
        lb_lrow_b(4 + e)
        lb_cols(e, 4 + e)
    A.pop()
    ctx.barrier()

    A.push()
    WS1 = (512, 256)
    at = A.bf(MHB * CAPL)
    gt2 = [[A.bf(1040) for _ in range(CAPT)] for _ in range(2)]
    xe2 = [A.bf(FB * CAPL) for _ in range(2)]
    ysb = A.f32(1024)
    sgt = A.f32(1)
    ctx.moe_sg = None
    Lint = ctx.moe_lint
    pbg2 = {}
    sc_guard = None
    ctx.moe_ye = {}

    def do_gathers(e):
        p = e % 2
        for tt in range(CAPT):
            ctx.dma("g", "ms2", gt2[p][tt][:, :], tab.ap(),
                    in_offset=IndirectOffsetOnAxis(
                        Lint[:, e * CAPT + tt: e * CAPT + tt + 1], 0),
                    deps=(ctx.moe_lg_e[e], ctx.moe_ye.get(p),
                          ctx.tok("ms0"), ctx.tok("ms2")) if tt == 0 else ())
        return ctx.tok("ms2")

    ctx.xe_guard = {}
    ctx.xe_done = {}
    ctx.xe_pb_free = None

    def mk_xe_tasks(e, tgall):
        p = e % 2
        gt = gt2[p]
        xe = xe2[p]
        tasks = []
        for fb in range(FB):
            for t0, nt in ((0, 4), (4, 2)):
                def task(fb=fb, t0=t0, nt=nt):
                    p4 = pb[4].bitcast(BF16)
                    ctx.wait("t", ctx.tok("t"), tgall, ctx.xe_pb_free, tidb)
                    for i in range(nt):
                        itp = nc.tensor.transpose(
                            p4[:, i * 128:(i + 1) * 128],
                            gt[t0 + i][:, fb * 128:(fb + 1) * 128],
                            identb[:, :])
                    tm = ctx.mark("t", itp)
                    ctx.xe_pb_free = ctx.op(
                        "s", lambda: nc.scalar.copy(
                            xe[:, fb * CAPL + t0 * 128:
                               fb * CAPL + (t0 + nt) * 128],
                            p4[:, 0:nt * 128]), tm, ctx.xe_guard.get(p))
                    ctx.xe_done[p] = ctx.xe_pb_free
                tasks.append(task)
        return tasks

    tgn = do_gathers(0)
    for t_ in mk_xe_tasks(0, tgn):
        t_()
    pend = []
    at_free = None
    for e in range(NE):
        gt = gt2[e % 2]
        xe = xe2[e % 2]
        tgall = tgn
        txe = ctx.xe_done[e % 2]
        le = e + 2 if e + 2 < NE else None
        if le is not None:
            lb_part1(le, tcols)
        if e + 1 < NE:
            tgn = do_gathers(e + 1)
            pend = mk_xe_tasks(e + 1, tgn)
        w1_last = w3_last = None
        for wi, w0 in ((0, 0), (1, 512)):
            W = WS1[wi]
            for hb in range(MHB):
                bank = pb[hb % 2]
                ctx.wait("t", ctx.tok("t"), tw1_tok[e], txe,
                         pbg2.get(hb % 2))
                for fb in range(FB):
                    im = nc.tensor.matmul(
                        bank[:, 0:W],
                        w1[:, fb * MH + hb * 128: fb * MH + hb * 128 + 128],
                        xe[:, fb * CAPL + w0: fb * CAPL + w0 + W],
                        start=(fb == 0), stop=(fb == FB - 1))
                tm = ctx.mark("t", im)
                w1_last = tm
                tse = ctx.op("s", lambda hb=hb, bank=bank, w0=w0, W=W:
                             nc.scalar.activation(
                                 at[:, hb * CAPL + w0: hb * CAPL + w0 + W],
                                 bank[:, 0:W], AF.Silu), tm,
                             pbg2.get(("a", hb, w0)), at_free)
                pbg2[hb % 2] = tse
                pbg2[("a", hb, w0)] = tse
                if le is not None and wi == 0 and hb == 6:
                    lb_base(5)
                if le is not None and wi == 1 and hb == 4:
                    lb_lrow_b(5)
                if le is not None and wi == 1 and hb == 10:
                    lb_cols(le, 5)
            if le is not None and wi == 0:
                lb_staccs(5)
            for hb in range(MHB):
                bank = pb[2 + hb % 2]
                ctx.wait("t", ctx.tok("t"), tw3_tok[e], pbg2.get(2 + hb % 2),
                         pbg2.get(("a", hb, w0)))
                for fb in range(FB):
                    im = nc.tensor.matmul(
                        bank[:, 0:W],
                        w3[:, fb * MH + hb * 128: fb * MH + hb * 128 + 128],
                        xe[:, fb * CAPL + w0: fb * CAPL + w0 + W],
                        start=(fb == 0), stop=(fb == FB - 1))
                tm = ctx.mark("t", im)
                w3_last = tm
                the = ctx.op("v", lambda hb=hb, bank=bank, w0=w0, W=W:
                             nc.vector.tensor_mul(
                                 at[:, hb * CAPL + w0: hb * CAPL + w0 + W],
                                 at[:, hb * CAPL + w0: hb * CAPL + w0 + W],
                                 bank[:, 0:W]), tm, pbg2.get(("h", hb, w0)))
                pbg2[2 + hb % 2] = the
                pbg2[("h", hb, w0)] = the
                if hb % 2 == 1 and pend:
                    pend.pop(0)()
            if le is not None and wi == 0:
                lb_lrow_a(5)
        if e + 1 < NE:
            load_w1(e + 1, w1_last, ctx.pcw_tok.get(e + 1))
            load_w3(e + 1, w3_last)
        w2_last = None
        for tt in range(CAPT):
            for nb in range(2):
                bank = pb[6 + nb]
                ctx.wait("t", ctx.tok("t"), tw2_tok[e], pbg2.get(6 + nb))
                for hb in range(MHB):
                    w0 = 0 if tt < 4 else 512
                    ctx.wait("t", pbg2[("h", hb, w0)])
                    im = nc.tensor.matmul(
                        bank[:, 0:512],
                        at[:, hb * CAPL + tt * 128: hb * CAPL + tt * 128 + 128],
                        w2[:, hb * C + nb * 512: hb * C + (nb + 1) * 512],
                        start=(hb == 0), stop=(hb == MHB - 1))
                tm = ctx.mark("t", im)
                w2_last = tm
                if nb == 0:
                    sgf = ctx.op("v", lambda tt=tt, e=e: nc.vector.tensor_add(
                        sgt, gt[tt][:, 1024 + e:1025 + e],
                        gt[tt][:, 1032 + e:1033 + e]), tgall, ctx.moe_sg)
                tye = ctx.op("v", lambda bank=bank, nb=nb:
                             nc.vector.tensor_scalar(
                                 ysb[:, nb * 512:(nb + 1) * 512],
                                 bank[:, 0:512], sgt,
                                 None, ALU.mult), tm, sc_guard, sgf)
                ctx.moe_sg = tye
                pbg2[6 + nb] = tye
                ctx.moe_ye[e % 2] = tye
            ctx.xe_guard[e % 2] = ctx.tok("t")
            ctx.wait("g", tye, ctx.tok("ms0"))
            sc_guard = ctx.dma("g", "ms0", dlt.ap(), ysb[:, :],
                               out_offset=IndirectOffsetOnAxis(
                                   Lint[:, e * CAPT + tt:
                                        e * CAPT + tt + 1], 0),
                               compute_op=ALU.add)
        if e + 1 < NE:
            load_w2(e + 1, w2_last)
        at_free = w2_last
    A.pop()
    ctx.barrier()

    A.push()
    identf = A.f32(128)
    dl = A.f32(1024)
    xts = [A.f32(CH) for _ in range(FB)]
    t2 = ctx.op("v", lambda: nc.vector.memset(identf[:, :], 1.0))
    tidf = ctx.op("g", lambda: nc.gpsimd.affine_select(
        identf[:, :], identf[:, :], pattern=[[-1, 128]], base=0,
        channel_multiplier=1, compare_op=ALU.is_equal, fill=0.0), t2)
    pbg3 = {}
    stg = []
    dl2 = [dl, A.f32(1024)]
    vtok2 = {}
    dstr = ("ms1", "cc0")
    for seq in range(NSEQ):
        for ch in range(NCH):
            lts = [None, None]
            for fb in range(FB):
                lts[fb % 2] = ctx.dma(
                    "q", "xld" + str(fb % 2), xts[fb],
                    xs.ap()[seq][fb * 128:(fb + 1) * 128,
                                 ch * CH:(ch + 1) * CH],
                    deps=tuple(stg[-1:]) if fb < 2 else ())
            stg = []
            vtok = None
            for tt in range(4):
                p = tt % 2
                r0 = seq * T + ch * CH + tt * 128
                dld = ctx.dma("q", dstr[p], dl2[p][:, :],
                              dlt.ap()[r0:r0 + 128, :],
                              deps=(vtok2.get(p), ctx.tok(dstr[p])))
                for fb in range(FB):
                    ctx.wait("t", ctx.tok("t"), dld, tidf, pbg3.get(fb % 2))
                    itp = nc.tensor.transpose(
                        pb[fb % 2][:, 0:128],
                        dl2[p][:, fb * 128:(fb + 1) * 128], identf[:, :])
                    ttp = ctx.mark("t", itp)
                    vtok = ctx.op("v", lambda fb=fb, tt=tt, p=p:
                                  nc.vector.tensor_add(
                                      xts[fb][:, tt * 128:(tt + 1) * 128],
                                      xts[fb][:, tt * 128:(tt + 1) * 128],
                                      pb[fb % 2][:, 0:128]), ttp, lts[fb % 2])
                    pbg3[fb % 2] = vtok
                vtok2[p] = vtok
            for fb in range(FB):
                stg.append(ctx.dma(
                    "q", "xst", xs.ap()[seq][fb * 128:(fb + 1) * 128,
                                             ch * CH:(ch + 1) * CH],
                    xts[fb], deps=(vtok,)))
    ctx.stream_ready = (ctx.tok("xst"), ctx.tok("ms3"))
    A.pop()
    A.pop()


def build_program(stages=ALL_STAGES):
    nc = bass.Bass("TRN2", target_bir_lowering=False)
    global _LAST_CTX
    dt = {}

    def din(name, shape, dtype=F32):
        dt[name] = nc.dram_tensor(name, list(shape), dtype, kind="ExternalInput")

    din("x", [NSEQ, C, T])
    din("wfc", [C, HID]); din("wpr", [HID, C])
    din("wk", [C, C]); din("wv", [C, C]); din("wr", [C, C]); din("wo", [C, C])
    din("fwk", [C, HID]); din("fwr", [C, C]); din("fwv", [HID, C])
    for nm in ("ln1g", "ln1b", "ln2g", "ln2b", "tdec", "tfirst",
               "tmk", "tmv", "tmr", "ftmk", "ftmr"):
        din(nm, [128, FB])
    din("wqkv", [C, 3 * C]); din("bqkv", [128, 24]); din("bqkvr", [1, 3 * C]); din("boutr", [1, C])
    din("wout", [C, C]); din("bout", [128, FB])
    din("wg", [C, NE])
    din("w1", [NE, C, MH]); din("w3", [NE, C, MH]); din("w2", [NE, MH, C])

    y_out = nc.dram_tensor("y", [NSEQ, C, T], F32, kind="ExternalOutput")
    xs = nc.dram_tensor("xs", [NSEQ, C, T], F32, kind="Internal")
    xs2 = nc.dram_tensor("xs2", [NSEQ, C, T], F32, kind="Internal")
    aux = {
        "xnfm": nc.dram_tensor("xnfm", [NSEQ, C, T], BF16, kind="Internal"),
        "cc_in": nc.dram_tensor("cc_in", [NE, NTOK], F32, kind="Internal"),
        "tab": nc.dram_tensor("tab", [TROWS, 1040], BF16, kind="Internal"),
        "dlt": nc.dram_tensor("dlt", [TROWS, C], F32, kind="Internal"),
        "cc_out": [nc.dram_tensor(f"cc_out{i}", [NCORES, NE, NTOK], F32,
                                  kind="Internal", addr_space="Shared")
                   for i in range(2)],
        "w1b": nc.dram_tensor("w1b", [NE, C, MH], BF16, kind="Internal"),
        "w3b": nc.dram_tensor("w3b", [NE, C, MH], BF16, kind="Internal"),
        "w2b": nc.dram_tensor("w2b", [NE, MH, C], BF16, kind="Internal"),
    }

    ctx = Ctx(nc)
    with contextlib.ExitStack() as stack:
        ctx.init_sems(stack)
        big = stack.enter_context(nc.sbuf_tensor("big", [128, BIG_N], BF16))
        sm = stack.enter_context(nc.sbuf_tensor("sm", [128, 512], F32))
        ctx.pb = [stack.enter_context(nc.psum_tensor(f"pb{i}", [128, 512], F32))
                  for i in range(8)]
        ctx.arena = Arena(big)
        stack.enter_context(nc.Block())

        smal = {}
        off = 0
        names = ("ln1g", "ln1b", "ln2g", "ln2b", "tdec", "tfirst",
                 "tmk", "tmv", "tmr", "ftmk", "ftmr")
        for nm in names:
            smal[nm] = sm[:, off:off + FB]; off += FB
            ctx.dma("q", "ini", smal[nm], dt[nm].ap())
        smal["bqkv"] = sm[:, off:off + 24]; off += 24
        ctx.dma("q", "ini", smal["bqkv"], dt["bqkv"].ap())
        smal["bout"] = sm[:, off:off + FB]; off += FB
        ctx.dma("q", "ini", smal["bout"], dt["bout"].ap())
        tok_small = ctx.tok("ini")
        ctx.c_rsum = sm[:, off:off + 1]; off += 1
        ctx.c_ones = sm[:, off:off + 1]; off += 1
        ctx.c_brow = sm[0:1, off:off + 128]; off += 128
        ctx.c_eps = sm[:, off:off + 1]; off += 1
        ctx.c_o128 = sm[:, off:off + 1]; off += 1
        ctx.c_ones16 = sm.bitcast(F16)[:, 2 * off:2 * off + 1]; off += 1
        nc.vector.memset(ctx.c_rsum, 1.0 / 1024.0)
        nc.vector.memset(ctx.c_ones, 1.0)
        nc.vector.memset(ctx.c_eps, 1e-5)
        nc.vector.memset(ctx.c_o128, 1.0 / 128.0)
        nc.vector.memset(ctx.c_ones16, 1.0)
        tc = ctx.mark("v", nc.vector.memset(ctx.c_brow, 1.0))
        ctx.wait("t", tc)
        ctx.ln_prior = []
        ctx.lnsq_guard = None
        ctx.moe_g_guard = None
        ctx.moe_x_guard = None
        ctx.moe_pb1 = {}
        drv = {"wdec": sm[:, off:off + FB], "eu": sm[:, off + FB:off + 2 * FB]}
        off += 2 * FB
        t1 = ctx.op("s", lambda: nc.scalar.activation(
            drv["wdec"], smal["tdec"], AF.Exp), tok_small)
        t2 = ctx.op("s", lambda: nc.scalar.activation(
            drv["wdec"], drv["wdec"], AF.Exp, scale=-1.0), t1)
        ts = ctx.op("s", lambda: nc.scalar.activation(
            drv["eu"], smal["tfirst"], AF.Exp), tok_small)
        ctx.drv_tok = ts
        ctx.smal = smal
        ctx.drv = drv

        for s_ in range(NSEQ):
            tin = ctx.dma("g", "ms0", xs.ap()[s_], dt["x"].ap()[s_])
        ctx.stream_ready = (tin,)

        ctx.pcw_tok = {}
        if "moe" in stages:
            for e in range(NE):
                ctx.dma("g", "pcw", aux["w1b"].ap()[e], dt["w1"].ap()[e])
                ctx.dma("g", "pcw", aux["w3b"].ap()[e], dt["w3"].ap()[e])
                ctx.dma("g", "pcw", aux["w2b"].ap()[e], dt["w2"].ap()[e])
            for e in range(NE):
                ctx.pcw_tok[e] = ctx.tok("pcw")

        moe_i = 0
        for st in stages:
            if st == "f0":
                emit_ffn(ctx, dt, xs)
            elif st == "rwkv":
                emit_rwkv(ctx, dt, xs, xs2)
            elif st == "attn":
                emit_attn(ctx, dt, xs)
            elif st == "moe":
                emit_moe(ctx, dt, xs, {**aux, "cc": aux["cc_out"][moe_i]})
                moe_i += 1

        _LAST_CTX = ctx
        ctx.barrier()
        ctx.wait("g", ctx.tok("xst"))
        for s_ in range(NSEQ):
            tout = ctx.dma("g", "fin", y_out.ap()[s_], xs.ap()[s_])
        ctx.wait("g", tout)
    return nc


def _prep_inputs(inputs):
    f32 = np.float32
    g = lambda k: np.asarray(inputs[k], dtype=f32)

    def vec128(v):
        return np.ascontiguousarray(np.asarray(v, f32).reshape(-1)
                                    .reshape(FB, 128).T)

    com = {
        "wfc": np.ascontiguousarray(g("L0_ffn_W_fc").T),
        "wpr": np.ascontiguousarray(g("L0_ffn_W_proj").T),
        "wk": np.ascontiguousarray(g("L0_attn_Wk").T),
        "wv": np.ascontiguousarray(g("L0_attn_Wv").T),
        "wr": np.ascontiguousarray(g("L0_attn_Wr").T),
        "wo": np.ascontiguousarray(g("L0_attn_Wo").T),
        "fwk": np.ascontiguousarray(g("L0_attn_fWk").T),
        "fwr": np.ascontiguousarray(g("L0_attn_fWr").T),
        "fwv": np.ascontiguousarray(g("L0_attn_fWv").T),
        "ln1g": vec128(g("L0_attn_ln1_g")), "ln1b": vec128(g("L0_attn_ln1_b")),
        "ln2g": vec128(g("L0_attn_ln2_g")), "ln2b": vec128(g("L0_attn_ln2_b")),
        "tdec": vec128(g("L0_attn_time_decay")),
        "tfirst": vec128(g("L0_attn_time_first")),
        "tmk": vec128(g("L0_attn_tmk")), "tmv": vec128(g("L0_attn_tmv")),
        "tmr": vec128(g("L0_attn_tmr")),
        "ftmk": vec128(g("L0_attn_f_tmk")), "ftmr": vec128(g("L0_attn_f_tmr")),
        "wqkv": np.ascontiguousarray(g("L1_attn_Wqkv")),
        "bqkv": np.ascontiguousarray(g("L1_attn_bqkv").reshape(24, 128).T),
        "wout": np.ascontiguousarray(g("L1_attn_Wout")),
        "bqkvr": np.ascontiguousarray(g("L1_attn_bqkv").reshape(1, 3 * C)),
        "boutr": np.ascontiguousarray(g("L1_attn_bout").reshape(1, C)),
        "bout": vec128(g("L1_attn_bout")),
        "wg": np.ascontiguousarray(g("L1_ffn_Wg").T),
        "w1": np.ascontiguousarray(g("L1_ffn_W1").transpose(0, 2, 1)),
        "w3": np.ascontiguousarray(g("L1_ffn_W3").transpose(0, 2, 1)),
        "w2": np.ascontiguousarray(g("L1_ffn_W2").transpose(0, 2, 1)),
    }
    x = g("L0_attn_x")
    in_maps = []
    for c in range(NCORES):
        m = dict(com)
        m["x"] = np.ascontiguousarray(x[c * NSEQ:(c + 1) * NSEQ].transpose(0, 2, 1))
        in_maps.append(m)
    return in_maps


_LAST_CTX = None
_PROG_CACHE = {}


def kernel(**inputs):
    key = tuple(ALL_STAGES)
    if key not in _PROG_CACHE:
        _PROG_CACHE[key] = build_program(ALL_STAGES)
    nc = _PROG_CACHE[key]
    in_maps = _prep_inputs(inputs)
    res = run_bass_kernel_spmd(nc, in_maps, core_ids=list(range(NCORES)))
    outs = [r["y"].transpose(0, 2, 1) for r in res.results]
    return np.ascontiguousarray(np.concatenate(outs, axis=0)).astype(np.float32)


# version 14
# speedup vs baseline: 1.0347x; 1.0347x over previous
import contextlib
import numpy as np
import concourse.bass as bass
import concourse.mybir as mybir
from concourse.bass_core_rust import IndirectOffsetOnAxis
from concourse.bass_utils import run_bass_kernel_spmd

F32 = mybir.dt.float32
F16 = mybir.dt.float16
BF16 = mybir.dt.bfloat16
I32 = mybir.dt.int32
AF = mybir.ActivationFunctionType
ALU = mybir.AluOpType

NCORES = 8
B, T, C = 16, 2048, 1024
NSEQ = B // NCORES
FB = C // 128
HID = 4096
HB = HID // 128
NE = 8
MH = 2048
CH = 512
NCH = T // CH
NTOK = NSEQ * T

SNN_B = (0.24419428096993973, 0.4639691338428854, 0.6861859295255306,
         0.9059607823984763, 1.0839784132255623)

ALL_STAGES = ("f0", "rwkv", "f0", "moe", "attn", "moe")
BIG_N = 100 * 1024


class Ctx:
    def __init__(self, nc):
        self.nc = nc
        self.E = {"t": nc.tensor, "v": nc.vector, "s": nc.scalar,
                  "g": nc.gpsimd, "q": nc.sync}
        self.sem = {}
        self.cnt = {}
        self.waited = {}

    DMA_STREAMS = ("ini", "xld0", "xld1", "xst", "wt0", "wt1", "wA0", "wA1",
                   "wB0", "wB1", "wC0", "wC1", "ms0", "ms1", "ms2", "ms3",
                   "ms4", "cc0", "cc1", "pcw", "wD0", "wD1", "fin")

    def init_sems(self, stack):
        for name in ("t", "v", "s", "g", "q") + self.DMA_STREAMS:
            self.sem[name] = stack.enter_context(self.nc.semaphore("sem_" + name))
            self.cnt[name] = 0

    def tok(self, name):
        return (name, self.cnt[name])

    def mark(self, name, inst, n=1):
        self.cnt[name] += n
        inst.then_inc(self.sem[name], n)
        return (name, self.cnt[name])

    def wait(self, waiter, *tokens):
        for tk in tokens:
            if tk is None:
                continue
            sname, v = tk
            if v <= 0:
                continue
            key = (waiter, sname)
            if self.waited.get(key, 0) >= v:
                continue
            self.waited[key] = v
            self.E[waiter].wait_ge(self.sem[sname], v)

    def op(self, eng, emit, *deps):
        self.wait(eng, self.tok(eng), *deps)
        return self.mark(eng, emit())

    def dma(self, qname, stream, out, in_, deps=(), transpose=False,
            in_offset=None, out_offset=None, **kw):
        eng = self.E[qname]
        self.wait(qname, *deps)
        if transpose:
            inst = eng.dma_start_transpose(out, in_)
        elif in_offset is not None or out_offset is not None:
            inst = eng.indirect_dma_start(out=out, out_offset=out_offset,
                                          in_=in_, in_offset=in_offset, **kw)
        else:
            inst = eng.dma_start(out=out, in_=in_, **kw)
        return self.mark(stream, inst, 16)

    def barrier(self):
        toks = [self.tok(s) for s in self.DMA_STREAMS]
        for name in ("t", "v", "s", "g", "q"):
            self.wait(name, self.tok(name), *toks)
            toks.append(self.mark(name, self.E[name].nop()))
        for name in ("t", "v", "s", "g", "q"):
            self.wait(name, *toks)


class Arena:

    def __init__(self, big):
        self.big = big
        self.off = 0
        self._marks = []

    def push(self):
        self._marks.append(self.off)

    def pop(self):
        self.off = self._marks.pop()

    def bf(self, n):
        o = self._align(2)
        self.off = o + 2 * n
        assert self.off <= 2 * BIG_N, f"SBUF arena overflow: {self.off}"
        return self.big[:, o // 2: o // 2 + n]

    def f32(self, n):
        o = self._align(4)
        self.off = o + 4 * n
        assert self.off <= 2 * BIG_N, f"SBUF arena overflow: {self.off}"
        return self.big.bitcast(F32)[:, o // 4: o // 4 + n]

    def f16(self, n):
        o = self._align(4)
        self.off = o + 2 * n
        assert self.off <= 2 * BIG_N, f"SBUF arena overflow: {self.off}"
        return self.big.bitcast(F16)[:, o // 2: o // 2 + n]

    def i32(self, n):
        o = self._align(4)
        self.off = o + 4 * n
        assert self.off <= 2 * BIG_N, f"SBUF arena overflow: {self.off}"
        return self.big.bitcast(I32)[:, o // 4: o // 4 + n]

    def _align(self, a):
        return (self.off + a - 1) // a * a


def ln_chunk(ctx, seq, ch, xs, xts, xnts, gain=None, bias=None, src=None,
             xts_free=(), apply_deps=()):
    nc, pb = ctx.nc, ctx.pb
    t0 = ch * CH
    if src is None:
        lts = [None, None]
        for fb in range(FB):
            lts[fb % 2] = ctx.dma(
                "q", "xld" + str(fb % 2), xts[fb],
                xs.ap()[seq][fb * 128:(fb + 1) * 128, t0:t0 + CH],
                deps=xts_free if fb < 2 else ())
        ldtoks = [lts[fb % 2] for fb in range(FB)]
    else:
        ldtoks = [src] * FB
    sqt = [None, None]
    tsqs = []
    for fb in range(FB):
        sq = ctx.lnsq[:, (fb % 2) * CH:(fb % 2) * CH + CH]
        tsq = ctx.op("s", lambda sq=sq, fb=fb: nc.scalar.activation(
            sq, xts[fb], AF.Square), ldtoks[fb], sqt[fb % 2], ctx.lnsq_guard,
            *ctx.ln_prior)
        tsqs.append(tsq)
        ctx.wait("t", ctx.tok("t"), ldtoks[fb], tsq,
                 *(ctx.ln_prior if fb == 0 else ()))
        nc.tensor.matmul(pb[4][0:1, 0:CH], ctx.c_rsum, xts[fb],
                         start=(fb == 0), stop=(fb == FB - 1))
        i2 = nc.tensor.matmul(pb[5][0:1, 0:CH], ctx.c_rsum, sq,
                              start=(fb == 0), stop=(fb == FB - 1))
        sqt[fb % 2] = ctx.mark("t", i2)
    tstat = ctx.tok("t")
    mu2 = ctx.lnrows[0:1, 0:CH]
    rstd = ctx.lnrows[0:1, CH:2 * CH]
    mrs = ctx.lnrows[0:1, 2 * CH:3 * CH]
    ts0 = ctx.op("s", lambda: nc.scalar.activation(mu2, pb[4][0:1, 0:CH],
                                                   AF.Square), tstat, *ctx.ln_prior)
    tv = ctx.op("v", lambda: nc.vector.tensor_sub(mu2, pb[5][0:1, 0:CH], mu2),
                ts0, tstat)
    ts = ctx.op("s", lambda: nc.scalar.activation(rstd, mu2, AF.Ln,
                                                  bias=ctx.c_eps[0:1, :]), tv)
    tr = ctx.op("s", lambda: nc.scalar.activation(rstd, rstd, AF.Exp,
                                                  scale=-0.5), ts)
    tv2 = ctx.op("v", lambda: nc.vector.tensor_mul(mrs, pb[4][0:1, 0:CH], rstd), tr)
    ctx.wait("t", ctx.tok("t"), tv2, ts0)
    ib = None
    for j in range(0, CH, 512):
        w = min(512, CH - j)
        nc.tensor.matmul(pb[4][:, j:j + w], ctx.c_brow,
                         rstd[0:1, j:j + w], start=True, stop=True)
        ib = nc.tensor.matmul(pb[5][:, j:j + w], ctx.c_brow,
                              mrs[0:1, j:j + w], start=True, stop=True)
    tb = ctx.mark("t", ib)
    outs = []
    t1tok = [tsqs[-2], tsqs[-1]]
    for fb in range(FB):
        t1 = ctx.lnsq[:, 0:CH] if fb % 2 == 0 else ctx.lnsq[:, CH:2 * CH]
        tm1 = ctx.op("v", lambda t1=t1, fb=fb: nc.vector.tensor_mul(
            t1, xts[fb], pb[4][:, 0:CH]), tb, t1tok[fb % 2],
            *(apply_deps if fb == 0 else ()))
        if gain is None:
            iv = ctx.op("v", lambda t1=t1, fb=fb: nc.vector.tensor_sub(
                xnts[fb], t1, pb[5][:, 0:CH]), tm1)
            outs.append(iv)
            t1tok[fb % 2] = iv
        else:
            tv3 = ctx.op("v", lambda t1=t1: nc.vector.tensor_sub(
                t1, t1, pb[5][:, 0:CH]), tm1)
            si = ctx.op("v", lambda t1=t1, fb=fb: nc.vector.tensor_scalar(
                xnts[fb], t1, gain[:, fb:fb + 1], bias[:, fb:fb + 1],
                ALU.mult, ALU.add), tv3)
            outs.append(si)
            t1tok[fb % 2] = si
    ctx.ln_prior = [ctx.tok("v"), ctx.tok("s")]
    return outs, ldtoks


def emit_ffn(ctx, dt, xs):
    nc, pb = ctx.nc, ctx.pb
    ctx.barrier()
    A = ctx.arena
    A.push()
    wpr = A.bf(HB * C)
    wfcq = [A.bf(FB * 1024) for _ in range(2)]
    h = A.bf(HB * CH)
    xn = [A.bf(CH) for _ in range(FB)]
    rt = [A.bf(CH) for _ in range(2)]
    xts = [A.f32(CH) for _ in range(FB)]
    yt = A.f32(CH)
    acc = A.f32(CH)
    ctx.lnsq = A.f32(2 * CH)
    ctx.lnrows = A.f32(3 * CH)

    twpr = None
    for hb in range(HB):
        twpr = ctx.dma("g", "wt0", wpr[:, hb * C:(hb + 1) * C],
                       dt["wpr"].ap()[hb * 128:(hb + 1) * 128, :])
    ctx.wait("q", *ctx.stream_ready)

    nq_total = NSEQ * NCH * 4

    def load_wfcq(qg, dep):
        buf = wfcq[qg % 2]
        q = qg % 4
        tq = None
        for fb in range(FB):
            tq = ctx.dma("g", "wA" + str(qg % 2),
                         buf[:, fb * 1024:(fb + 1) * 1024],
                         dt["wfc"].ap()[fb * 128:(fb + 1) * 128,
                                        q * 1024:(q + 1) * 1024],
                         deps=(dep,) if fb == 0 else ())
        return tq

    wfct = {0: load_wfcq(0, None), 1: load_wfcq(1, None)}
    qcons = {}
    evt = {}
    ebt = {}
    ybt = {}
    rtt = [None, None]
    resid_tok = []
    for seq in range(NSEQ):
        for ch in range(NCH):
            cidx = seq * NCH + ch
            xn_toks, _ = ln_chunk(ctx, seq, ch, xs, xts, xn,
                                  xts_free=resid_tok[-1:])
            resid_tok = []
            for q in range(4):
                qg = cidx * 4 + q
                ctx.wait("t", wfct[qg], *(xn_toks if q == 0 else ()))
                for k8 in range(8):
                    hb = q * 8 + k8
                    bank = pb[hb % 4]
                    ctx.wait("t", ctx.tok("t"), ebt.get(hb % 4))
                    for fb in range(FB):
                        im = nc.tensor.matmul(
                            bank[:, 0:CH],
                            wfcq[qg % 2][:, fb * 1024 + k8 * 128:
                                         fb * 1024 + k8 * 128 + 128],
                            xn[fb], start=(fb == 0), stop=(fb == FB - 1))
                    tm = ctx.mark("t", im)
                    tmx = ctx.op("v", lambda p=hb % 2, bank=bank:
                                 nc.vector.tensor_scalar_max(
                                     rt[p], bank[:, 0:CH], 0.0),
                                 tm, rtt[hb % 2])
                    tie = ctx.op("s", lambda p=hb % 2, hb=hb:
                                 nc.scalar.activation(
                                     h[:, hb * CH:(hb + 1) * CH], rt[p],
                                     AF.Square), tmx, evt.get(hb))
                    evt[hb] = ebt[hb % 4] = rtt[hb % 2] = tie
                qcons[qg] = ctx.tok("t")
                if qg + 2 < nq_total:
                    wfct[qg + 2] = load_wfcq(qg + 2, qcons[qg])
            for cb in range(FB):
                ybank = pb[6 + cb % 2]
                ctx.wait("t", ctx.tok("t"), twpr, ybt.get(cb % 2))
                for hb in range(HB):
                    ctx.wait("t", evt[hb])
                    im = nc.tensor.matmul(
                        ybank[:, 0:CH],
                        wpr[:, hb * C + cb * 128: hb * C + cb * 128 + 128],
                        h[:, hb * CH:(hb + 1) * CH],
                        start=(hb == 0), stop=(hb == HB - 1))
                tm = ctx.mark("t", im)
                tv = ctx.op("v", lambda ybank=ybank: nc.vector.tensor_copy(
                    yt, ybank[:, 0:CH]), tm)
                ybt[cb % 2] = tv
                ta = ctx.op("v", lambda: nc.vector.tensor_scalar(
                    acc, yt, SNN_B[0], None, ALU.is_gt), tv)
                for i in (1, 2, 3, 4):
                    ta = ctx.op("v", lambda i=i: nc.vector.scalar_tensor_tensor(
                        acc, yt, SNN_B[i], acc, ALU.is_gt, ALU.add), ta)
                tvx = ctx.op("v", lambda cb=cb: nc.vector.scalar_tensor_tensor(
                    xts[cb], acc, 0.1, xts[cb], ALU.mult, ALU.add), ta)
                resid_tok.append(ctx.dma(
                    "q", "xst", xs.ap()[seq][cb * 128:(cb + 1) * 128,
                                             ch * CH:(ch + 1) * CH], xts[cb],
                    deps=(tvx,)))
    ctx.stream_ready = (ctx.tok("xst"), ctx.tok("ms3"))
    A.pop()


def emit_rwkv(ctx, dt, xs, xs2):
    nc, pb = ctx.nc, ctx.pb
    ctx.barrier()
    A = ctx.arena
    sm = ctx.smal
    drv = ctx.drv

    A.push()
    wk = A.bf(FB * C); wv = A.bf(FB * C); wr = A.bf(FB * C); wo = A.bf(FB * C)
    xts = [A.f32(CH) for _ in range(FB)]
    xp = xts
    xn1 = [A.bf(CH + 1) for _ in range(FB)]
    xk = [A.bf(CH) for _ in range(FB)]
    xv = [A.bf(CH) for _ in range(FB)]
    xr = [A.bf(CH) for _ in range(FB)]
    kt = [A.f32(CH) for _ in range(FB)]
    vt = [A.f32(CH) for _ in range(FB)]
    sr = [A.bf(CH) for _ in range(FB)]
    rw = [A.bf(CH) for _ in range(FB)]
    At = A.f32(CH + 1); Bt = A.f32(CH + 1)
    ab_st = A.f32(2 * FB)
    ctx.lnsq = A.f32(2 * CH)
    ctx.lnrows = A.f32(3 * CH)

    tw = None
    for w_ap, nm in ((wk, "wk"), (wv, "wv"), (wr, "wr"), (wo, "wo")):
        for fb in range(FB):
            tw = ctx.dma("g", "wt0", w_ap[:, fb * C:(fb + 1) * C],
                         dt[nm].ap()[fb * 128:(fb + 1) * 128, :])
    ctx.wait("q", *ctx.stream_ready)

    ebt = {}
    obt = {}
    scan_dep = []
    r1_resid = []
    for seq in range(NSEQ):
        tz = ctx.op("v", lambda: nc.vector.memset(ab_st[:, 0:2 * FB], 0.0),
                    *scan_dep)
        zcols = []
        for fb in range(FB):
            zcols.append(ctx.op("v", lambda fb=fb: nc.vector.memset(
                xn1[fb][:, 0:1], 0.0)))
        for ch in range(NCH):
            xp_toks, _ = ln_chunk(ctx, seq, ch, xs, xts, xp,
                                  xts_free=r1_resid[-1:])
            if ch > 0:
                for fb in range(FB):
                    ctx.op("v", lambda fb=fb: nc.vector.tensor_copy(
                        xn1[fb][:, 0:1], xn1[fb][:, CH:CH + 1]))
            xn_views = [xn1[fb][:, 1:1 + CH] for fb in range(FB)]
            t_src = xp_toks[-1]
            xn_toks, _ = ln_chunk(ctx, seq, ch, xs, xp, xn_views,
                                  gain=sm["ln1g"], bias=sm["ln1b"], src=t_src)
            mix_toks = []
            for fb in range(FB):
                td = ctx.op("v", lambda fb=fb: nc.vector.tensor_sub(
                    rw[fb], xn1[fb][:, 1:1 + CH], xn1[fb][:, 0:CH]), xn_toks[fb])
                ctx.op("v", lambda fb=fb: nc.vector.scalar_tensor_tensor(
                    xk[fb], rw[fb], sm["tmk"][:, fb:fb + 1], xn1[fb][:, 0:CH],
                    ALU.mult, ALU.add), td)
                ctx.op("v", lambda fb=fb: nc.vector.scalar_tensor_tensor(
                    xv[fb], rw[fb], sm["tmv"][:, fb:fb + 1], xn1[fb][:, 0:CH],
                    ALU.mult, ALU.add))
                tm3 = ctx.op("v", lambda fb=fb: nc.vector.scalar_tensor_tensor(
                    xr[fb], rw[fb], sm["tmr"][:, fb:fb + 1], xn1[fb][:, 0:CH],
                    ALU.mult, ALU.add))
                mix_toks.append(tm3)
            proj_ev = []
            for w_ap, xin, which in ((wk, xk, "k"), (wv, xv, "v"), (wr, xr, "r")):
                for ob in range(FB):
                    bank = pb[(ob % 4)]
                    ctx.wait("t", ctx.tok("t"), tw, ebt.get(ob % 4),
                             mix_toks[-1])
                    for fb in range(FB):
                        im = nc.tensor.matmul(
                            bank[:, 0:CH],
                            w_ap[:, fb * C + ob * 128: fb * C + ob * 128 + 128],
                            xin[fb], start=(fb == 0), stop=(fb == FB - 1))
                    tm = ctx.mark("t", im)
                    if which == "k":
                        ev = ctx.op("s", lambda ob=ob, bank=bank:
                                    nc.scalar.copy(kt[ob], bank[:, 0:CH]),
                                    tm)
                    elif which == "v":
                        ev = ctx.op("v", lambda ob=ob, bank=bank:
                                    nc.vector.tensor_copy(vt[ob], bank[:, 0:CH]),
                                    tm)
                    else:
                        ev = ctx.op("s", lambda ob=ob, bank=bank:
                                    nc.scalar.activation(sr[ob], bank[:, 0:CH],
                                                         AF.Sigmoid), tm)
                    ebt[ob % 4] = ev
                    proj_ev.append(ev)
            Et = ctx.lnsq[:, 0:CH]
            EVt = ctx.lnsq[:, CH:2 * CH]
            rw_toks = []
            for fb in range(FB):
                te = ctx.op("s", lambda fb=fb: nc.scalar.activation(
                    Et, kt[fb], AF.Exp), proj_ev[fb], ctx.ln_prior[0],
                    *( [rw_toks[-1]] if fb else []))
                tev = ctx.op("v", lambda fb=fb: nc.vector.tensor_mul(
                    EVt, Et, vt[fb]), te, proj_ev[FB + fb])
                ca = ctx.op("v", lambda fb=fb: nc.vector.tensor_copy(
                    At[:, 0:1], ab_st[:, fb:fb + 1]), tz)
                ta = ctx.op("v", lambda fb=fb: nc.vector.tensor_tensor_scan(
                    At[:, 1:1 + CH], drv["wdec"][:, fb:fb + 1].to_broadcast([128, CH]),
                    EVt, ab_st[:, fb:fb + 1], ALU.mult, ALU.add), tev, ca,
                    ctx.drv_tok)
                cb_ = ctx.op("v", lambda fb=fb: nc.vector.tensor_copy(
                    Bt[:, 0:1], ab_st[:, FB + fb:FB + fb + 1]))
                tb_ = ctx.op("v", lambda fb=fb: nc.vector.tensor_tensor_scan(
                    Bt[:, 1:1 + CH], drv["wdec"][:, fb:fb + 1].to_broadcast([128, CH]),
                    Et, ab_st[:, FB + fb:FB + fb + 1], ALU.mult, ALU.add), cb_)
                ctx.op("v", lambda fb=fb: nc.vector.tensor_copy(
                    ab_st[:, fb:fb + 1], At[:, CH:CH + 1]), ta)
                ctx.op("v", lambda fb=fb: nc.vector.tensor_copy(
                    ab_st[:, FB + fb:FB + fb + 1], Bt[:, CH:CH + 1]), tb_)
                tn = ctx.op("v", lambda fb=fb: nc.vector.scalar_tensor_tensor(
                    EVt, EVt, drv["eu"][:, fb:fb + 1], At[:, 0:CH],
                    ALU.mult, ALU.add), ta)
                tdn = ctx.op("v", lambda fb=fb: nc.vector.scalar_tensor_tensor(
                    Et, Et, drv["eu"][:, fb:fb + 1], Bt[:, 0:CH],
                    ALU.mult, ALU.add), tb_)
                trl = ctx.op("s", lambda: nc.scalar.activation(
                    Et, Et, AF.Ln), tdn)
                tr_ = ctx.op("s", lambda: nc.scalar.activation(
                    Et, Et, AF.Exp, scale=-1.0), trl)
                ty = ctx.op("v", lambda fb=fb: nc.vector.tensor_mul(
                    kt[fb], EVt, Et), tr_, tn)
                trw = ctx.op("v", lambda fb=fb: nc.vector.tensor_mul(
                    rw[fb], kt[fb], sr[fb]), ty, proj_ev[2 * FB + fb])
                rw_toks.append(trw)
            ctx.lnsq_guard = ctx.tok("v")
            for ob in range(FB):
                bank = pb[6 + ob % 2]
                ctx.wait("t", ctx.tok("t"), obt.get(ob % 2), rw_toks[-1])
                for fb in range(FB):
                    im = nc.tensor.matmul(
                        bank[:, 0:CH],
                        wo[:, fb * C + ob * 128: fb * C + ob * 128 + 128],
                        rw[fb], start=(fb == 0), stop=(fb == FB - 1))
                tm = ctx.mark("t", im)
                tv = ctx.op("v", lambda ob=ob, bank=bank: nc.vector.tensor_add(
                    xp[ob], xp[ob], bank[:, 0:CH]), tm)
                obt[ob % 2] = tv
                r1_resid.append(ctx.dma(
                    "q", "xst",
                    xs2.ap()[seq][ob * 128:(ob + 1) * 128,
                                  ch * CH:(ch + 1) * CH], xp[ob],
                    deps=(tv,)))
            scan_dep = [ctx.tok("xst")]
    A.pop()
    ctx.barrier()
    r1_done = (ctx.tok("xst"),)

    A.push()
    fwv = A.bf(16 * C * 2)
    fwkq = [A.bf(FB * 1024) for _ in range(2)]
    fwr = A.bf(FB * C)
    kk = A.bf(16 * CH * 2)
    xn2 = [A.bf(CH + 1) for _ in range(FB)]
    xk2 = [A.bf(CH) for _ in range(FB)]
    xr2 = [kk[:, fb * CH:(fb + 1) * CH] for fb in range(FB)]
    sr2 = [A.bf(CH) for _ in range(FB)]
    rt = [A.bf(CH) for _ in range(2)]
    xts2 = [A.f32(CH) for _ in range(FB)]
    ctx.lnsq = A.f32(2 * CH)
    ctx.lnrows = A.f32(3 * CH)
    tmp = ctx.lnsq[:, 0:CH]
    xo = ctx.lnsq[:, CH:2 * CH]

    tfv = None
    for hb in range(32):
        tfv = ctx.dma("g", "wt1", fwv[:, hb * C:(hb + 1) * C],
                      dt["fwv"].ap()[hb * 128:(hb + 1) * 128, :])
    for fb in range(FB):
        tfr = ctx.dma("g", "wt1", fwr[:, fb * C:(fb + 1) * C],
                      dt["fwr"].ap()[fb * 128:(fb + 1) * 128, :])
    tfr = ctx.tok("wt1")
    ctx.wait("q", *r1_done)

    nq_total = NSEQ * NCH * 4

    def load_fwkq(qg, dep):
        buf = fwkq[qg % 2]
        q = qg % 4
        tq = None
        for fb in range(FB):
            tq = ctx.dma("g", "wA" + str(qg % 2),
                         buf[:, fb * 1024:(fb + 1) * 1024],
                         dt["fwk"].ap()[fb * 128:(fb + 1) * 128,
                                        q * 1024:(q + 1) * 1024],
                         deps=(dep,) if fb == 0 else ())
        return tq

    fwkt = {0: load_fwkq(0, None), 1: load_fwkq(1, None)}
    ctx.xo_guard = None
    qcons = {}
    evt = {}
    ebt = {}
    ybt = {}
    rtt = [None, None]
    resid_tok = []
    for seq in range(NSEQ):
        zc = []
        for fb in range(FB):
            zc.append(ctx.op("v", lambda fb=fb: nc.vector.memset(
                xn2[fb][:, 0:1], 0.0), *resid_tok[-1:]))
        for ch in range(NCH):
            cidx = seq * NCH + ch
            xn_toks, _ = ln_chunk(ctx, seq, ch, xs2, xts2,
                                  [xn2[fb][:, 1:1 + CH] for fb in range(FB)],
                                  gain=sm["ln2g"], bias=sm["ln2b"],
                                  xts_free=resid_tok[-1:])
            mix_toks = []
            for fb in range(FB):
                td = ctx.op("v", lambda fb=fb: nc.vector.tensor_sub(
                    rt[0], xn2[fb][:, 1:1 + CH], xn2[fb][:, 0:CH]), xn_toks[fb],
                    rtt[0])
                ctx.op("v", lambda fb=fb: nc.vector.scalar_tensor_tensor(
                    xk2[fb], rt[0], sm["ftmk"][:, fb:fb + 1], xn2[fb][:, 0:CH],
                    ALU.mult, ALU.add), td)
                tm2 = ctx.op("v", lambda fb=fb: nc.vector.scalar_tensor_tensor(
                    xr2[fb], rt[0], sm["ftmr"][:, fb:fb + 1], xn2[fb][:, 0:CH],
                    ALU.mult, ALU.add))
                rtt[0] = tm2
                mix_toks.append(tm2)
            for fb in range(FB):
                ctx.op("v", lambda fb=fb: nc.vector.tensor_copy(
                    xn2[fb][:, 0:1], xn2[fb][:, CH:CH + 1]), mix_toks[-1])
            sh_tok = ctx.tok("v")
            sr_toks = []
            for ob in range(FB):
                bank = pb[6 + ob % 2]
                ctx.wait("t", ctx.tok("t"), tfr, ybt.get(ob % 2), mix_toks[-1])
                for fb in range(FB):
                    im = nc.tensor.matmul(
                        bank[:, 0:CH],
                        fwr[:, fb * C + ob * 128: fb * C + ob * 128 + 128],
                        xr2[fb], start=(fb == 0), stop=(fb == FB - 1))
                tm = ctx.mark("t", im)
                ts_ = ctx.op("s", lambda ob=ob, bank=bank: nc.scalar.activation(
                    sr2[ob], bank[:, 0:CH], AF.Sigmoid), tm)
                ybt[ob % 2] = ts_
                sr_toks.append(ts_)
            for q in range(4):
                qg = cidx * 4 + q
                ctx.wait("t", fwkt[qg])
                for k8 in range(8):
                    hb = q * 8 + k8
                    bank = pb[hb % 4]
                    ctx.wait("t", ctx.tok("t"), ebt.get(hb % 4))
                    for fb in range(FB):
                        im = nc.tensor.matmul(
                            bank[:, 0:CH],
                            fwkq[qg % 2][:, fb * 1024 + k8 * 128:
                                         fb * 1024 + k8 * 128 + 128],
                            xk2[fb], start=(fb == 0), stop=(fb == FB - 1))
                    tm = ctx.mark("t", im)
                    tmx = ctx.op("v", lambda p=hb % 2, bank=bank:
                                 nc.vector.tensor_scalar_max(
                                     rt[p], bank[:, 0:CH], 0.0),
                                 tm, rtt[hb % 2])
                    tie = ctx.op("s", lambda p=hb % 2, hb=hb:
                                 nc.scalar.activation(
                                     kk[:, hb * CH:(hb + 1) * CH], rt[p],
                                     AF.Square), tmx, evt.get(hb))
                    evt[hb] = ebt[hb % 4] = rtt[hb % 2] = tie
                qcons[qg] = ctx.tok("t")
                if qg + 2 < nq_total:
                    fwkt[qg + 2] = load_fwkq(qg + 2, qcons[qg])
            for cb in range(FB):
                ybank = pb[6 + cb % 2]
                tld = ctx.dma("q", "ms1", xo,
                              xs.ap()[seq][cb * 128:(cb + 1) * 128,
                                           ch * CH:(ch + 1) * CH],
                              deps=(ctx.xo_guard, *ctx.ln_prior))
                ctx.wait("t", ctx.tok("t"), tfv, ybt.get(cb % 2))
                for hb in range(32):
                    ctx.wait("t", evt[hb])
                    im = nc.tensor.matmul(
                        ybank[:, 0:CH],
                        fwv[:, hb * C + cb * 128: hb * C + cb * 128 + 128],
                        kk[:, hb * CH:(hb + 1) * CH],
                        start=(hb == 0), stop=(hb == 31))
                tm = ctx.mark("t", im)
                tv1 = ctx.op("v", lambda cb=cb, ybank=ybank: nc.vector.tensor_mul(
                    tmp, ybank[:, 0:CH], sr2[cb]), tm, sr_toks[cb])
                ybt[cb % 2] = tv1
                tv2 = ctx.op("v", lambda cb=cb: nc.vector.tensor_add(
                    xts2[cb], xts2[cb], tmp), tv1)
                tv3 = ctx.op("v", lambda cb=cb: nc.vector.tensor_add(
                    xts2[cb], xts2[cb], xo), tv2, tld)
                ctx.xo_guard = tv3
                ctx.lnsq_guard = tv3
                resid_tok.append(ctx.dma(
                    "q", "xst", xs.ap()[seq][cb * 128:(cb + 1) * 128,
                                             ch * CH:(ch + 1) * CH], xts2[cb],
                    deps=(tv3,)))
    ctx.stream_ready = (ctx.tok("xst"), ctx.tok("ms3"))
    A.pop()


def emit_attn(ctx, dt, xs):
    nc, pb = ctx.nc, ctx.pb
    ctx.barrier()
    A = ctx.arena
    A.push()
    wqkv = A.bf(FB * 3 * C)
    qa = A.bf(FB * T)
    ka = A.bf(FB * T)
    va = A.bf(16 * C)
    mask = A.bf(384)
    ident = A.bf(128)
    onesr = A.bf(128)
    vbias = A.bf(C)
    boutr = A.bf(C)

    tz1 = ctx.op("v", lambda: nc.vector.memset(mask[:, :], 0.0))
    tz2 = ctx.op("v", lambda: nc.vector.memset(ident[:, :], 1.0))
    tz3 = ctx.op("v", lambda: nc.vector.memset(onesr[0:1, :], 1.0))
    ti = ctx.op("g", lambda: nc.gpsimd.affine_select(
        ident[:, :], ident[:, :], pattern=[[-1, 128]], base=0,
        channel_multiplier=1, compare_op=ALU.is_equal, fill=0.0), tz2)
    tm1 = ctx.op("g", lambda: nc.gpsimd.affine_select(
        mask[:, :], mask[:, :], pattern=[[1, 384]], base=-64,
        channel_multiplier=-1, compare_op=ALU.is_ge, fill=-240000.0), tz1)
    tm2 = ctx.op("g", lambda: nc.gpsimd.affine_select(
        mask[:, :], mask[:, :], pattern=[[-1, 384]], base=192,
        channel_multiplier=1, compare_op=ALU.is_ge, fill=-240000.0), tm1)
    ctx.dma("g", "ms2", vbias[0:1, :], dt["bqkvr"].ap()[0:1, 2 * C:3 * C])
    tb1 = tb2 = ctx.dma("g", "ms2", boutr[0:1, :], dt["boutr"].ap())
    ctx.dbg = dict(qa=qa, ka=ka, va=va)
    ctx.dbg2 = {}
    tw = None
    for fb in range(FB):
        tw = ctx.dma("g", "wt0", wqkv[:, fb * 3 * C:(fb + 1) * 3 * C],
                     dt["wqkv"].ap()[fb * 128:(fb + 1) * 128, :])
    ctx.wait("q", *ctx.stream_ready)

    for seq in range(NSEQ):
        A.push()
        xn = [A.bf(CH) for _ in range(FB)]
        xts = [A.f32(CH) for _ in range(FB)]
        ctx.lnsq = A.f32(2 * CH)
        ctx.lnrows = A.f32(3 * CH)
        ebt = {}
        for ch in range(NCH):
            xn_toks, _ = ln_chunk(ctx, seq, ch, xs, xts, xn,
                                  xts_free=tuple(ctx.ln_prior))
            t0 = ch * CH
            for which, colo, dst in (("q", 0, qa), ("k", C, ka)):
                for ob in range(FB):
                    bank = pb[ob % 4]
                    ctx.wait("t", ctx.tok("t"), tw, ebt.get(ob % 4), xn_toks[-1])
                    for fb in range(FB):
                        im = nc.tensor.matmul(
                            bank[:, 0:CH],
                            wqkv[:, fb * 3 * C + colo + ob * 128:
                                 fb * 3 * C + colo + ob * 128 + 128],
                            xn[fb], start=(fb == 0), stop=(fb == FB - 1))
                    tm = ctx.mark("t", im)
                    if which == "q":
                        ev = ctx.op("v", lambda ob=ob, bank=bank, t0=t0:
                                    nc.vector.tensor_scalar(
                                        qa[:, ob * T + t0: ob * T + t0 + CH],
                                        bank[:, 0:CH],
                                        ctx.smal["bqkv"][:, ob:ob + 1], None,
                                        ALU.add), tm)
                    else:
                        ev = ctx.op("v", lambda ob=ob, bank=bank, t0=t0:
                                    nc.vector.tensor_scalar(
                                        ka[:, ob * T + t0: ob * T + t0 + CH],
                                        bank[:, 0:CH],
                                        ctx.smal["bqkv"][:, 8 + ob:9 + ob], None,
                                        ALU.add), tm)
                    ebt[ob % 4] = ev
            for st_ in range(4):
                tt = t0 + st_ * 128
                for nb in range(2):
                    bank = pb[6 + nb]
                    ctx.wait("t", ctx.tok("t"), ebt.get(("v", nb)), tb1, tz3)
                    for fb in range(FB):
                        nc.tensor.matmul(
                            bank[:, 0:CH],
                            xn[fb][:, st_ * 128:(st_ + 1) * 128],
                            wqkv[:, fb * 3 * C + 2 * C + nb * CH:
                                 fb * 3 * C + 2 * C + (nb + 1) * CH],
                            start=(fb == 0), stop=False)
                    im = nc.tensor.matmul(
                        bank[:, 0:CH], onesr[0:1, 0:128],
                        vbias[0:1, nb * CH:(nb + 1) * CH],
                        start=False, stop=True)
                    tm = ctx.mark("t", im)
                    ev = ctx.op("v", lambda tt=tt, nb=nb, bank=bank:
                                nc.vector.tensor_copy(
                                    va[:, (tt // 128) * C + nb * CH:
                                       (tt // 128) * C + (nb + 1) * CH],
                                    bank[:, 0:CH]), tm)
                    ebt[("v", nb)] = ev
        A.pop()
        ctx.barrier()
        A.push()
        wout = A.bf(FB * C)
        asb = A.bf(384)
        asb_b = A.bf(384)
        atb = A.bf(384)
        atb_b = A.bf(384)
        dz = A.bf(128)
        dz_b = A.bf(128)
        scb2 = [A.f32(384), A.f32(384)]
        avb2 = [A.bf(FB * 128) for _ in range(2)]
        zrow = A.f32(4)
        xrt2 = [A.f32(128) for _ in range(2)]
        two = None
        for fb in range(FB):
            two = ctx.dma("g", "wt1", wout[:, fb * C:(fb + 1) * C],
                          dt["wout"].ap()[fb * 128:(fb + 1) * 128, :],
                          deps=(ctx.tok("wt1"),) if fb == 0 else ())
        nih = T // 128
        ebt2 = {}
        ctx.xo_guards = {}
        asb2 = [asb, asb_b]
        atb2 = [atb, atb_b]
        dz2 = [dz, dz_b]
        def mk_outproj(ih, avt):
            def emit():
                i0 = ih * 128
                avb = avb2[ih % 2]
                for ob in range(FB):
                    bank = pb[6 + ob % 2]
                    xrt = xrt2[ob % 2]
                    tld = ctx.dma("q", "ms1", xrt,
                                  xs.ap()[seq][ob * 128:(ob + 1) * 128,
                                               i0:i0 + 128],
                                  deps=(ctx.xo_guards.get(ob % 2),))
                    ctx.wait("t", ctx.tok("t"), ebt2.get(("o", ob % 2)), avt,
                             two)
                    for fb in range(FB):
                        im = nc.tensor.matmul(
                            bank[:, 0:128],
                            wout[:, fb * C + ob * 128: fb * C + ob * 128 + 128],
                            avb[:, fb * 128:(fb + 1) * 128],
                            start=(fb == 0), stop=(fb == FB - 1))
                    tm = ctx.mark("t", im)
                    ta = ctx.op("v", lambda bank=bank, xrt=xrt, ob=ob:
                                nc.vector.scalar_tensor_tensor(
                                    xrt, bank[:, 0:128],
                                    ctx.smal["bout"][:, ob:ob + 1], xrt,
                                    ALU.add, ALU.add), tm, tld)
                    ebt2[("o", ob % 2)] = ta
                    tst = ctx.dma("q", "xst" if ob % 2 == 0 else "ms3",
                                  xs.ap()[seq][ob * 128:(ob + 1) * 128,
                                               i0:i0 + 128],
                                  xrt, deps=(ta,))
                    ctx.xo_guards[ob % 2] = tst
                ctx.avb_guard[ih % 2] = ctx.tok("t")
            return emit

        pending_op = None
        ctx.avb_guard = {}
        for ih in range(nih):
            i0 = ih * 128
            jb0 = max(0, ih - 1)
            jb1 = min(nih, ih + 2)
            kw = (jb1 - jb0) * 128
            moff = 128 * (1 - (ih - jb0))
            avt = None
            st = {}

            def stage_scores(h):
                p = h % 2
                fbq = h // 2
                po = (h % 2) * 64
                sb_ = pb[p]
                ctx.wait("t", ctx.tok("t"), ebt2.get(("s", p)), ti, tm2)
                for jj in range(jb0, jb1):
                    nc.tensor.matmul(
                        sb_[:, (jj - jb0) * 128:(jj - jb0 + 1) * 128],
                        qa[po:po + 64, fbq * T + i0: fbq * T + i0 + 128],
                        ka[po:po + 64, fbq * T + jj * 128: fbq * T + (jj + 1) * 128],
                        start=(jj == jb0), stop=False)
                im = nc.tensor.matmul(
                    sb_[:, 0:kw], ident[:, 0:128],
                    mask[:, moff:moff + kw], start=False, stop=True)
                return ctx.mark("t", im)

            def stage_soft(h, tsc):
                p = h % 2
                sb_ = pb[p]
                scb = scb2[p]
                zr = zrow[:, 2 * p:2 * p + 2]
                tev = ctx.op("s", lambda: nc.scalar.copy(
                    scb[:, 0:kw], sb_[:, 0:kw]), tsc, st.get(("sc", p)))
                ebt2[("s", p)] = tev
                tmx = ctx.op("v", lambda: nc.vector.tensor_reduce(
                    zr[:, 0:1], scb[:, 0:kw], mybir.AxisListType.X,
                    ALU.max), tev, st.get(("zr", p)))
                tsk = ctx.op("v", lambda: nc.vector.tensor_scalar(
                    zr[:, 0:1], zr[:, 0:1], -0.125, None, ALU.mult), tmx)
                tex = ctx.op("s", lambda: nc.scalar.activation(
                    asb2[p][:, 0:kw], scb[:, 0:kw], AF.Exp,
                    bias=zr[:, 0:1], scale=0.125,
                    accum_out=zr[:, 1:2]), tsk, st.get(("as", p)))
                st[("sc", p)] = tex
                tzi = ctx.op("v", lambda: nc.vector.reciprocal(
                    zr[:, 1:2], zr[:, 1:2]), tex)
                tdz = ctx.op("v", lambda: nc.vector.tensor_scalar(
                    dz2[p][:, :], ident[:, :], zr[:, 1:2], None, ALU.mult),
                    tzi, ti, st.get(("dz", p)))
                st[("zr", p)] = tdz
                return tex, tdz

            def stage_trans(h, tex, tdz):
                p = h % 2
                atp = pb[2 + p]
                ctx.wait("t", ctx.tok("t"), ebt2.get(("at", p)), tex, tdz)
                im = None
                for jj in range(jb0, jb1):
                    im = nc.tensor.matmul(
                        atp[:, (jj - jb0) * 128:(jj - jb0 + 1) * 128],
                        asb2[p][:, (jj - jb0) * 128:(jj - jb0 + 1) * 128],
                        dz2[p][:, :], start=(jj == jb0), stop=(jj == jb1 - 1))
                tat = ctx.mark("t", im)
                st[("as", p)] = st[("dz", p)] = tat
                tae = ctx.op("s", lambda: nc.scalar.activation(
                    atb2[p][:, 0:kw], atp[:, 0:kw], AF.Copy), tat,
                    st.get(("ab", p)))
                ebt2[("at", p)] = tae
                return tae

            def stage_av(h, tae):
                p = h % 2
                avp = pb[4 + p]
                ctx.wait("t", ctx.tok("t"), ebt2.get(("av", p)), tae)
                im = None
                for jj in range(jb0, jb1):
                    im = nc.tensor.matmul(
                        avp[0:64, 0:128],
                        va[:, jj * C + h * 64: jj * C + h * 64 + 64],
                        atb2[p][:, (jj - jb0) * 128:(jj - jb0 + 1) * 128],
                        start=(jj == jb0), stop=(jj == jb1 - 1))
                tv_ = ctx.mark("t", im)
                st[("ab", p)] = tv_
                tve = ctx.op("v", lambda h=h, avp=avp, avb=avb2[ih % 2]:
                             nc.vector.tensor_copy(
                                 avb[(h % 2) * 64:(h % 2) * 64 + 64,
                                     (h // 2) * 128:(h // 2) * 128 + 128],
                                 avp[0:64, 0:128]), tv_,
                             ctx.avb_guard.get(ih % 2) if h == 0 else None)
                ebt2[("av", p)] = tve
                return tve

            tscs = {0: stage_scores(0)}
            softs = {}
            taes = {}
            for h in range(16):
                if h + 1 < 16:
                    tscs[h + 1] = stage_scores(h + 1)
                if h == 2 and pending_op is not None:
                    pending_op()
                    pending_op = None
                if h >= 2:
                    taes[h - 2] = stage_trans(h - 2, *softs[h - 2])
                if h >= 3:
                    avt = stage_av(h - 3, taes[h - 3])
                softs[h] = stage_soft(h, tscs[h])
            taes[14] = stage_trans(14, *softs[14])
            avt = stage_av(13, taes[13])
            taes[15] = stage_trans(15, *softs[15])
            avt = stage_av(14, taes[14])
            avt = stage_av(15, taes[15])
            pending_op = mk_outproj(ih, avt)
        if pending_op is not None:
            pending_op()
        A.pop()
        ctx.barrier()
    ctx.stream_ready = (ctx.tok("xst"), ctx.tok("ms3"))
    A.pop()


CAPG = 4096
CAPL = 768
CAPT = CAPL // 128
TROWS = 4096 + NE * CAPT + 16


def emit_moe(ctx, dt, xs, aux):
    nc, pb = ctx.nc, ctx.pb
    ctx.barrier()
    A = ctx.arena
    xnfm, cc_in, cc = aux["xnfm"], aux["cc_in"], aux["cc"]
    tab, dlt = aux["tab"], aux["dlt"]
    NIT = 32
    A.push()
    ctx.moe_lint = A.i32(NE * CAPT)
    MHB = MH // 128
    w1 = A.bf(FB * MH)
    w3 = A.bf(FB * MH)
    w2 = A.bf(MHB * C)
    w1b, w3b, w2b = aux["w1b"], aux["w3b"], aux["w2b"]
    tw1_tok = {}; tw3_tok = {}; tw2_tok = {}

    def load_w1(e, *deps):
        for fb in range(FB):
            ctx.dma("q", "wD" + str(e % 2), w1[:, fb * MH:(fb + 1) * MH],
                    w1b.ap()[e][fb * 128:(fb + 1) * 128, :],
                    deps=deps if fb == 0 else ())
        tw1_tok[e] = ctx.tok("wD" + str(e % 2))

    def load_w3(e, *deps):
        for fb in range(FB):
            ctx.dma("q", "wD" + str(e % 2), w3[:, fb * MH:(fb + 1) * MH],
                    w3b.ap()[e][fb * 128:(fb + 1) * 128, :],
                    deps=deps + (ctx.tok("wD" + str(e % 2)),)
                    if fb == 0 else ())
        tw3_tok[e] = ctx.tok("wD" + str(e % 2))

    def load_w2(e, *deps):
        for hb in range(MHB):
            ctx.dma("q", "wD" + str(e % 2), w2[:, hb * C:(hb + 1) * C],
                    w2b.ap()[e][hb * 128:(hb + 1) * 128, :],
                    deps=deps + (ctx.tok("wD" + str(e % 2)),)
                    if hb == 0 else ())
        tw2_tok[e] = ctx.tok("wD" + str(e % 2))

    identb = A.bf(128)
    slt = A.bf(128)
    jrow = A.f16(CAPL)
    ones16 = A.f16(2)
    Sacc = A.f16(CAPL)
    lrowf = A.f32(CAPL)
    sc32 = A.f32(32); m16 = A.f16(32); c16 = A.f16(32); rtb = A.bf(2)
    jkl = A.f32(16)
    lo = A.f32(NE)
    ctx.identb = identb
    tzi = ctx.op("v", lambda: nc.vector.memset(identb[:, :], 1.0))
    tidb = ctx.op("g", lambda: nc.gpsimd.affine_select(
        identb[:, :], identb[:, :], pattern=[[-1, 128]], base=0,
        channel_multiplier=1, compare_op=ALU.is_equal, fill=0.0), tzi)
    t1s = ctx.op("v", lambda: nc.vector.memset(slt[:, :], 1.0))
    tslt = ctx.op("g", lambda: nc.gpsimd.affine_select(
        slt[:, :], slt[:, :], pattern=[[1, 128]], base=-1,
        channel_multiplier=-1, compare_op=ALU.is_ge, fill=0.0), t1s)
    tjr = ctx.op("g", lambda: nc.gpsimd.iota(
        jrow[:, 0:CAPL], pattern=[[1, CAPL]], base=0, channel_multiplier=0,
        allow_small_or_imprecise_dtypes=True))
    to16 = ctx.op("v", lambda: nc.vector.memset(ones16[:, 0:1], 1.0))

    lb = {"grd": None, "base": None, "lrow": None, "sacc": None, "rtbT": None}

    def lb_part1(e, *deps):
        tsm = ctx.op("v", lambda: nc.vector.memset(Sacc[:, 0:CAPL], 0.0),
                     lb["sacc"], *deps)
        tl = ctx.dma("g", "ms4", sc32[:, :],
                     cc_in.ap()[e].rearrange("(p u) -> p u", p=128),
                     deps=(tsm,))
        tm_ = ctx.op("v", lambda e=e: nc.vector.tensor_scalar(
            m16[:, :], sc32[:, :], lo[:, e:e + 1], None, ALU.is_gt), tl)
        tsc = ctx.op("v", lambda: nc.vector.tensor_tensor_scan(
            c16[:, :], ones16[:, 0:1].to_broadcast([128, 32]), m16[:, :], 0.0,
            ALU.mult, ALU.add), tm_, to16)
        lb["p1"] = ctx.op("v", lambda: nc.vector.tensor_copy(
            rtb[:, 0:1], c16[:, 31:32]), tsc, lb["rtbT"])

    def lb_base(bank):
        ctx.wait("t", ctx.tok("t"), lb["p1"], tslt, lb["grd"])
        ib = nc.tensor.matmul(pb[bank][:, 6:7], slt[:, :], rtb[:, 0:1],
                              start=True, stop=True)
        lb["rtbT"] = lb["base"] = ctx.mark("t", ib)

    def lb_staccs(bank):
        tcf = ctx.op("v", lambda: nc.vector.tensor_scalar(
            c16[:, :], c16[:, :], pb[bank][:, 6:7], None, ALU.add),
            lb["base"])
        lb["grd"] = tcf
        tsa = tcf
        for u in range(32):
            tsa = ctx.op("v", lambda u=u: nc.vector.scalar_tensor_tensor(
                Sacc[:, 0:CAPL], jrow[:, 0:CAPL], c16[:, u:u + 1],
                Sacc[:, 0:CAPL], ALU.is_ge, ALU.add), tsa, tjr)
        lb["stacc"] = tsa

    def lb_lrow_a(bank):
        ctx.wait("t", ctx.tok("t"), lb["stacc"], lb["grd"])
        il = nc.tensor.matmul(pb[bank][0:1, 0:512], ctx.c_ones16,
                              Sacc[:, 0:512], start=True, stop=True)
        tlr = ctx.mark("t", il)
        lb["grd"] = ctx.op("v", lambda: nc.vector.tensor_copy(
            lrowf[0:1, 0:512], pb[bank][0:1, 0:512]), tlr)

    def lb_lrow_b(bank):
        ctx.wait("t", ctx.tok("t"), lb["grd"])
        il = nc.tensor.matmul(pb[bank][0:1, 0:256], ctx.c_ones16,
                              Sacc[:, 512:CAPL], start=True, stop=True)
        tlr = ctx.mark("t", il)
        lb["sacc"] = lb["grd"] = ctx.op(
            "v", lambda: nc.vector.tensor_copy(
                lrowf[0:1, 512:CAPL], pb[bank][0:1, 0:256]), tlr)

    def lb_cols(e, bank):
        Lint = ctx.moe_lint
        ctx.wait("t", ctx.tok("t"), lb["grd"])
        ic = None
        for tt in range(CAPT):
            ic = nc.tensor.matmul(pb[bank][:, tt:tt + 1],
                                  lrowf[0:1, tt * 128:(tt + 1) * 128],
                                  ctx.c_ones[0:1, 0:1], start=True, stop=True)
        tic = ctx.mark("t", ic)
        tov = ctx.op("v", lambda bank=bank: nc.vector.tensor_scalar(
            jkl[:, 0:CAPT], pb[bank][:, 0:CAPT], 4095.5, None, ALU.is_gt),
            tic)
        tof = None
        for tt in range(CAPT):
            tof = ctx.op("v", lambda tt=tt, e=e, bank=bank:
                         nc.vector.scalar_tensor_tensor(
                             jkl[:, 8 + tt:9 + tt], jkl[:, tt:tt + 1],
                             float(e * CAPT + tt), pb[bank][:, tt:tt + 1],
                             ALU.mult, ALU.add), tov)
        lb["grd"] = tof
        lg = ctx.op("v", lambda e=e: nc.vector.tensor_copy(
            Lint[:, e * CAPT:(e + 1) * CAPT], jkl[:, 8:8 + CAPT]), tof)
        ctx.moe_lg_e[e] = lg

    A.push()
    wg = A.bf(FB * NE)
    xn = [A.bf(CH) for _ in range(FB)]
    xts = [A.f32(CH) for _ in range(FB)]
    scrow = A.f32(CH)
    xtk = [A.bf(1024) for _ in range(2)]
    zt = A.bf(1040)
    ztf = A.f32(1024)
    ctx.lnsq = A.f32(2 * CH)
    ctx.lnrows = A.f32(3 * CH)
    twg = None
    for fb in range(FB):
        twg = ctx.dma("g", "ms2", wg[:, fb * NE:(fb + 1) * NE],
                      dt["wg"].ap()[fb * 128:(fb + 1) * 128, :],
                      deps=(ctx.tok("ms2"),) if fb == 0 else ())
    tz0 = ctx.op("v", lambda: nc.vector.memset(zt[:, :], 0.0))
    tz1 = ctx.op("v", lambda: nc.vector.memset(ztf[:, :], 0.0))
    ctx.dma("q", "ms1", tab.ap()[4096:4096 + 64, :], zt[0:64, :], deps=(tz0,))
    for r in range(0, TROWS, 128):
        n = min(128, TROWS - r)
        ctx.dma("q", "ms1", dlt.ap()[r:r + n, :], ztf[0:n, :], deps=(tz1,))
    sct = None
    xtkg = [None, None]
    tbg = {}
    for seq in range(NSEQ):
        for ch in range(NCH):
            xn_toks, _ = ln_chunk(ctx, seq, ch, xs, xts, xn,
                                  xts_free=tuple(ctx.ln_prior),
                                  apply_deps=(ctx.tok("xst"), ctx.tok("wC0"),
                                              ctx.tok("wC1")))
            for fb in range(FB):
                ctx.dma("q", "xst",
                        xnfm.ap()[seq][fb * 128:(fb + 1) * 128,
                                       ch * CH:(ch + 1) * CH],
                        xn[fb], deps=(xn_toks[fb],))
            for sub in range(4):
                p = sub % 2
                r0 = seq * T + ch * CH + sub * 128
                bA, bB = (1, 2) if p == 0 else (3, 6)
                pA = pb[bA].bitcast(BF16)
                pB = pb[bB].bitcast(BF16)
                ctx.wait("t", ctx.tok("t"), xn_toks[-1], tidb,
                         tbg.get(p))
                for fb in range(FB):
                    bank = pA if fb < 4 else pB
                    itp = nc.tensor.transpose(
                        bank[:, (fb % 4) * 128:(fb % 4) * 128 + 128],
                        xn[fb][:, sub * 128:(sub + 1) * 128],
                        ctx.identb[:, :])
                ttp = ctx.mark("t", itp)
                te1 = ctx.op("s", lambda p=p, pA=pA: nc.scalar.copy(
                    xtk[p][:, 0:512], pA[:, 0:512]), ttp, xtkg[p])
                te2 = ctx.op("s", lambda p=p, pB=pB: nc.scalar.copy(
                    xtk[p][:, 512:1024], pB[:, 0:512]))
                tbg[p] = te2
                xtkg[p] = ctx.dma("q", "wC" + str(p),
                                  tab.ap()[r0:r0 + 128, 0:1024],
                                  xtk[p][:, :], deps=(te2,))
            ctx.wait("t", ctx.tok("t"), twg, sct, xn_toks[-1])
            for fb in range(FB):
                im = nc.tensor.matmul(pb[0][0:NE, 0:CH],
                                      wg[:, fb * NE:(fb + 1) * NE], xn[fb],
                                      start=(fb == 0), stop=(fb == FB - 1))
            tm = ctx.mark("t", im)
            sct = ctx.op("v", lambda: nc.vector.tensor_copy(
                scrow[0:NE, :], pb[0][0:NE, 0:CH]), tm, ctx.tok("ms1"))
            ctx.dma("q", "ms1", cc_in.ap()[0:NE, seq * T + ch * CH:
                                           seq * T + (ch + 1) * CH],
                    scrow[0:NE, :], deps=(sct,))
    A.pop()
    ctx.barrier()

    load_w1(0, ctx.pcw_tok.get(0))
    load_w3(0, ctx.pcw_tok.get(0))
    load_w2(0, ctx.pcw_tok.get(0))
    A.push()
    sct_all = [A.f32(256) for _ in range(NE)]
    hi = A.f32(NE); mid = A.f32(NE)
    cnts = A.f32(NE); msk = A.i32(NE); msk2 = A.i32(NE); negth = A.f32(NE)
    e1t = A.f32(256); junk = A.f32(256)
    zpart = A.f32(NE)
    thcol = A.f32(1); nthcol = A.f32(1); zhcol = A.f32(1)
    sloc = A.f32(CH); gloc = A.f32(CH); e1loc = A.f32(CH)
    gtmb = A.bf(4 * NE)
    identf = A.f32(128)
    onesf = A.f32(128)
    Lint = ctx.moe_lint
    t2 = ctx.op("v", lambda: nc.vector.memset(identf[:, :], 1.0))
    tidf = ctx.op("g", lambda: nc.gpsimd.affine_select(
        identf[:, :], identf[:, :], pattern=[[-1, 128]], base=0,
        channel_multiplier=1, compare_op=ALU.is_equal, fill=0.0), t2)
    tonf = ctx.op("v", lambda: nc.vector.memset(onesf[:, :], 1.0))
    tcc = ctx.op("g", lambda: nc.gpsimd.collective_compute(
        "AllGather", ALU.bypass, replica_groups=[list(range(NCORES))],
        ins=[cc_in.ap()], outs=[cc.ap()]), ctx.tok("ms1"))
    tld = None
    for e in range(NE):
        for sl in range(NCORES):
            tld = ctx.dma("g", "ms0", sct_all[e][:, sl * 32:(sl + 1) * 32],
                          cc.ap()[sl][e].rearrange("(p u) -> p u", p=128),
                          deps=(tcc,))
    ctx.op("v", lambda: nc.vector.memset(lo, -1000.0))
    t0v = ctx.op("v", lambda: nc.vector.memset(hi, 1000.0))
    pbg = {}
    tsel = {0: t0v, 4: t0v}
    tcg = {}

    def bis_counts(e0):
        ctx.op("v", lambda: nc.vector.tensor_add(
            mid[:, e0:e0 + 4], lo[:, e0:e0 + 4], hi[:, e0:e0 + 4]), tsel[e0])
        tmid = ctx.op("v", lambda: nc.vector.tensor_scalar(
            mid[:, e0:e0 + 4], mid[:, e0:e0 + 4], 0.5, None, ALU.mult))
        tc = None
        for e in range(e0, e0 + 4):
            tc = ctx.op("v", lambda e=e: nc.vector.tensor_scalar(
                junk, sct_all[e][:, :], mid[:, e:e + 1], 0.0, ALU.is_gt,
                ALU.add, accum_out=cnts[:, e:e + 1]), tmid, tld)
        tcg[e0] = tc

    def bis_update(e0, bank):
        ctx.wait("t", ctx.tok("t"), tcg[e0], tonf, pbg.get(bank))
        i3 = nc.tensor.matmul(pb[bank][:, 0:4], onesf, cnts[:, e0:e0 + 4],
                              start=True, stop=True)
        t3 = ctx.mark("t", i3)
        tmk = ctx.op("v", lambda: nc.vector.tensor_scalar(
            msk[:, e0:e0 + 4], pb[bank][:, 0:4], float(CAPG), None,
            ALU.is_ge), t3)
        tmk2 = ctx.op("v", lambda: nc.vector.tensor_scalar(
            msk2[:, e0:e0 + 4], pb[bank][:, 0:4], float(CAPG), None,
            ALU.is_lt), t3)
        pbg[bank] = tmk2
        ctx.op("v", lambda: nc.vector.select(
            lo[:, e0:e0 + 4], msk[:, e0:e0 + 4], mid[:, e0:e0 + 4],
            lo[:, e0:e0 + 4], add_drain=True), tmk)
        tsel[e0] = ctx.op("v", lambda: nc.vector.select(
            hi[:, e0:e0 + 4], msk2[:, e0:e0 + 4], mid[:, e0:e0 + 4],
            hi[:, e0:e0 + 4], add_drain=True), tmk2)

    bis_counts(0)
    bis_counts(4)
    for it in range(NIT):
        bis_update(0, 2)
        if it + 1 < NIT:
            bis_counts(0)
        bis_update(4, 3)
        if it + 1 < NIT:
            bis_counts(4)
    t0v = tsel[4]

    tng = ctx.op("v", lambda: nc.vector.tensor_scalar(
        negth, lo, -1.0, None, ALU.mult), t0v)
    tzp = None
    for e in range(NE):
        te1 = ctx.op("s", lambda e=e: nc.scalar.activation(
            e1t, sct_all[e][:, :], AF.Exp, bias=negth[:, e:e + 1]), tng, tzp)
        tzp = ctx.op("v", lambda e=e: nc.vector.scalar_tensor_tensor(
            junk, sct_all[e][:, :], lo[:, e:e + 1], e1t, ALU.is_gt, ALU.mult,
            accum_out=zpart[:, e:e + 1]), te1)
    ctx.wait("t", ctx.tok("t"), tzp, pbg.get(1))
    i1 = nc.tensor.matmul(pb[1][0:1, 0:NE], ctx.c_ones, zpart,
                          start=True, stop=True)
    t1m = ctx.mark("t", i1)
    tzv = ctx.op("v", lambda: nc.vector.tensor_copy(
        zpart[0:1, 0:NE], pb[1][0:1, 0:NE]), t1m)
    tzr = ctx.op("v", lambda: nc.vector.reciprocal(
        zpart[0:1, 0:NE], zpart[0:1, 0:NE]), tzv)
    tzh = ctx.op("v", lambda: nc.vector.tensor_scalar(
        zpart[0:1, 0:NE], zpart[0:1, 0:NE], 0.5, None, ALU.mult), tzr)
    ctx.wait("t", ctx.tok("t"), tzh, tng, pbg.get(3))
    nc.tensor.matmul(pb[3][0:NE, 0:1], lo, ctx.c_o128, start=True, stop=True)
    ic2 = nc.tensor.matmul(pb[3][0:NE, 1:2], negth, ctx.c_o128,
                           start=True, stop=True)
    tcm = ctx.mark("t", ic2)
    ctx.wait("t", ctx.tok("t"), tzh)
    ib = nc.tensor.matmul(pb[2][:, 0:NE], ctx.c_brow, zpart[0:1, 0:NE],
                          start=True, stop=True)
    tbz = ctx.mark("t", ib)
    tcpy = ctx.op("v", lambda: nc.vector.tensor_copy(
        junk[:, 0:NE], pb[2][:, 0:NE]), tbz)
    pbg[2] = tcpy
    ctx.wait("t", ctx.tok("t"), tcpy)
    ic3 = nc.tensor.matmul(pb[3][0:NE, 2:3], junk[:, 0:NE], ctx.c_o128,
                           start=True, stop=True)
    tcm2 = ctx.mark("t", ic3)
    ctx.op("v", lambda: nc.vector.tensor_copy(thcol[0:NE, :],
                                              pb[3][0:NE, 0:1]), tcm, tcm2)
    ctx.op("v", lambda: nc.vector.tensor_copy(nthcol[0:NE, :],
                                              pb[3][0:NE, 1:2]))
    tcols = ctx.op("v", lambda: nc.vector.tensor_copy(zhcol[0:NE, :],
                                                      pb[3][0:NE, 2:3]))
    pbg[3] = tcols
    tgl = None
    gldg = {}
    ctx.moe_ttg = None
    for seq in range(NSEQ):
        for ch in range(NCH):
            o = seq * T + ch * CH
            tl = ctx.dma("q", "ms1", sloc[0:NE, :], cc_in.ap()[0:NE, o:o + CH],
                         deps=(tgl,))
            te1 = ctx.op("s", lambda: nc.scalar.activation(
                e1loc[0:NE, :], sloc[0:NE, :], AF.Exp,
                bias=nthcol[0:NE, :]), tl, tcols, tgl)
            tg1 = ctx.op("v", lambda: nc.vector.scalar_tensor_tensor(
                gloc[0:NE, :], sloc[0:NE, :], thcol[0:NE, :], e1loc[0:NE, :],
                ALU.is_gt, ALU.mult), te1, ctx.moe_ttg)
            tg2 = ctx.op("v", lambda: nc.vector.tensor_scalar(
                gloc[0:NE, :], gloc[0:NE, :], zhcol[0:NE, :], None, ALU.mult),
                tg1)
            tgl = tg2
            for sub in range(4):
                p = sub % 2
                ctx.wait("t", ctx.tok("t"), tg2, pbg.get(4), tidf)
                it_ = nc.tensor.transpose(
                    pb[4][0:128, 0:NE],
                    gloc[0:NE, sub * 128:(sub + 1) * 128],
                    identf[0:NE, 0:NE])
                tt_ = ctx.mark("t", it_)
                ctx.moe_ttg = tt_
                ghl = gtmb[:, p * 2 * NE:(p + 1) * 2 * NE]
                ctx.op("v", lambda ghl=ghl: nc.vector.tensor_copy(
                    ghl[:, 0:NE], pb[4][:, 0:NE]), tt_, gldg.get(p))
                tgv = ctx.op("v", lambda ghl=ghl: nc.vector.tensor_sub(
                    ghl[:, NE:2 * NE], pb[4][:, 0:NE], ghl[:, 0:NE]))
                pbg[4] = tgv
                gldg[p] = ctx.dma(
                    "q", "wC" + str(p),
                    tab.ap()[o + sub * 128: o + sub * 128 + 128, 1024:1040],
                    ghl, deps=(tgv,))
    ctx.moe_lg_e = {}
    lb["grd"] = pbg.get(4)
    for e in (0, 1):
        lb_part1(e, tcols)
        lb_base(4 + e)
        lb_staccs(4 + e)
        lb_lrow_a(4 + e)
        lb_lrow_b(4 + e)
        lb_cols(e, 4 + e)
    A.pop()
    ctx.barrier()

    A.push()
    WS1 = (512, 256)
    at = A.bf(MHB * CAPL)
    gt2 = [[A.bf(1040) for _ in range(CAPT)] for _ in range(2)]
    xe2 = [A.bf(FB * CAPL) for _ in range(2)]
    ysb = A.f32(1024)
    sgt = A.f32(1)
    ctx.moe_sg = None
    Lint = ctx.moe_lint
    pbg2 = {}
    sc_guard = None
    ctx.moe_ye = {}

    def do_gathers(e):
        p = e % 2
        for tt in range(CAPT):
            ctx.dma("g", "ms2", gt2[p][tt][:, :], tab.ap(),
                    in_offset=IndirectOffsetOnAxis(
                        Lint[:, e * CAPT + tt: e * CAPT + tt + 1], 0),
                    deps=(ctx.moe_lg_e[e], ctx.moe_ye.get(p),
                          ctx.tok("ms0"), ctx.tok("ms2")) if tt == 0 else ())
        return ctx.tok("ms2")

    ctx.xe_guard = {}
    ctx.xe_done = {}
    ctx.xe_pb_free = None

    def mk_xe_tasks(e, tgall):
        p = e % 2
        gt = gt2[p]
        xe = xe2[p]
        tasks = []
        for fb in range(FB):
            for t0, nt in ((0, 4), (4, 2)):
                def task(fb=fb, t0=t0, nt=nt):
                    p4 = pb[4].bitcast(BF16)
                    ctx.wait("t", ctx.tok("t"), tgall, ctx.xe_pb_free, tidb)
                    for i in range(nt):
                        itp = nc.tensor.transpose(
                            p4[:, i * 128:(i + 1) * 128],
                            gt[t0 + i][:, fb * 128:(fb + 1) * 128],
                            identb[:, :])
                    tm = ctx.mark("t", itp)
                    ctx.xe_pb_free = ctx.op(
                        "s", lambda: nc.scalar.copy(
                            xe[:, fb * CAPL + t0 * 128:
                               fb * CAPL + (t0 + nt) * 128],
                            p4[:, 0:nt * 128]), tm, ctx.xe_guard.get(p))
                    ctx.xe_done[p] = ctx.xe_pb_free
                tasks.append(task)
        return tasks

    tgn = do_gathers(0)
    for t_ in mk_xe_tasks(0, tgn):
        t_()
    pend = []
    at_free = None
    for e in range(NE):
        gt = gt2[e % 2]
        xe = xe2[e % 2]
        tgall = tgn
        txe = ctx.xe_done[e % 2]
        le = e + 2 if e + 2 < NE else None
        if le is not None:
            lb_part1(le, tcols)
        if e + 1 < NE:
            tgn = do_gathers(e + 1)
            pend = mk_xe_tasks(e + 1, tgn)
        w1_last = w3_last = None
        for wi, w0 in ((0, 0), (1, 512)):
            W = WS1[wi]
            for hb in range(MHB):
                bank = pb[hb % 2]
                ctx.wait("t", ctx.tok("t"), tw1_tok[e], txe,
                         pbg2.get(hb % 2))
                for fb in range(FB):
                    im = nc.tensor.matmul(
                        bank[:, 0:W],
                        w1[:, fb * MH + hb * 128: fb * MH + hb * 128 + 128],
                        xe[:, fb * CAPL + w0: fb * CAPL + w0 + W],
                        start=(fb == 0), stop=(fb == FB - 1))
                tm = ctx.mark("t", im)
                w1_last = tm
                tse = ctx.op("s", lambda hb=hb, bank=bank, w0=w0, W=W:
                             nc.scalar.activation(
                                 at[:, hb * CAPL + w0: hb * CAPL + w0 + W],
                                 bank[:, 0:W], AF.Silu), tm,
                             pbg2.get(("a", hb, w0)), at_free)
                pbg2[hb % 2] = tse
                pbg2[("a", hb, w0)] = tse
                if le is not None and wi == 0 and hb == 6:
                    lb_base(5)
                if le is not None and wi == 1 and hb == 4:
                    lb_lrow_b(5)
                if le is not None and wi == 1 and hb == 10:
                    lb_cols(le, 5)
            if le is not None and wi == 0:
                lb_staccs(5)
            for hb in range(MHB):
                bank = pb[2 + hb % 2]
                ctx.wait("t", ctx.tok("t"), tw3_tok[e], pbg2.get(2 + hb % 2),
                         pbg2.get(("a", hb, w0)))
                for fb in range(FB):
                    im = nc.tensor.matmul(
                        bank[:, 0:W],
                        w3[:, fb * MH + hb * 128: fb * MH + hb * 128 + 128],
                        xe[:, fb * CAPL + w0: fb * CAPL + w0 + W],
                        start=(fb == 0), stop=(fb == FB - 1))
                tm = ctx.mark("t", im)
                w3_last = tm
                the = ctx.op("v", lambda hb=hb, bank=bank, w0=w0, W=W:
                             nc.vector.tensor_mul(
                                 at[:, hb * CAPL + w0: hb * CAPL + w0 + W],
                                 at[:, hb * CAPL + w0: hb * CAPL + w0 + W],
                                 bank[:, 0:W]), tm, pbg2.get(("h", hb, w0)))
                pbg2[2 + hb % 2] = the
                pbg2[("h", hb, w0)] = the
                if hb % 2 == 1 and pend:
                    pend.pop(0)()
            if le is not None and wi == 0:
                lb_lrow_a(5)
        if e + 1 < NE:
            load_w1(e + 1, w1_last, ctx.pcw_tok.get(e + 1))
            load_w3(e + 1, w3_last)
        w2_last = None
        for tt in range(CAPT):
            for nb in range(2):
                bank = pb[6 + nb]
                ctx.wait("t", ctx.tok("t"), tw2_tok[e], pbg2.get(6 + nb))
                for hb in range(MHB):
                    w0 = 0 if tt < 4 else 512
                    ctx.wait("t", pbg2[("h", hb, w0)])
                    im = nc.tensor.matmul(
                        bank[:, 0:512],
                        at[:, hb * CAPL + tt * 128: hb * CAPL + tt * 128 + 128],
                        w2[:, hb * C + nb * 512: hb * C + (nb + 1) * 512],
                        start=(hb == 0), stop=(hb == MHB - 1))
                tm = ctx.mark("t", im)
                w2_last = tm
                if nb == 0:
                    sgf = ctx.op("v", lambda tt=tt, e=e: nc.vector.tensor_add(
                        sgt, gt[tt][:, 1024 + e:1025 + e],
                        gt[tt][:, 1032 + e:1033 + e]), tgall, ctx.moe_sg)
                tye = ctx.op("v", lambda bank=bank, nb=nb:
                             nc.vector.tensor_scalar(
                                 ysb[:, nb * 512:(nb + 1) * 512],
                                 bank[:, 0:512], sgt,
                                 None, ALU.mult), tm, sc_guard, sgf)
                ctx.moe_sg = tye
                pbg2[6 + nb] = tye
                ctx.moe_ye[e % 2] = tye
            ctx.xe_guard[e % 2] = ctx.tok("t")
            ctx.wait("g", tye, ctx.tok("ms0"))
            sc_guard = ctx.dma("g", "ms0", dlt.ap(), ysb[:, :],
                               out_offset=IndirectOffsetOnAxis(
                                   Lint[:, e * CAPT + tt:
                                        e * CAPT + tt + 1], 0),
                               compute_op=ALU.add)
        if e + 1 < NE:
            load_w2(e + 1, w2_last)
        at_free = w2_last
    A.pop()
    ctx.barrier()

    A.push()
    identf = A.f32(128)
    dl = A.f32(1024)
    xts = [A.f32(CH) for _ in range(FB)]
    t2 = ctx.op("v", lambda: nc.vector.memset(identf[:, :], 1.0))
    tidf = ctx.op("g", lambda: nc.gpsimd.affine_select(
        identf[:, :], identf[:, :], pattern=[[-1, 128]], base=0,
        channel_multiplier=1, compare_op=ALU.is_equal, fill=0.0), t2)
    pbg3 = {}
    stg = []
    dl2 = [dl, A.f32(1024)]
    vtok2 = {}
    dstr = ("ms1", "cc0")
    for seq in range(NSEQ):
        for ch in range(NCH):
            lts = [None, None]
            for fb in range(FB):
                lts[fb % 2] = ctx.dma(
                    "q", "xld" + str(fb % 2), xts[fb],
                    xs.ap()[seq][fb * 128:(fb + 1) * 128,
                                 ch * CH:(ch + 1) * CH],
                    deps=tuple(stg[-1:]) if fb < 2 else ())
            stg = []
            vtok = None
            for tt in range(4):
                p = tt % 2
                r0 = seq * T + ch * CH + tt * 128
                dld = ctx.dma("q", dstr[p], dl2[p][:, :],
                              dlt.ap()[r0:r0 + 128, :],
                              deps=(vtok2.get(p), ctx.tok(dstr[p])))
                for fb in range(FB):
                    ctx.wait("t", ctx.tok("t"), dld, tidf, pbg3.get(fb % 2))
                    itp = nc.tensor.transpose(
                        pb[fb % 2][:, 0:128],
                        dl2[p][:, fb * 128:(fb + 1) * 128], identf[:, :])
                    ttp = ctx.mark("t", itp)
                    vtok = ctx.op("v", lambda fb=fb, tt=tt, p=p:
                                  nc.vector.tensor_add(
                                      xts[fb][:, tt * 128:(tt + 1) * 128],
                                      xts[fb][:, tt * 128:(tt + 1) * 128],
                                      pb[fb % 2][:, 0:128]), ttp, lts[fb % 2])
                    pbg3[fb % 2] = vtok
                vtok2[p] = vtok
            for fb in range(FB):
                stg.append(ctx.dma(
                    "q", "xst", xs.ap()[seq][fb * 128:(fb + 1) * 128,
                                             ch * CH:(ch + 1) * CH],
                    xts[fb], deps=(vtok,)))
    ctx.stream_ready = (ctx.tok("xst"), ctx.tok("ms3"))
    A.pop()
    A.pop()


def build_program(stages=ALL_STAGES):
    nc = bass.Bass("TRN2", target_bir_lowering=False)
    global _LAST_CTX
    dt = {}

    def din(name, shape, dtype=F32):
        dt[name] = nc.dram_tensor(name, list(shape), dtype, kind="ExternalInput")

    din("x", [NSEQ, C, T])
    din("wfc", [C, HID]); din("wpr", [HID, C])
    din("wk", [C, C]); din("wv", [C, C]); din("wr", [C, C]); din("wo", [C, C])
    din("fwk", [C, HID]); din("fwr", [C, C]); din("fwv", [HID, C])
    for nm in ("ln1g", "ln1b", "ln2g", "ln2b", "tdec", "tfirst",
               "tmk", "tmv", "tmr", "ftmk", "ftmr"):
        din(nm, [128, FB])
    din("wqkv", [C, 3 * C]); din("bqkv", [128, 24]); din("bqkvr", [1, 3 * C]); din("boutr", [1, C])
    din("wout", [C, C]); din("bout", [128, FB])
    din("wg", [C, NE])
    din("w1", [NE, C, MH]); din("w3", [NE, C, MH]); din("w2", [NE, MH, C])

    y_out = nc.dram_tensor("y", [NSEQ, C, T], F32, kind="ExternalOutput")
    xs = nc.dram_tensor("xs", [NSEQ, C, T], F32, kind="Internal")
    xs2 = nc.dram_tensor("xs2", [NSEQ, C, T], F32, kind="Internal")
    aux = {
        "xnfm": nc.dram_tensor("xnfm", [NSEQ, C, T], BF16, kind="Internal"),
        "cc_in": nc.dram_tensor("cc_in", [NE, NTOK], F32, kind="Internal"),
        "tab": nc.dram_tensor("tab", [TROWS, 1040], BF16, kind="Internal"),
        "dlt": nc.dram_tensor("dlt", [TROWS, C], F32, kind="Internal"),
        "cc_out": [nc.dram_tensor(f"cc_out{i}", [NCORES, NE, NTOK], F32,
                                  kind="Internal", addr_space="Shared")
                   for i in range(2)],
        "w1b": nc.dram_tensor("w1b", [NE, C, MH], BF16, kind="Internal"),
        "w3b": nc.dram_tensor("w3b", [NE, C, MH], BF16, kind="Internal"),
        "w2b": nc.dram_tensor("w2b", [NE, MH, C], BF16, kind="Internal"),
    }

    ctx = Ctx(nc)
    with contextlib.ExitStack() as stack:
        ctx.init_sems(stack)
        big = stack.enter_context(nc.sbuf_tensor("big", [128, BIG_N], BF16))
        sm = stack.enter_context(nc.sbuf_tensor("sm", [128, 512], F32))
        ctx.pb = [stack.enter_context(nc.psum_tensor(f"pb{i}", [128, 512], F32))
                  for i in range(8)]
        ctx.arena = Arena(big)
        stack.enter_context(nc.Block())

        smal = {}
        off = 0
        names = ("ln1g", "ln1b", "ln2g", "ln2b", "tdec", "tfirst",
                 "tmk", "tmv", "tmr", "ftmk", "ftmr")
        for nm in names:
            smal[nm] = sm[:, off:off + FB]; off += FB
            ctx.dma("q", "ini", smal[nm], dt[nm].ap())
        smal["bqkv"] = sm[:, off:off + 24]; off += 24
        ctx.dma("q", "ini", smal["bqkv"], dt["bqkv"].ap())
        smal["bout"] = sm[:, off:off + FB]; off += FB
        ctx.dma("q", "ini", smal["bout"], dt["bout"].ap())
        tok_small = ctx.tok("ini")
        ctx.c_rsum = sm[:, off:off + 1]; off += 1
        ctx.c_ones = sm[:, off:off + 1]; off += 1
        ctx.c_brow = sm[0:1, off:off + 128]; off += 128
        ctx.c_eps = sm[:, off:off + 1]; off += 1
        ctx.c_o128 = sm[:, off:off + 1]; off += 1
        ctx.c_ones16 = sm.bitcast(F16)[:, 2 * off:2 * off + 1]; off += 1
        nc.vector.memset(ctx.c_rsum, 1.0 / 1024.0)
        nc.vector.memset(ctx.c_ones, 1.0)
        nc.vector.memset(ctx.c_eps, 1e-5)
        nc.vector.memset(ctx.c_o128, 1.0 / 128.0)
        nc.vector.memset(ctx.c_ones16, 1.0)
        tc = ctx.mark("v", nc.vector.memset(ctx.c_brow, 1.0))
        ctx.wait("t", tc)
        ctx.ln_prior = []
        ctx.lnsq_guard = None
        ctx.moe_g_guard = None
        ctx.moe_x_guard = None
        ctx.moe_pb1 = {}
        drv = {"wdec": sm[:, off:off + FB], "eu": sm[:, off + FB:off + 2 * FB]}
        off += 2 * FB
        t1 = ctx.op("s", lambda: nc.scalar.activation(
            drv["wdec"], smal["tdec"], AF.Exp), tok_small)
        t2 = ctx.op("s", lambda: nc.scalar.activation(
            drv["wdec"], drv["wdec"], AF.Exp, scale=-1.0), t1)
        ts = ctx.op("s", lambda: nc.scalar.activation(
            drv["eu"], smal["tfirst"], AF.Exp), tok_small)
        ctx.drv_tok = ts
        ctx.smal = smal
        ctx.drv = drv

        for s_ in range(NSEQ):
            tin = ctx.dma("g", "ms0", xs.ap()[s_], dt["x"].ap()[s_])
        ctx.stream_ready = (tin,)

        ctx.pcw_tok = {}
        if "moe" in stages:
            for e in range(NE):
                ctx.dma("g", "pcw", aux["w1b"].ap()[e], dt["w1"].ap()[e])
                ctx.dma("g", "pcw", aux["w3b"].ap()[e], dt["w3"].ap()[e])
                ctx.dma("g", "pcw", aux["w2b"].ap()[e], dt["w2"].ap()[e])
            for e in range(NE):
                ctx.pcw_tok[e] = ctx.tok("pcw")

        moe_i = 0
        for st in stages:
            if st == "f0":
                emit_ffn(ctx, dt, xs)
            elif st == "rwkv":
                emit_rwkv(ctx, dt, xs, xs2)
            elif st == "attn":
                emit_attn(ctx, dt, xs)
            elif st == "moe":
                emit_moe(ctx, dt, xs, {**aux, "cc": aux["cc_out"][moe_i]})
                moe_i += 1

        _LAST_CTX = ctx
        ctx.barrier()
        ctx.wait("g", ctx.tok("xst"))
        for s_ in range(NSEQ):
            tout = ctx.dma("g", "fin", y_out.ap()[s_], xs.ap()[s_])
        ctx.wait("g", tout)
    return nc


def _prep_inputs(inputs):
    f32 = np.float32
    g = lambda k: np.asarray(inputs[k], dtype=f32)

    def vec128(v):
        return np.ascontiguousarray(np.asarray(v, f32).reshape(-1)
                                    .reshape(FB, 128).T)

    com = {
        "wfc": np.ascontiguousarray(g("L0_ffn_W_fc").T),
        "wpr": np.ascontiguousarray(g("L0_ffn_W_proj").T),
        "wk": np.ascontiguousarray(g("L0_attn_Wk").T),
        "wv": np.ascontiguousarray(g("L0_attn_Wv").T),
        "wr": np.ascontiguousarray(g("L0_attn_Wr").T),
        "wo": np.ascontiguousarray(g("L0_attn_Wo").T),
        "fwk": np.ascontiguousarray(g("L0_attn_fWk").T),
        "fwr": np.ascontiguousarray(g("L0_attn_fWr").T),
        "fwv": np.ascontiguousarray(g("L0_attn_fWv").T),
        "ln1g": vec128(g("L0_attn_ln1_g")), "ln1b": vec128(g("L0_attn_ln1_b")),
        "ln2g": vec128(g("L0_attn_ln2_g")), "ln2b": vec128(g("L0_attn_ln2_b")),
        "tdec": vec128(g("L0_attn_time_decay")),
        "tfirst": vec128(g("L0_attn_time_first")),
        "tmk": vec128(g("L0_attn_tmk")), "tmv": vec128(g("L0_attn_tmv")),
        "tmr": vec128(g("L0_attn_tmr")),
        "ftmk": vec128(g("L0_attn_f_tmk")), "ftmr": vec128(g("L0_attn_f_tmr")),
        "wqkv": np.ascontiguousarray(g("L1_attn_Wqkv")),
        "bqkv": np.ascontiguousarray(g("L1_attn_bqkv").reshape(24, 128).T),
        "wout": np.ascontiguousarray(g("L1_attn_Wout")),
        "bqkvr": np.ascontiguousarray(g("L1_attn_bqkv").reshape(1, 3 * C)),
        "boutr": np.ascontiguousarray(g("L1_attn_bout").reshape(1, C)),
        "bout": vec128(g("L1_attn_bout")),
        "wg": np.ascontiguousarray(g("L1_ffn_Wg").T),
        "w1": np.ascontiguousarray(g("L1_ffn_W1").transpose(0, 2, 1)),
        "w3": np.ascontiguousarray(g("L1_ffn_W3").transpose(0, 2, 1)),
        "w2": np.ascontiguousarray(g("L1_ffn_W2").transpose(0, 2, 1)),
    }
    x = g("L0_attn_x")
    in_maps = []
    for c in range(NCORES):
        m = dict(com)
        m["x"] = np.ascontiguousarray(x[c * NSEQ:(c + 1) * NSEQ].transpose(0, 2, 1))
        in_maps.append(m)
    return in_maps


_LAST_CTX = None
_PROG_CACHE = {}


def kernel(**inputs):
    key = tuple(ALL_STAGES)
    if key not in _PROG_CACHE:
        _PROG_CACHE[key] = build_program(ALL_STAGES)
    nc = _PROG_CACHE[key]
    in_maps = _prep_inputs(inputs)
    res = run_bass_kernel_spmd(nc, in_maps, core_ids=list(range(NCORES)))
    outs = [r["y"].transpose(0, 2, 1) for r in res.results]
    return np.ascontiguousarray(np.concatenate(outs, axis=0)).astype(np.float32)


# version 15
# speedup vs baseline: 1.0428x; 1.0078x over previous
import contextlib
import numpy as np
import concourse.bass as bass
import concourse.mybir as mybir
from concourse.bass_core_rust import IndirectOffsetOnAxis
from concourse.bass_utils import run_bass_kernel_spmd

F32 = mybir.dt.float32
F16 = mybir.dt.float16
BF16 = mybir.dt.bfloat16
I32 = mybir.dt.int32
AF = mybir.ActivationFunctionType
ALU = mybir.AluOpType

NCORES = 8
B, T, C = 16, 2048, 1024
NSEQ = B // NCORES
FB = C // 128
HID = 4096
HB = HID // 128
NE = 8
MH = 2048
CH = 512
NCH = T // CH
NTOK = NSEQ * T

SNN_B = (0.24419428096993973, 0.4639691338428854, 0.6861859295255306,
         0.9059607823984763, 1.0839784132255623)

ALL_STAGES = ("f0", "rwkv", "f0", "moe", "attn", "moe")
BIG_N = 100 * 1024


class Ctx:
    def __init__(self, nc):
        self.nc = nc
        self.E = {"t": nc.tensor, "v": nc.vector, "s": nc.scalar,
                  "g": nc.gpsimd, "q": nc.sync}
        self.sem = {}
        self.cnt = {}
        self.waited = {}

    DMA_STREAMS = ("ini", "xld0", "xld1", "xst", "wt0", "wt1", "wA0", "wA1",
                   "wB0", "wB1", "wC0", "wC1", "ms0", "ms1", "ms2", "ms3",
                   "ms4", "cc0", "cc1", "pcw", "wD0", "wD1", "fin")

    def init_sems(self, stack):
        for name in ("t", "v", "s", "g", "q") + self.DMA_STREAMS:
            self.sem[name] = stack.enter_context(self.nc.semaphore("sem_" + name))
            self.cnt[name] = 0

    def tok(self, name):
        return (name, self.cnt[name])

    def mark(self, name, inst, n=1):
        self.cnt[name] += n
        inst.then_inc(self.sem[name], n)
        return (name, self.cnt[name])

    def wait(self, waiter, *tokens):
        for tk in tokens:
            if tk is None:
                continue
            sname, v = tk
            if v <= 0:
                continue
            key = (waiter, sname)
            if self.waited.get(key, 0) >= v:
                continue
            self.waited[key] = v
            self.E[waiter].wait_ge(self.sem[sname], v)

    def op(self, eng, emit, *deps):
        self.wait(eng, self.tok(eng), *deps)
        return self.mark(eng, emit())

    def dma(self, qname, stream, out, in_, deps=(), transpose=False,
            in_offset=None, out_offset=None, **kw):
        eng = self.E[qname]
        self.wait(qname, *deps)
        if transpose:
            inst = eng.dma_start_transpose(out, in_)
        elif in_offset is not None or out_offset is not None:
            inst = eng.indirect_dma_start(out=out, out_offset=out_offset,
                                          in_=in_, in_offset=in_offset, **kw)
        else:
            inst = eng.dma_start(out=out, in_=in_, **kw)
        return self.mark(stream, inst, 16)

    def barrier(self):
        toks = [self.tok(s) for s in self.DMA_STREAMS]
        for name in ("t", "v", "s", "g", "q"):
            self.wait(name, self.tok(name), *toks)
            toks.append(self.mark(name, self.E[name].nop()))
        for name in ("t", "v", "s", "g", "q"):
            self.wait(name, *toks)


class Arena:

    def __init__(self, big):
        self.big = big
        self.off = 0
        self._marks = []

    def push(self):
        self._marks.append(self.off)

    def pop(self):
        self.off = self._marks.pop()

    def bf(self, n):
        o = self._align(2)
        self.off = o + 2 * n
        assert self.off <= 2 * BIG_N, f"SBUF arena overflow: {self.off}"
        return self.big[:, o // 2: o // 2 + n]

    def f32(self, n):
        o = self._align(4)
        self.off = o + 4 * n
        assert self.off <= 2 * BIG_N, f"SBUF arena overflow: {self.off}"
        return self.big.bitcast(F32)[:, o // 4: o // 4 + n]

    def f16(self, n):
        o = self._align(4)
        self.off = o + 2 * n
        assert self.off <= 2 * BIG_N, f"SBUF arena overflow: {self.off}"
        return self.big.bitcast(F16)[:, o // 2: o // 2 + n]

    def i32(self, n):
        o = self._align(4)
        self.off = o + 4 * n
        assert self.off <= 2 * BIG_N, f"SBUF arena overflow: {self.off}"
        return self.big.bitcast(I32)[:, o // 4: o // 4 + n]

    def _align(self, a):
        return (self.off + a - 1) // a * a


def ln_chunk(ctx, seq, ch, xs, xts, xnts, gain=None, bias=None, src=None,
             xts_free=(), apply_deps=()):
    nc, pb = ctx.nc, ctx.pb
    t0 = ch * CH
    if src is None:
        lts = [None, None]
        for fb in range(FB):
            lts[fb % 2] = ctx.dma(
                "q", "xld" + str(fb % 2), xts[fb],
                xs.ap()[seq][fb * 128:(fb + 1) * 128, t0:t0 + CH],
                deps=xts_free if fb < 2 else ())
        ldtoks = [lts[fb % 2] for fb in range(FB)]
    else:
        ldtoks = [src] * FB
    sqt = [None, None]
    tsqs = []
    for fb in range(FB):
        sq = ctx.lnsq[:, (fb % 2) * CH:(fb % 2) * CH + CH]
        tsq = ctx.op("s", lambda sq=sq, fb=fb: nc.scalar.activation(
            sq, xts[fb], AF.Square), ldtoks[fb], sqt[fb % 2], ctx.lnsq_guard,
            *ctx.ln_prior)
        tsqs.append(tsq)
        ctx.wait("t", ctx.tok("t"), ldtoks[fb], tsq,
                 *(ctx.ln_prior if fb == 0 else ()))
        nc.tensor.matmul(pb[4][0:1, 0:CH], ctx.c_rsum, xts[fb],
                         start=(fb == 0), stop=(fb == FB - 1))
        i2 = nc.tensor.matmul(pb[5][0:1, 0:CH], ctx.c_rsum, sq,
                              start=(fb == 0), stop=(fb == FB - 1))
        sqt[fb % 2] = ctx.mark("t", i2)
    tstat = ctx.tok("t")
    mu2 = ctx.lnrows[0:1, 0:CH]
    rstd = ctx.lnrows[0:1, CH:2 * CH]
    mrs = ctx.lnrows[0:1, 2 * CH:3 * CH]
    ts0 = ctx.op("s", lambda: nc.scalar.activation(mu2, pb[4][0:1, 0:CH],
                                                   AF.Square), tstat, *ctx.ln_prior)
    tv = ctx.op("v", lambda: nc.vector.tensor_sub(mu2, pb[5][0:1, 0:CH], mu2),
                ts0, tstat)
    ts = ctx.op("s", lambda: nc.scalar.activation(rstd, mu2, AF.Ln,
                                                  bias=ctx.c_eps[0:1, :]), tv)
    tr = ctx.op("s", lambda: nc.scalar.activation(rstd, rstd, AF.Exp,
                                                  scale=-0.5), ts)
    tv2 = ctx.op("v", lambda: nc.vector.tensor_mul(mrs, pb[4][0:1, 0:CH], rstd), tr)
    ctx.wait("t", ctx.tok("t"), tv2, ts0)
    ib = None
    for j in range(0, CH, 512):
        w = min(512, CH - j)
        nc.tensor.matmul(pb[4][:, j:j + w], ctx.c_brow,
                         rstd[0:1, j:j + w], start=True, stop=True)
        ib = nc.tensor.matmul(pb[5][:, j:j + w], ctx.c_brow,
                              mrs[0:1, j:j + w], start=True, stop=True)
    tb = ctx.mark("t", ib)
    outs = []
    t1tok = [tsqs[-2], tsqs[-1]]
    for fb in range(FB):
        t1 = ctx.lnsq[:, 0:CH] if fb % 2 == 0 else ctx.lnsq[:, CH:2 * CH]
        tm1 = ctx.op("v", lambda t1=t1, fb=fb: nc.vector.tensor_mul(
            t1, xts[fb], pb[4][:, 0:CH]), tb, t1tok[fb % 2],
            *(apply_deps if fb == 0 else ()))
        if gain is None:
            iv = ctx.op("v", lambda t1=t1, fb=fb: nc.vector.tensor_sub(
                xnts[fb], t1, pb[5][:, 0:CH]), tm1)
            outs.append(iv)
            t1tok[fb % 2] = iv
        else:
            tv3 = ctx.op("v", lambda t1=t1: nc.vector.tensor_sub(
                t1, t1, pb[5][:, 0:CH]), tm1)
            si = ctx.op("v", lambda t1=t1, fb=fb: nc.vector.tensor_scalar(
                xnts[fb], t1, gain[:, fb:fb + 1], bias[:, fb:fb + 1],
                ALU.mult, ALU.add), tv3)
            outs.append(si)
            t1tok[fb % 2] = si
    ctx.ln_prior = [ctx.tok("v"), ctx.tok("s")]
    return outs, ldtoks


def emit_ffn(ctx, dt, xs):
    nc, pb = ctx.nc, ctx.pb
    ctx.barrier()
    A = ctx.arena
    A.push()
    wpr = A.bf(HB * C)
    wfcq = [A.bf(FB * 1024) for _ in range(2)]
    h = A.bf(HB * CH)
    xn = [A.bf(CH) for _ in range(FB)]
    rt = [A.bf(CH) for _ in range(2)]
    xts = [A.f32(CH) for _ in range(FB)]
    yt = A.f32(CH)
    acc = A.f32(CH)
    ctx.lnsq = A.f32(2 * CH)
    ctx.lnrows = A.f32(3 * CH)

    twpr = None
    for hb in range(HB):
        twpr = ctx.dma("g", "wt0", wpr[:, hb * C:(hb + 1) * C],
                       dt["wpr"].ap()[hb * 128:(hb + 1) * 128, :])
    ctx.wait("q", *ctx.stream_ready)

    nq_total = NSEQ * NCH * 4

    def load_wfcq(qg, dep):
        buf = wfcq[qg % 2]
        q = qg % 4
        tq = None
        for fb in range(FB):
            tq = ctx.dma("g", "wA" + str(qg % 2),
                         buf[:, fb * 1024:(fb + 1) * 1024],
                         dt["wfc"].ap()[fb * 128:(fb + 1) * 128,
                                        q * 1024:(q + 1) * 1024],
                         deps=(dep,) if fb == 0 else ())
        return tq

    wfct = {0: load_wfcq(0, None), 1: load_wfcq(1, None)}
    qcons = {}
    evt = {}
    ebt = {}
    ybt = {}
    rtt = [None, None]
    resid_tok = []
    for seq in range(NSEQ):
        for ch in range(NCH):
            cidx = seq * NCH + ch
            xn_toks, _ = ln_chunk(ctx, seq, ch, xs, xts, xn,
                                  xts_free=resid_tok[-1:])
            resid_tok = []
            for q in range(4):
                qg = cidx * 4 + q
                ctx.wait("t", wfct[qg], *(xn_toks if q == 0 else ()))
                for k8 in range(8):
                    hb = q * 8 + k8
                    bank = pb[hb % 4]
                    ctx.wait("t", ctx.tok("t"), ebt.get(hb % 4))
                    for fb in range(FB):
                        im = nc.tensor.matmul(
                            bank[:, 0:CH],
                            wfcq[qg % 2][:, fb * 1024 + k8 * 128:
                                         fb * 1024 + k8 * 128 + 128],
                            xn[fb], start=(fb == 0), stop=(fb == FB - 1))
                    tm = ctx.mark("t", im)
                    tmx = ctx.op("v", lambda p=hb % 2, bank=bank:
                                 nc.vector.tensor_scalar_max(
                                     rt[p], bank[:, 0:CH], 0.0),
                                 tm, rtt[hb % 2])
                    tie = ctx.op("s", lambda p=hb % 2, hb=hb:
                                 nc.scalar.activation(
                                     h[:, hb * CH:(hb + 1) * CH], rt[p],
                                     AF.Square), tmx, evt.get(hb))
                    evt[hb] = ebt[hb % 4] = rtt[hb % 2] = tie
                qcons[qg] = ctx.tok("t")
                if qg + 2 < nq_total:
                    wfct[qg + 2] = load_wfcq(qg + 2, qcons[qg])
            for cb in range(FB):
                ybank = pb[6 + cb % 2]
                ctx.wait("t", ctx.tok("t"), twpr, ybt.get(cb % 2))
                for hb in range(HB):
                    ctx.wait("t", evt[hb])
                    im = nc.tensor.matmul(
                        ybank[:, 0:CH],
                        wpr[:, hb * C + cb * 128: hb * C + cb * 128 + 128],
                        h[:, hb * CH:(hb + 1) * CH],
                        start=(hb == 0), stop=(hb == HB - 1))
                tm = ctx.mark("t", im)
                tv = ctx.op("v", lambda ybank=ybank: nc.vector.tensor_copy(
                    yt, ybank[:, 0:CH]), tm)
                ybt[cb % 2] = tv
                ta = ctx.op("v", lambda: nc.vector.tensor_scalar(
                    acc, yt, SNN_B[0], None, ALU.is_gt), tv)
                for i in (1, 2, 3, 4):
                    ta = ctx.op("v", lambda i=i: nc.vector.scalar_tensor_tensor(
                        acc, yt, SNN_B[i], acc, ALU.is_gt, ALU.add), ta)
                tvx = ctx.op("v", lambda cb=cb: nc.vector.scalar_tensor_tensor(
                    xts[cb], acc, 0.1, xts[cb], ALU.mult, ALU.add), ta)
                resid_tok.append(ctx.dma(
                    "q", "xst", xs.ap()[seq][cb * 128:(cb + 1) * 128,
                                             ch * CH:(ch + 1) * CH], xts[cb],
                    deps=(tvx,)))
    ctx.stream_ready = (ctx.tok("xst"), ctx.tok("ms3"))
    A.pop()


def emit_rwkv(ctx, dt, xs, xs2):
    nc, pb = ctx.nc, ctx.pb
    ctx.barrier()
    A = ctx.arena
    sm = ctx.smal
    drv = ctx.drv

    A.push()
    wk = A.bf(FB * C); wv = A.bf(FB * C); wr = A.bf(FB * C); wo = A.bf(FB * C)
    xts = [A.f32(CH) for _ in range(FB)]
    xp = xts
    xn1 = [A.bf(CH + 1) for _ in range(FB)]
    xk = [A.bf(CH) for _ in range(FB)]
    xv = [A.bf(CH) for _ in range(FB)]
    xr = [A.bf(CH) for _ in range(FB)]
    kt = [A.f32(CH) for _ in range(FB)]
    vt = [A.f32(CH) for _ in range(FB)]
    sr = [A.bf(CH) for _ in range(FB)]
    rw = [A.bf(CH) for _ in range(FB)]
    At = A.f32(CH + 1); Bt = A.f32(CH + 1)
    ab_st = A.f32(2 * FB)
    ctx.lnsq = A.f32(2 * CH)
    ctx.lnrows = A.f32(3 * CH)

    tw = None
    for w_ap, nm in ((wk, "wk"), (wv, "wv"), (wr, "wr"), (wo, "wo")):
        for fb in range(FB):
            tw = ctx.dma("g", "wt0", w_ap[:, fb * C:(fb + 1) * C],
                         dt[nm].ap()[fb * 128:(fb + 1) * 128, :])
    ctx.wait("q", *ctx.stream_ready)

    ebt = {}
    obt = {}
    scan_dep = []
    r1_resid = []
    for seq in range(NSEQ):
        tz = ctx.op("v", lambda: nc.vector.memset(ab_st[:, 0:2 * FB], 0.0),
                    *scan_dep)
        zcols = []
        for fb in range(FB):
            zcols.append(ctx.op("v", lambda fb=fb: nc.vector.memset(
                xn1[fb][:, 0:1], 0.0)))
        for ch in range(NCH):
            xp_toks, _ = ln_chunk(ctx, seq, ch, xs, xts, xp,
                                  xts_free=r1_resid[-1:])
            if ch > 0:
                for fb in range(FB):
                    ctx.op("v", lambda fb=fb: nc.vector.tensor_copy(
                        xn1[fb][:, 0:1], xn1[fb][:, CH:CH + 1]))
            xn_views = [xn1[fb][:, 1:1 + CH] for fb in range(FB)]
            t_src = xp_toks[-1]
            xn_toks, _ = ln_chunk(ctx, seq, ch, xs, xp, xn_views,
                                  gain=sm["ln1g"], bias=sm["ln1b"], src=t_src)
            mix_toks = []
            for fb in range(FB):
                td = ctx.op("v", lambda fb=fb: nc.vector.tensor_sub(
                    rw[fb], xn1[fb][:, 1:1 + CH], xn1[fb][:, 0:CH]), xn_toks[fb])
                ctx.op("v", lambda fb=fb: nc.vector.scalar_tensor_tensor(
                    xk[fb], rw[fb], sm["tmk"][:, fb:fb + 1], xn1[fb][:, 0:CH],
                    ALU.mult, ALU.add), td)
                ctx.op("v", lambda fb=fb: nc.vector.scalar_tensor_tensor(
                    xv[fb], rw[fb], sm["tmv"][:, fb:fb + 1], xn1[fb][:, 0:CH],
                    ALU.mult, ALU.add))
                tm3 = ctx.op("v", lambda fb=fb: nc.vector.scalar_tensor_tensor(
                    xr[fb], rw[fb], sm["tmr"][:, fb:fb + 1], xn1[fb][:, 0:CH],
                    ALU.mult, ALU.add))
                mix_toks.append(tm3)
            proj_ev = []
            for w_ap, xin, which in ((wk, xk, "k"), (wv, xv, "v"), (wr, xr, "r")):
                for ob in range(FB):
                    bank = pb[(ob % 4)]
                    ctx.wait("t", ctx.tok("t"), tw, ebt.get(ob % 4),
                             mix_toks[-1])
                    for fb in range(FB):
                        im = nc.tensor.matmul(
                            bank[:, 0:CH],
                            w_ap[:, fb * C + ob * 128: fb * C + ob * 128 + 128],
                            xin[fb], start=(fb == 0), stop=(fb == FB - 1))
                    tm = ctx.mark("t", im)
                    if which == "k":
                        ev = ctx.op("s", lambda ob=ob, bank=bank:
                                    nc.scalar.copy(kt[ob], bank[:, 0:CH]),
                                    tm)
                    elif which == "v":
                        ev = ctx.op("v", lambda ob=ob, bank=bank:
                                    nc.vector.tensor_copy(vt[ob], bank[:, 0:CH]),
                                    tm)
                    else:
                        ev = ctx.op("s", lambda ob=ob, bank=bank:
                                    nc.scalar.activation(sr[ob], bank[:, 0:CH],
                                                         AF.Sigmoid), tm)
                    ebt[ob % 4] = ev
                    proj_ev.append(ev)
            Et = ctx.lnsq[:, 0:CH]
            EVt = ctx.lnsq[:, CH:2 * CH]
            rw_toks = []
            for fb in range(FB):
                te = ctx.op("s", lambda fb=fb: nc.scalar.activation(
                    Et, kt[fb], AF.Exp), proj_ev[fb], ctx.ln_prior[0],
                    *( [rw_toks[-1]] if fb else []))
                tev = ctx.op("v", lambda fb=fb: nc.vector.tensor_mul(
                    EVt, Et, vt[fb]), te, proj_ev[FB + fb])
                ca = ctx.op("v", lambda fb=fb: nc.vector.tensor_copy(
                    At[:, 0:1], ab_st[:, fb:fb + 1]), tz)
                ta = ctx.op("v", lambda fb=fb: nc.vector.tensor_tensor_scan(
                    At[:, 1:1 + CH], drv["wdec"][:, fb:fb + 1].to_broadcast([128, CH]),
                    EVt, ab_st[:, fb:fb + 1], ALU.mult, ALU.add), tev, ca,
                    ctx.drv_tok)
                cb_ = ctx.op("v", lambda fb=fb: nc.vector.tensor_copy(
                    Bt[:, 0:1], ab_st[:, FB + fb:FB + fb + 1]))
                tb_ = ctx.op("v", lambda fb=fb: nc.vector.tensor_tensor_scan(
                    Bt[:, 1:1 + CH], drv["wdec"][:, fb:fb + 1].to_broadcast([128, CH]),
                    Et, ab_st[:, FB + fb:FB + fb + 1], ALU.mult, ALU.add), cb_)
                ctx.op("v", lambda fb=fb: nc.vector.tensor_copy(
                    ab_st[:, fb:fb + 1], At[:, CH:CH + 1]), ta)
                ctx.op("v", lambda fb=fb: nc.vector.tensor_copy(
                    ab_st[:, FB + fb:FB + fb + 1], Bt[:, CH:CH + 1]), tb_)
                tn = ctx.op("v", lambda fb=fb: nc.vector.scalar_tensor_tensor(
                    EVt, EVt, drv["eu"][:, fb:fb + 1], At[:, 0:CH],
                    ALU.mult, ALU.add), ta)
                tdn = ctx.op("v", lambda fb=fb: nc.vector.scalar_tensor_tensor(
                    Et, Et, drv["eu"][:, fb:fb + 1], Bt[:, 0:CH],
                    ALU.mult, ALU.add), tb_)
                trl = ctx.op("s", lambda: nc.scalar.activation(
                    Et, Et, AF.Ln), tdn)
                tr_ = ctx.op("s", lambda: nc.scalar.activation(
                    Et, Et, AF.Exp, scale=-1.0), trl)
                ty = ctx.op("v", lambda fb=fb: nc.vector.tensor_mul(
                    kt[fb], EVt, Et), tr_, tn)
                trw = ctx.op("v", lambda fb=fb: nc.vector.tensor_mul(
                    rw[fb], kt[fb], sr[fb]), ty, proj_ev[2 * FB + fb])
                rw_toks.append(trw)
            ctx.lnsq_guard = ctx.tok("v")
            for ob in range(FB):
                bank = pb[6 + ob % 2]
                ctx.wait("t", ctx.tok("t"), obt.get(ob % 2), rw_toks[-1])
                for fb in range(FB):
                    im = nc.tensor.matmul(
                        bank[:, 0:CH],
                        wo[:, fb * C + ob * 128: fb * C + ob * 128 + 128],
                        rw[fb], start=(fb == 0), stop=(fb == FB - 1))
                tm = ctx.mark("t", im)
                tv = ctx.op("v", lambda ob=ob, bank=bank: nc.vector.tensor_add(
                    xp[ob], xp[ob], bank[:, 0:CH]), tm)
                obt[ob % 2] = tv
                r1_resid.append(ctx.dma(
                    "q", "xst",
                    xs2.ap()[seq][ob * 128:(ob + 1) * 128,
                                  ch * CH:(ch + 1) * CH], xp[ob],
                    deps=(tv,)))
            scan_dep = [ctx.tok("xst")]
    A.pop()
    ctx.barrier()
    r1_done = (ctx.tok("xst"),)

    A.push()
    fwv = A.bf(16 * C * 2)
    fwkq = [A.bf(FB * 1024) for _ in range(2)]
    fwr = A.bf(FB * C)
    kk = A.bf(16 * CH * 2)
    xn2 = [A.bf(CH + 1) for _ in range(FB)]
    xk2 = [A.bf(CH) for _ in range(FB)]
    xr2 = [kk[:, fb * CH:(fb + 1) * CH] for fb in range(FB)]
    sr2 = [A.bf(CH) for _ in range(FB)]
    rt = [A.bf(CH) for _ in range(2)]
    xts2 = [A.f32(CH) for _ in range(FB)]
    ctx.lnsq = A.f32(2 * CH)
    ctx.lnrows = A.f32(3 * CH)
    tmp = ctx.lnsq[:, 0:CH]
    xo = ctx.lnsq[:, CH:2 * CH]

    tfv = None
    for hb in range(32):
        tfv = ctx.dma("g", "wt1", fwv[:, hb * C:(hb + 1) * C],
                      dt["fwv"].ap()[hb * 128:(hb + 1) * 128, :])
    for fb in range(FB):
        tfr = ctx.dma("g", "wt1", fwr[:, fb * C:(fb + 1) * C],
                      dt["fwr"].ap()[fb * 128:(fb + 1) * 128, :])
    tfr = ctx.tok("wt1")
    ctx.wait("q", *r1_done)

    nq_total = NSEQ * NCH * 4

    def load_fwkq(qg, dep):
        buf = fwkq[qg % 2]
        q = qg % 4
        tq = None
        for fb in range(FB):
            tq = ctx.dma("g", "wA" + str(qg % 2),
                         buf[:, fb * 1024:(fb + 1) * 1024],
                         dt["fwk"].ap()[fb * 128:(fb + 1) * 128,
                                        q * 1024:(q + 1) * 1024],
                         deps=(dep,) if fb == 0 else ())
        return tq

    fwkt = {0: load_fwkq(0, None), 1: load_fwkq(1, None)}
    ctx.xo_guard = None
    qcons = {}
    evt = {}
    ebt = {}
    ybt = {}
    rtt = [None, None]
    resid_tok = []
    for seq in range(NSEQ):
        zc = []
        for fb in range(FB):
            zc.append(ctx.op("v", lambda fb=fb: nc.vector.memset(
                xn2[fb][:, 0:1], 0.0), *resid_tok[-1:]))
        for ch in range(NCH):
            cidx = seq * NCH + ch
            xn_toks, _ = ln_chunk(ctx, seq, ch, xs2, xts2,
                                  [xn2[fb][:, 1:1 + CH] for fb in range(FB)],
                                  gain=sm["ln2g"], bias=sm["ln2b"],
                                  xts_free=resid_tok[-1:])
            mix_toks = []
            for fb in range(FB):
                td = ctx.op("v", lambda fb=fb: nc.vector.tensor_sub(
                    rt[0], xn2[fb][:, 1:1 + CH], xn2[fb][:, 0:CH]), xn_toks[fb],
                    rtt[0])
                ctx.op("v", lambda fb=fb: nc.vector.scalar_tensor_tensor(
                    xk2[fb], rt[0], sm["ftmk"][:, fb:fb + 1], xn2[fb][:, 0:CH],
                    ALU.mult, ALU.add), td)
                tm2 = ctx.op("v", lambda fb=fb: nc.vector.scalar_tensor_tensor(
                    xr2[fb], rt[0], sm["ftmr"][:, fb:fb + 1], xn2[fb][:, 0:CH],
                    ALU.mult, ALU.add))
                rtt[0] = tm2
                mix_toks.append(tm2)
            for fb in range(FB):
                ctx.op("v", lambda fb=fb: nc.vector.tensor_copy(
                    xn2[fb][:, 0:1], xn2[fb][:, CH:CH + 1]), mix_toks[-1])
            sh_tok = ctx.tok("v")
            sr_toks = []
            for ob in range(FB):
                bank = pb[6 + ob % 2]
                ctx.wait("t", ctx.tok("t"), tfr, ybt.get(ob % 2), mix_toks[-1])
                for fb in range(FB):
                    im = nc.tensor.matmul(
                        bank[:, 0:CH],
                        fwr[:, fb * C + ob * 128: fb * C + ob * 128 + 128],
                        xr2[fb], start=(fb == 0), stop=(fb == FB - 1))
                tm = ctx.mark("t", im)
                ts_ = ctx.op("s", lambda ob=ob, bank=bank: nc.scalar.activation(
                    sr2[ob], bank[:, 0:CH], AF.Sigmoid), tm)
                ybt[ob % 2] = ts_
                sr_toks.append(ts_)
            for q in range(4):
                qg = cidx * 4 + q
                ctx.wait("t", fwkt[qg])
                for k8 in range(8):
                    hb = q * 8 + k8
                    bank = pb[hb % 4]
                    ctx.wait("t", ctx.tok("t"), ebt.get(hb % 4))
                    for fb in range(FB):
                        im = nc.tensor.matmul(
                            bank[:, 0:CH],
                            fwkq[qg % 2][:, fb * 1024 + k8 * 128:
                                         fb * 1024 + k8 * 128 + 128],
                            xk2[fb], start=(fb == 0), stop=(fb == FB - 1))
                    tm = ctx.mark("t", im)
                    tmx = ctx.op("v", lambda p=hb % 2, bank=bank:
                                 nc.vector.tensor_scalar_max(
                                     rt[p], bank[:, 0:CH], 0.0),
                                 tm, rtt[hb % 2])
                    tie = ctx.op("s", lambda p=hb % 2, hb=hb:
                                 nc.scalar.activation(
                                     kk[:, hb * CH:(hb + 1) * CH], rt[p],
                                     AF.Square), tmx, evt.get(hb))
                    evt[hb] = ebt[hb % 4] = rtt[hb % 2] = tie
                qcons[qg] = ctx.tok("t")
                if qg + 2 < nq_total:
                    fwkt[qg + 2] = load_fwkq(qg + 2, qcons[qg])
            for cb in range(FB):
                ybank = pb[6 + cb % 2]
                tld = ctx.dma("q", "ms1", xo,
                              xs.ap()[seq][cb * 128:(cb + 1) * 128,
                                           ch * CH:(ch + 1) * CH],
                              deps=(ctx.xo_guard, *ctx.ln_prior))
                ctx.wait("t", ctx.tok("t"), tfv, ybt.get(cb % 2))
                for hb in range(32):
                    ctx.wait("t", evt[hb])
                    im = nc.tensor.matmul(
                        ybank[:, 0:CH],
                        fwv[:, hb * C + cb * 128: hb * C + cb * 128 + 128],
                        kk[:, hb * CH:(hb + 1) * CH],
                        start=(hb == 0), stop=(hb == 31))
                tm = ctx.mark("t", im)
                tv1 = ctx.op("v", lambda cb=cb, ybank=ybank: nc.vector.tensor_mul(
                    tmp, ybank[:, 0:CH], sr2[cb]), tm, sr_toks[cb])
                ybt[cb % 2] = tv1
                tv2 = ctx.op("v", lambda cb=cb: nc.vector.tensor_add(
                    xts2[cb], xts2[cb], tmp), tv1)
                tv3 = ctx.op("v", lambda cb=cb: nc.vector.tensor_add(
                    xts2[cb], xts2[cb], xo), tv2, tld)
                ctx.xo_guard = tv3
                ctx.lnsq_guard = tv3
                resid_tok.append(ctx.dma(
                    "q", "xst", xs.ap()[seq][cb * 128:(cb + 1) * 128,
                                             ch * CH:(ch + 1) * CH], xts2[cb],
                    deps=(tv3,)))
    ctx.stream_ready = (ctx.tok("xst"), ctx.tok("ms3"))
    A.pop()


def emit_attn(ctx, dt, xs):
    nc, pb = ctx.nc, ctx.pb
    ctx.barrier()
    A = ctx.arena
    A.push()
    wqkv = A.bf(FB * 3 * C)
    qa = A.bf(FB * T)
    ka = A.bf(FB * T)
    va = A.bf(16 * C)
    mask = A.bf(384)
    ident = A.bf(128)
    onesr = A.bf(128)
    vbias = A.bf(C)
    boutr = A.bf(C)

    tz1 = ctx.op("v", lambda: nc.vector.memset(mask[:, :], 0.0))
    tz2 = ctx.op("v", lambda: nc.vector.memset(ident[:, :], 1.0))
    tz3 = ctx.op("v", lambda: nc.vector.memset(onesr[0:1, :], 1.0))
    ti = ctx.op("g", lambda: nc.gpsimd.affine_select(
        ident[:, :], ident[:, :], pattern=[[-1, 128]], base=0,
        channel_multiplier=1, compare_op=ALU.is_equal, fill=0.0), tz2)
    tm1 = ctx.op("g", lambda: nc.gpsimd.affine_select(
        mask[:, :], mask[:, :], pattern=[[1, 384]], base=-64,
        channel_multiplier=-1, compare_op=ALU.is_ge, fill=-240000.0), tz1)
    tm2 = ctx.op("g", lambda: nc.gpsimd.affine_select(
        mask[:, :], mask[:, :], pattern=[[-1, 384]], base=192,
        channel_multiplier=1, compare_op=ALU.is_ge, fill=-240000.0), tm1)
    ctx.dma("g", "ms2", vbias[0:1, :], dt["bqkvr"].ap()[0:1, 2 * C:3 * C])
    tb1 = tb2 = ctx.dma("g", "ms2", boutr[0:1, :], dt["boutr"].ap())
    ctx.dbg = dict(qa=qa, ka=ka, va=va)
    ctx.dbg2 = {}
    tw = None
    for fb in range(FB):
        tw = ctx.dma("g", "wt0", wqkv[:, fb * 3 * C:(fb + 1) * 3 * C],
                     dt["wqkv"].ap()[fb * 128:(fb + 1) * 128, :])
    ctx.wait("q", *ctx.stream_ready)

    for seq in range(NSEQ):
        A.push()
        xn = [A.bf(CH) for _ in range(FB)]
        xts = [A.f32(CH) for _ in range(FB)]
        ctx.lnsq = A.f32(2 * CH)
        ctx.lnrows = A.f32(3 * CH)
        ebt = {}
        for ch in range(NCH):
            xn_toks, _ = ln_chunk(ctx, seq, ch, xs, xts, xn,
                                  xts_free=tuple(ctx.ln_prior))
            t0 = ch * CH
            for which, colo, dst in (("q", 0, qa), ("k", C, ka)):
                for ob in range(FB):
                    bank = pb[ob % 4]
                    ctx.wait("t", ctx.tok("t"), tw, ebt.get(ob % 4), xn_toks[-1])
                    for fb in range(FB):
                        im = nc.tensor.matmul(
                            bank[:, 0:CH],
                            wqkv[:, fb * 3 * C + colo + ob * 128:
                                 fb * 3 * C + colo + ob * 128 + 128],
                            xn[fb], start=(fb == 0), stop=(fb == FB - 1))
                    tm = ctx.mark("t", im)
                    if which == "q":
                        ev = ctx.op("v", lambda ob=ob, bank=bank, t0=t0:
                                    nc.vector.tensor_scalar(
                                        qa[:, ob * T + t0: ob * T + t0 + CH],
                                        bank[:, 0:CH],
                                        ctx.smal["bqkv"][:, ob:ob + 1], None,
                                        ALU.add), tm)
                    else:
                        ev = ctx.op("v", lambda ob=ob, bank=bank, t0=t0:
                                    nc.vector.tensor_scalar(
                                        ka[:, ob * T + t0: ob * T + t0 + CH],
                                        bank[:, 0:CH],
                                        ctx.smal["bqkv"][:, 8 + ob:9 + ob], None,
                                        ALU.add), tm)
                    ebt[ob % 4] = ev
            for st_ in range(4):
                tt = t0 + st_ * 128
                for nb in range(2):
                    bank = pb[6 + nb]
                    ctx.wait("t", ctx.tok("t"), ebt.get(("v", nb)), tb1, tz3)
                    for fb in range(FB):
                        nc.tensor.matmul(
                            bank[:, 0:CH],
                            xn[fb][:, st_ * 128:(st_ + 1) * 128],
                            wqkv[:, fb * 3 * C + 2 * C + nb * CH:
                                 fb * 3 * C + 2 * C + (nb + 1) * CH],
                            start=(fb == 0), stop=False)
                    im = nc.tensor.matmul(
                        bank[:, 0:CH], onesr[0:1, 0:128],
                        vbias[0:1, nb * CH:(nb + 1) * CH],
                        start=False, stop=True)
                    tm = ctx.mark("t", im)
                    ev = ctx.op("v", lambda tt=tt, nb=nb, bank=bank:
                                nc.vector.tensor_copy(
                                    va[:, (tt // 128) * C + nb * CH:
                                       (tt // 128) * C + (nb + 1) * CH],
                                    bank[:, 0:CH]), tm)
                    ebt[("v", nb)] = ev
        A.pop()
        ctx.barrier()
        A.push()
        wout = A.bf(FB * C)
        asb = A.bf(384)
        asb_b = A.bf(384)
        atb = A.bf(384)
        atb_b = A.bf(384)
        dz = A.bf(128)
        dz_b = A.bf(128)
        scb2 = [A.f32(384), A.f32(384)]
        avb2 = [A.bf(FB * 128) for _ in range(2)]
        zrow = A.f32(4)
        xrt2 = [A.f32(128) for _ in range(2)]
        two = None
        for fb in range(FB):
            two = ctx.dma("g", "wt1", wout[:, fb * C:(fb + 1) * C],
                          dt["wout"].ap()[fb * 128:(fb + 1) * 128, :],
                          deps=(ctx.tok("wt1"),) if fb == 0 else ())
        nih = T // 128
        ebt2 = {}
        ctx.xo_guards = {}
        asb2 = [asb, asb_b]
        atb2 = [atb, atb_b]
        dz2 = [dz, dz_b]
        def mk_outproj(ih, avt):
            def emit():
                i0 = ih * 128
                avb = avb2[ih % 2]
                for ob in range(FB):
                    bank = pb[6 + ob % 2]
                    xrt = xrt2[ob % 2]
                    tld = ctx.dma("q", "ms1", xrt,
                                  xs.ap()[seq][ob * 128:(ob + 1) * 128,
                                               i0:i0 + 128],
                                  deps=(ctx.xo_guards.get(ob % 2),))
                    ctx.wait("t", ctx.tok("t"), ebt2.get(("o", ob % 2)), avt,
                             two)
                    for fb in range(FB):
                        im = nc.tensor.matmul(
                            bank[:, 0:128],
                            wout[:, fb * C + ob * 128: fb * C + ob * 128 + 128],
                            avb[:, fb * 128:(fb + 1) * 128],
                            start=(fb == 0), stop=(fb == FB - 1))
                    tm = ctx.mark("t", im)
                    ta = ctx.op("v", lambda bank=bank, xrt=xrt, ob=ob:
                                nc.vector.scalar_tensor_tensor(
                                    xrt, bank[:, 0:128],
                                    ctx.smal["bout"][:, ob:ob + 1], xrt,
                                    ALU.add, ALU.add), tm, tld)
                    ebt2[("o", ob % 2)] = ta
                    tst = ctx.dma("q", "xst" if ob % 2 == 0 else "ms3",
                                  xs.ap()[seq][ob * 128:(ob + 1) * 128,
                                               i0:i0 + 128],
                                  xrt, deps=(ta,))
                    ctx.xo_guards[ob % 2] = tst
                ctx.avb_guard[ih % 2] = ctx.tok("t")
            return emit

        pending_op = None
        ctx.avb_guard = {}
        for ih in range(nih):
            i0 = ih * 128
            jb0 = max(0, ih - 1)
            jb1 = min(nih, ih + 2)
            kw = (jb1 - jb0) * 128
            moff = 128 * (1 - (ih - jb0))
            avt = None
            st = {}

            def stage_scores(h):
                p = h % 2
                fbq = h // 2
                po = (h % 2) * 64
                sb_ = pb[p]
                ctx.wait("t", ctx.tok("t"), ebt2.get(("s", p)), ti, tm2)
                for jj in range(jb0, jb1):
                    nc.tensor.matmul(
                        sb_[:, (jj - jb0) * 128:(jj - jb0 + 1) * 128],
                        qa[po:po + 64, fbq * T + i0: fbq * T + i0 + 128],
                        ka[po:po + 64, fbq * T + jj * 128: fbq * T + (jj + 1) * 128],
                        start=(jj == jb0), stop=False)
                im = nc.tensor.matmul(
                    sb_[:, 0:kw], ident[:, 0:128],
                    mask[:, moff:moff + kw], start=False, stop=True)
                return ctx.mark("t", im)

            def stage_soft(h, tsc):
                p = h % 2
                sb_ = pb[p]
                scb = scb2[p]
                zr = zrow[:, 2 * p:2 * p + 2]
                tev = ctx.op("s", lambda: nc.scalar.copy(
                    scb[:, 0:kw], sb_[:, 0:kw]), tsc, st.get(("sc", p)))
                ebt2[("s", p)] = tev
                tmx = ctx.op("v", lambda: nc.vector.tensor_reduce(
                    zr[:, 0:1], scb[:, 0:kw], mybir.AxisListType.X,
                    ALU.max), tev, st.get(("zr", p)))
                tsk = ctx.op("v", lambda: nc.vector.tensor_scalar(
                    zr[:, 0:1], zr[:, 0:1], -0.125, None, ALU.mult), tmx)
                tex = ctx.op("s", lambda: nc.scalar.activation(
                    asb2[p][:, 0:kw], scb[:, 0:kw], AF.Exp,
                    bias=zr[:, 0:1], scale=0.125,
                    accum_out=zr[:, 1:2]), tsk, st.get(("as", p)))
                st[("sc", p)] = tex
                tzi = ctx.op("v", lambda: nc.vector.reciprocal(
                    zr[:, 1:2], zr[:, 1:2]), tex)
                tdz = ctx.op("v", lambda: nc.vector.tensor_scalar(
                    dz2[p][:, :], ident[:, :], zr[:, 1:2], None, ALU.mult),
                    tzi, ti, st.get(("dz", p)))
                st[("zr", p)] = tdz
                return tex, tdz

            def stage_trans(h, tex, tdz):
                p = h % 2
                atp = pb[2 + p]
                ctx.wait("t", ctx.tok("t"), ebt2.get(("at", p)), tex, tdz)
                im = None
                for jj in range(jb0, jb1):
                    im = nc.tensor.matmul(
                        atp[:, (jj - jb0) * 128:(jj - jb0 + 1) * 128],
                        asb2[p][:, (jj - jb0) * 128:(jj - jb0 + 1) * 128],
                        dz2[p][:, :], start=(jj == jb0), stop=(jj == jb1 - 1))
                tat = ctx.mark("t", im)
                st[("as", p)] = st[("dz", p)] = tat
                tae = ctx.op("s", lambda: nc.scalar.activation(
                    atb2[p][:, 0:kw], atp[:, 0:kw], AF.Copy), tat,
                    st.get(("ab", p)))
                ebt2[("at", p)] = tae
                return tae

            def stage_av(h, tae):
                p = h % 2
                avp = pb[4 + p]
                ctx.wait("t", ctx.tok("t"), ebt2.get(("av", p)), tae)
                im = None
                for jj in range(jb0, jb1):
                    im = nc.tensor.matmul(
                        avp[0:64, 0:128],
                        va[:, jj * C + h * 64: jj * C + h * 64 + 64],
                        atb2[p][:, (jj - jb0) * 128:(jj - jb0 + 1) * 128],
                        start=(jj == jb0), stop=(jj == jb1 - 1))
                tv_ = ctx.mark("t", im)
                st[("ab", p)] = tv_
                tve = ctx.op("v", lambda h=h, avp=avp, avb=avb2[ih % 2]:
                             nc.vector.tensor_copy(
                                 avb[(h % 2) * 64:(h % 2) * 64 + 64,
                                     (h // 2) * 128:(h // 2) * 128 + 128],
                                 avp[0:64, 0:128]), tv_,
                             ctx.avb_guard.get(ih % 2) if h == 0 else None)
                ebt2[("av", p)] = tve
                return tve

            tscs = {0: stage_scores(0)}
            softs = {}
            taes = {}
            for h in range(16):
                if h + 1 < 16:
                    tscs[h + 1] = stage_scores(h + 1)
                if h == 2 and pending_op is not None:
                    pending_op()
                    pending_op = None
                if h >= 2:
                    taes[h - 2] = stage_trans(h - 2, *softs[h - 2])
                if h >= 3:
                    avt = stage_av(h - 3, taes[h - 3])
                softs[h] = stage_soft(h, tscs[h])
            taes[14] = stage_trans(14, *softs[14])
            avt = stage_av(13, taes[13])
            taes[15] = stage_trans(15, *softs[15])
            avt = stage_av(14, taes[14])
            avt = stage_av(15, taes[15])
            pending_op = mk_outproj(ih, avt)
        if pending_op is not None:
            pending_op()
        A.pop()
        ctx.barrier()
    ctx.stream_ready = (ctx.tok("xst"), ctx.tok("ms3"))
    A.pop()


CAPG = 4096
CAPL = 768
CAPT = CAPL // 128
TROWS = 4096 + NE * CAPT + 16


def emit_moe(ctx, dt, xs, aux):
    nc, pb = ctx.nc, ctx.pb
    ctx.barrier()
    A = ctx.arena
    xnfm, cc_in, cc = aux["xnfm"], aux["cc_in"], aux["cc"]
    tab, dlt = aux["tab"], aux["dlt"]
    NIT = 32
    A.push()
    ctx.moe_lint = A.i32(NE * CAPT)
    MHB = MH // 128
    w1 = A.bf(FB * MH)
    w3 = A.bf(FB * MH)
    w2 = A.bf(MHB * C)
    w1b, w3b, w2b = aux["w1b"], aux["w3b"], aux["w2b"]
    tw1_tok = {}; tw3_tok = {}; tw2_tok = {}

    def load_w1(e, *deps):
        for fb in range(FB):
            ctx.dma("q", "wD" + str(e % 2), w1[:, fb * MH:(fb + 1) * MH],
                    w1b.ap()[e][fb * 128:(fb + 1) * 128, :],
                    deps=deps if fb == 0 else ())
        tw1_tok[e] = ctx.tok("wD" + str(e % 2))

    def load_w3(e, *deps):
        for fb in range(FB):
            ctx.dma("q", "wD" + str(e % 2), w3[:, fb * MH:(fb + 1) * MH],
                    w3b.ap()[e][fb * 128:(fb + 1) * 128, :],
                    deps=deps + (ctx.tok("wD" + str(e % 2)),)
                    if fb == 0 else ())
        tw3_tok[e] = ctx.tok("wD" + str(e % 2))

    def load_w2(e, *deps):
        for hb in range(MHB):
            ctx.dma("q", "wD" + str(e % 2), w2[:, hb * C:(hb + 1) * C],
                    w2b.ap()[e][hb * 128:(hb + 1) * 128, :],
                    deps=deps + (ctx.tok("wD" + str(e % 2)),)
                    if hb == 0 else ())
        tw2_tok[e] = ctx.tok("wD" + str(e % 2))

    identb = A.bf(128)
    slt = A.bf(128)
    jrow = A.f16(CAPL)
    ones16 = A.f16(2)
    Sacc = A.f16(CAPL)
    lrowf = A.f32(CAPL)
    sc32 = A.f32(32); m16 = A.f16(32); c16 = A.f16(32); rtb = A.bf(2)
    jkl = A.f32(16)
    lo = A.f32(NE)
    ctx.identb = identb
    tzi = ctx.op("v", lambda: nc.vector.memset(identb[:, :], 1.0))
    tidb = ctx.op("g", lambda: nc.gpsimd.affine_select(
        identb[:, :], identb[:, :], pattern=[[-1, 128]], base=0,
        channel_multiplier=1, compare_op=ALU.is_equal, fill=0.0), tzi)
    t1s = ctx.op("v", lambda: nc.vector.memset(slt[:, :], 1.0))
    tslt = ctx.op("g", lambda: nc.gpsimd.affine_select(
        slt[:, :], slt[:, :], pattern=[[1, 128]], base=-1,
        channel_multiplier=-1, compare_op=ALU.is_ge, fill=0.0), t1s)
    tjr = ctx.op("g", lambda: nc.gpsimd.iota(
        jrow[:, 0:CAPL], pattern=[[1, CAPL]], base=0, channel_multiplier=0,
        allow_small_or_imprecise_dtypes=True))
    to16 = ctx.op("v", lambda: nc.vector.memset(ones16[:, 0:1], 1.0))

    lb = {"grd": None, "base": None, "lrow": None, "sacc": None, "rtbT": None}

    def lb_part1(e, *deps):
        tsm = ctx.op("v", lambda: nc.vector.memset(Sacc[:, 0:CAPL], 0.0),
                     lb["sacc"], *deps)
        tl = ctx.dma("g", "ms4", sc32[:, :],
                     cc_in.ap()[e].rearrange("(p u) -> p u", p=128),
                     deps=(tsm,))
        tm_ = ctx.op("v", lambda e=e: nc.vector.tensor_scalar(
            m16[:, :], sc32[:, :], lo[:, e:e + 1], None, ALU.is_gt), tl)
        tsc = ctx.op("v", lambda: nc.vector.tensor_tensor_scan(
            c16[:, :], ones16[:, 0:1].to_broadcast([128, 32]), m16[:, :], 0.0,
            ALU.mult, ALU.add), tm_, to16)
        lb["p1"] = ctx.op("v", lambda: nc.vector.tensor_copy(
            rtb[:, 0:1], c16[:, 31:32]), tsc, lb["rtbT"])

    def lb_base(bank):
        ctx.wait("t", ctx.tok("t"), lb["p1"], tslt, lb["grd"])
        ib = nc.tensor.matmul(pb[bank][:, 6:7], slt[:, :], rtb[:, 0:1],
                              start=True, stop=True)
        lb["rtbT"] = lb["base"] = ctx.mark("t", ib)

    def lb_staccs(bank):
        tcf = ctx.op("v", lambda: nc.vector.tensor_scalar(
            c16[:, :], c16[:, :], pb[bank][:, 6:7], None, ALU.add),
            lb["base"])
        lb["grd"] = tcf
        tsa = tcf
        for u in range(32):
            tsa = ctx.op("v", lambda u=u: nc.vector.scalar_tensor_tensor(
                Sacc[:, 0:CAPL], jrow[:, 0:CAPL], c16[:, u:u + 1],
                Sacc[:, 0:CAPL], ALU.is_ge, ALU.add), tsa, tjr)
        lb["stacc"] = tsa

    def lb_lrow_a(bank):
        ctx.wait("t", ctx.tok("t"), lb["stacc"], lb["grd"])
        il = nc.tensor.matmul(pb[bank][0:1, 0:512], ctx.c_ones16,
                              Sacc[:, 0:512], start=True, stop=True)
        tlr = ctx.mark("t", il)
        lb["grd"] = ctx.op("v", lambda: nc.vector.tensor_copy(
            lrowf[0:1, 0:512], pb[bank][0:1, 0:512]), tlr)

    def lb_lrow_b(bank):
        ctx.wait("t", ctx.tok("t"), lb["grd"])
        il = nc.tensor.matmul(pb[bank][0:1, 0:256], ctx.c_ones16,
                              Sacc[:, 512:CAPL], start=True, stop=True)
        tlr = ctx.mark("t", il)
        lb["sacc"] = lb["grd"] = ctx.op(
            "v", lambda: nc.vector.tensor_copy(
                lrowf[0:1, 512:CAPL], pb[bank][0:1, 0:256]), tlr)

    def lb_cols(e, bank):
        Lint = ctx.moe_lint
        ctx.wait("t", ctx.tok("t"), lb["grd"])
        ic = None
        for tt in range(CAPT):
            ic = nc.tensor.matmul(pb[bank][:, tt:tt + 1],
                                  lrowf[0:1, tt * 128:(tt + 1) * 128],
                                  ctx.c_ones[0:1, 0:1], start=True, stop=True)
        tic = ctx.mark("t", ic)
        tov = ctx.op("v", lambda bank=bank: nc.vector.tensor_scalar(
            jkl[:, 0:CAPT], pb[bank][:, 0:CAPT], 4095.5, None, ALU.is_gt),
            tic)
        tof = None
        for tt in range(CAPT):
            tof = ctx.op("v", lambda tt=tt, e=e, bank=bank:
                         nc.vector.scalar_tensor_tensor(
                             jkl[:, 8 + tt:9 + tt], jkl[:, tt:tt + 1],
                             float(e * CAPT + tt), pb[bank][:, tt:tt + 1],
                             ALU.mult, ALU.add), tov)
        lb["grd"] = tof
        lg = ctx.op("v", lambda e=e: nc.vector.tensor_copy(
            Lint[:, e * CAPT:(e + 1) * CAPT], jkl[:, 8:8 + CAPT]), tof)
        ctx.moe_lg_e[e] = lg

    A.push()
    wg = A.bf(FB * NE)
    xn = [A.bf(CH) for _ in range(FB)]
    xts = [A.f32(CH) for _ in range(FB)]
    scrow = A.f32(CH)
    xtk = [A.bf(1024) for _ in range(2)]
    zt = A.bf(1040)
    ztf = A.f32(1024)
    ctx.lnsq = A.f32(2 * CH)
    ctx.lnrows = A.f32(3 * CH)
    twg = None
    for fb in range(FB):
        twg = ctx.dma("g", "ms2", wg[:, fb * NE:(fb + 1) * NE],
                      dt["wg"].ap()[fb * 128:(fb + 1) * 128, :],
                      deps=(ctx.tok("ms2"),) if fb == 0 else ())
    tz0 = ctx.op("v", lambda: nc.vector.memset(zt[:, :], 0.0))
    tz1 = ctx.op("v", lambda: nc.vector.memset(ztf[:, :], 0.0))
    ctx.dma("q", "ms1", tab.ap()[4096:4096 + 64, :], zt[0:64, :], deps=(tz0,))
    for r in range(0, TROWS, 128):
        n = min(128, TROWS - r)
        ctx.dma("q", "ms1", dlt.ap()[r:r + n, :], ztf[0:n, :], deps=(tz1,))
    sct = None
    xtkg = [None, None]
    tbg = {}
    for seq in range(NSEQ):
        for ch in range(NCH):
            xn_toks, _ = ln_chunk(ctx, seq, ch, xs, xts, xn,
                                  xts_free=tuple(ctx.ln_prior),
                                  apply_deps=(ctx.tok("xst"), ctx.tok("wC0"),
                                              ctx.tok("wC1")))
            for fb in range(FB):
                ctx.dma("q", "xst",
                        xnfm.ap()[seq][fb * 128:(fb + 1) * 128,
                                       ch * CH:(ch + 1) * CH],
                        xn[fb], deps=(xn_toks[fb],))
            for sub in range(4):
                p = sub % 2
                r0 = seq * T + ch * CH + sub * 128
                bA, bB = (1, 2) if p == 0 else (3, 6)
                pA = pb[bA].bitcast(BF16)
                pB = pb[bB].bitcast(BF16)
                ctx.wait("t", ctx.tok("t"), xn_toks[-1], tidb,
                         tbg.get(p))
                for fb in range(FB):
                    bank = pA if fb < 4 else pB
                    itp = nc.tensor.transpose(
                        bank[:, (fb % 4) * 128:(fb % 4) * 128 + 128],
                        xn[fb][:, sub * 128:(sub + 1) * 128],
                        ctx.identb[:, :])
                ttp = ctx.mark("t", itp)
                te1 = ctx.op("s", lambda p=p, pA=pA: nc.scalar.copy(
                    xtk[p][:, 0:512], pA[:, 0:512]), ttp, xtkg[p])
                te2 = ctx.op("s", lambda p=p, pB=pB: nc.scalar.copy(
                    xtk[p][:, 512:1024], pB[:, 0:512]))
                tbg[p] = te2
                xtkg[p] = ctx.dma("q", "wC" + str(p),
                                  tab.ap()[r0:r0 + 128, 0:1024],
                                  xtk[p][:, :], deps=(te2,))
            ctx.wait("t", ctx.tok("t"), twg, sct, xn_toks[-1])
            for fb in range(FB):
                im = nc.tensor.matmul(pb[0][0:NE, 0:CH],
                                      wg[:, fb * NE:(fb + 1) * NE], xn[fb],
                                      start=(fb == 0), stop=(fb == FB - 1))
            tm = ctx.mark("t", im)
            sct = ctx.op("v", lambda: nc.vector.tensor_copy(
                scrow[0:NE, :], pb[0][0:NE, 0:CH]), tm, ctx.tok("ms1"))
            ctx.dma("q", "ms1", cc_in.ap()[0:NE, seq * T + ch * CH:
                                           seq * T + (ch + 1) * CH],
                    scrow[0:NE, :], deps=(sct,))
    A.pop()
    ctx.barrier()

    load_w1(0, ctx.pcw_tok.get(0))
    load_w3(0, ctx.pcw_tok.get(0))
    load_w2(0, ctx.pcw_tok.get(0))
    A.push()
    scg = A.f32(2048)
    junk = A.f32(2048)
    hic = A.f32(1); midc = A.f32(1); loc = A.f32(1)
    cntc = A.f32(1); mskc = A.i32(1); msk2c = A.i32(1)
    negc = A.f32(1); zc = A.f32(1); zhc = A.f32(1)
    thcol = A.f32(1); nthcol = A.f32(1); zhcol = A.f32(1)
    sloc = A.f32(CH); gloc = A.f32(CH); e1loc = A.f32(CH)
    gtmb = A.bf(4 * NE)
    identf = A.f32(128)
    onesf = A.f32(128)
    g8 = A.f32(128)
    gt16 = A.f32(NE)
    BDs = A.f32(128)
    Lint = ctx.moe_lint
    t2 = ctx.op("v", lambda: nc.vector.memset(identf[:, :], 1.0))
    tidf = ctx.op("g", lambda: nc.gpsimd.affine_select(
        identf[:, :], identf[:, :], pattern=[[-1, 128]], base=0,
        channel_multiplier=1, compare_op=ALU.is_equal, fill=0.0), t2)
    tonf = ctx.op("v", lambda: nc.vector.memset(onesf[:, :], 1.0))
    tg1 = ctx.op("v", lambda: nc.vector.memset(g8[0:NE, :], 1.0))
    tg2 = ctx.op("g", lambda: nc.gpsimd.affine_select(
        g8[0:NE, :], g8[0:NE, :], pattern=[[1, 128]], base=0,
        channel_multiplier=-16, compare_op=ALU.is_ge, fill=0.0), tg1)
    tg3 = ctx.op("g", lambda: nc.gpsimd.affine_select(
        g8[0:NE, :], g8[0:NE, :], pattern=[[-1, 128]], base=15,
        channel_multiplier=16, compare_op=ALU.is_ge, fill=0.0), tg2)
    tg4 = ctx.op("v", lambda: nc.vector.memset(gt16[:, :], 1.0 / 16.0))
    tg5 = ctx.op("g", lambda: nc.gpsimd.affine_select(
        gt16[:, :], gt16[:, :], pattern=[[-16, NE]], base=0,
        channel_multiplier=1, compare_op=ALU.is_ge, fill=0.0), tg4)
    tg6 = ctx.op("g", lambda: nc.gpsimd.affine_select(
        gt16[:, :], gt16[:, :], pattern=[[16, NE]], base=15,
        channel_multiplier=-1, compare_op=ALU.is_ge, fill=0.0), tg5)
    ctx.wait("t", ctx.tok("t"), tg3)
    ibd = nc.tensor.matmul(pb[1][:, 0:128], g8[0:NE, :], g8[0:NE, :],
                           start=True, stop=True)
    tbd0 = ctx.mark("t", ibd)
    tbd = ctx.op("v", lambda: nc.vector.tensor_copy(
        BDs[:, :], pb[1][:, 0:128]), tbd0)
    tcc = ctx.op("g", lambda: nc.gpsimd.collective_compute(
        "AllGather", ALU.bypass, replica_groups=[list(range(NCORES))],
        ins=[cc_in.ap()], outs=[cc.ap()]), ctx.tok("ms1"))
    tld = None
    for e in range(NE):
        for sl in range(NCORES):
            tld = ctx.dma("g", "ms0", scg[e * 16:(e + 1) * 16,
                                          sl * 256:(sl + 1) * 256],
                          cc.ap()[sl][e].rearrange("(p u) -> p u", p=16),
                          deps=(tcc,))
    ctx.op("v", lambda: nc.vector.memset(loc, -1000.0))
    t0v = ctx.op("v", lambda: nc.vector.memset(hic, 1000.0))
    pbg = {1: tbd}
    for it in range(NIT):
        ctx.op("v", lambda: nc.vector.tensor_add(midc, loc, hic), t0v)
        tmid = ctx.op("v", lambda: nc.vector.tensor_scalar(
            midc, midc, 0.5, None, ALU.mult))
        tc = ctx.op("v", lambda: nc.vector.tensor_scalar(
            junk, scg[:, :], midc, 0.0, ALU.is_gt, ALU.add,
            accum_out=cntc), tmid, tld)
        ctx.wait("t", ctx.tok("t"), tc, tbd, pbg.get(2))
        i3 = nc.tensor.matmul(pb[2][:, 0:1], BDs[:, :], cntc,
                              start=True, stop=True)
        t3 = ctx.mark("t", i3)
        tmk = ctx.op("v", lambda: nc.vector.tensor_scalar(
            mskc, pb[2][:, 0:1], float(CAPG), None, ALU.is_ge), t3)
        tmk2 = ctx.op("v", lambda: nc.vector.tensor_scalar(
            msk2c, pb[2][:, 0:1], float(CAPG), None, ALU.is_lt), t3)
        pbg[2] = tmk2
        ctx.op("v", lambda: nc.vector.select(loc, mskc, midc, loc,
                                             add_drain=True), tmk)
        t0v = ctx.op("v", lambda: nc.vector.select(hic, msk2c, midc, hic,
                                                   add_drain=True), tmk2)

    tng = ctx.op("v", lambda: nc.vector.tensor_scalar(
        negc, loc, -1.0, None, ALU.mult), t0v)
    te1 = ctx.op("s", lambda: nc.scalar.activation(
        junk, scg[:, :], AF.Exp, bias=negc), tng)
    tzp = ctx.op("v", lambda: nc.vector.scalar_tensor_tensor(
        junk, scg[:, :], loc, junk, ALU.is_gt, ALU.mult,
        accum_out=zc), te1)
    ctx.wait("t", ctx.tok("t"), tzp, pbg.get(1))
    iz = nc.tensor.matmul(pb[1][:, 0:1], BDs[:, :], zc, start=True, stop=True)
    tzm = ctx.mark("t", iz)
    tzv = ctx.op("v", lambda: nc.vector.tensor_copy(zhc, pb[1][:, 0:1]), tzm)
    tzl = ctx.op("s", lambda: nc.scalar.activation(zhc, zhc, AF.Ln), tzv)
    tze = ctx.op("s", lambda: nc.scalar.activation(zhc, zhc, AF.Exp,
                                                   scale=-1.0), tzl)
    tzh = ctx.op("v", lambda: nc.vector.tensor_scalar(
        zhc, zhc, 0.5, None, ALU.mult), tze)
    ctx.wait("t", ctx.tok("t"), tzh, tng, tg6, pbg.get(3))
    nc.tensor.matmul(pb[3][0:NE, 0:1], gt16[:, :], loc, start=True, stop=True)
    ic2 = nc.tensor.matmul(pb[3][0:NE, 1:2], gt16[:, :], zhc,
                           start=True, stop=True)
    tcm = ctx.mark("t", ic2)
    tthc = ctx.op("v", lambda: nc.vector.tensor_copy(
        thcol[0:NE, :], pb[3][0:NE, 0:1]), tcm)
    tnthc = ctx.op("v", lambda: nc.vector.tensor_scalar(
        nthcol[0:NE, :], thcol[0:NE, :], -1.0, None, ALU.mult), tthc)
    tzhc = ctx.op("v", lambda: nc.vector.tensor_copy(
        zhcol[0:NE, :], pb[3][0:NE, 1:2]))
    ttmp = ctx.op("v", lambda: nc.vector.tensor_scalar(
        junk[:, 0:NE], gt16[:, :], loc, None, ALU.mult), tzhc)
    ctx.wait("t", ctx.tok("t"), ttmp, tonf, pbg.get(2))
    ilo = nc.tensor.matmul(pb[2][:, 0:NE], onesf, junk[:, 0:NE],
                           start=True, stop=True)
    tlo = ctx.mark("t", ilo)
    tcols = ctx.op("v", lambda: nc.vector.tensor_copy(
        lo[:, :], pb[2][:, 0:NE]), tlo)
    pbg[3] = tcols
    tgl = None
    gldg = {}
    ctx.moe_ttg = None
    for seq in range(NSEQ):
        for ch in range(NCH):
            o = seq * T + ch * CH
            tl = ctx.dma("q", "ms1", sloc[0:NE, :], cc_in.ap()[0:NE, o:o + CH],
                         deps=(tgl,))
            te1 = ctx.op("s", lambda: nc.scalar.activation(
                e1loc[0:NE, :], sloc[0:NE, :], AF.Exp,
                bias=nthcol[0:NE, :]), tl, tcols, tgl)
            tg1 = ctx.op("v", lambda: nc.vector.scalar_tensor_tensor(
                gloc[0:NE, :], sloc[0:NE, :], thcol[0:NE, :], e1loc[0:NE, :],
                ALU.is_gt, ALU.mult), te1, ctx.moe_ttg)
            tg2 = ctx.op("v", lambda: nc.vector.tensor_scalar(
                gloc[0:NE, :], gloc[0:NE, :], zhcol[0:NE, :], None, ALU.mult),
                tg1)
            tgl = tg2
            for sub in range(4):
                p = sub % 2
                ctx.wait("t", ctx.tok("t"), tg2, pbg.get(4), tidf)
                it_ = nc.tensor.transpose(
                    pb[4][0:128, 0:NE],
                    gloc[0:NE, sub * 128:(sub + 1) * 128],
                    identf[0:NE, 0:NE])
                tt_ = ctx.mark("t", it_)
                ctx.moe_ttg = tt_
                ghl = gtmb[:, p * 2 * NE:(p + 1) * 2 * NE]
                ctx.op("v", lambda ghl=ghl: nc.vector.tensor_copy(
                    ghl[:, 0:NE], pb[4][:, 0:NE]), tt_, gldg.get(p))
                tgv = ctx.op("v", lambda ghl=ghl: nc.vector.tensor_sub(
                    ghl[:, NE:2 * NE], pb[4][:, 0:NE], ghl[:, 0:NE]))
                pbg[4] = tgv
                gldg[p] = ctx.dma(
                    "q", "wC" + str(p),
                    tab.ap()[o + sub * 128: o + sub * 128 + 128, 1024:1040],
                    ghl, deps=(tgv,))
    ctx.moe_lg_e = {}
    lb["grd"] = pbg.get(4)
    for e in (0, 1):
        lb_part1(e, tcols)
        lb_base(4 + e)
        lb_staccs(4 + e)
        lb_lrow_a(4 + e)
        lb_lrow_b(4 + e)
        lb_cols(e, 4 + e)
    A.pop()
    ctx.barrier()

    A.push()
    WS1 = (512, 256)
    at = A.bf(MHB * CAPL)
    gt2 = [[A.bf(1040) for _ in range(CAPT)] for _ in range(2)]
    xe2 = [A.bf(FB * CAPL) for _ in range(2)]
    ysb = A.f32(1024)
    sgt = A.f32(1)
    ctx.moe_sg = None
    Lint = ctx.moe_lint
    pbg2 = {}
    sc_guard = None
    ctx.moe_ye = {}

    def do_gathers(e):
        p = e % 2
        for tt in range(CAPT):
            ctx.dma("g", "ms2", gt2[p][tt][:, :], tab.ap(),
                    in_offset=IndirectOffsetOnAxis(
                        Lint[:, e * CAPT + tt: e * CAPT + tt + 1], 0),
                    deps=(ctx.moe_lg_e[e], ctx.moe_ye.get(p),
                          ctx.tok("ms0"), ctx.tok("ms2")) if tt == 0 else ())
        return ctx.tok("ms2")

    ctx.xe_guard = {}
    ctx.xe_done = {}
    ctx.xe_pb_free = None

    def mk_xe_tasks(e, tgall):
        p = e % 2
        gt = gt2[p]
        xe = xe2[p]
        tasks = []
        for fb in range(FB):
            for t0, nt in ((0, 4), (4, 2)):
                def task(fb=fb, t0=t0, nt=nt):
                    p4 = pb[4].bitcast(BF16)
                    ctx.wait("t", ctx.tok("t"), tgall, ctx.xe_pb_free, tidb)
                    for i in range(nt):
                        itp = nc.tensor.transpose(
                            p4[:, i * 128:(i + 1) * 128],
                            gt[t0 + i][:, fb * 128:(fb + 1) * 128],
                            identb[:, :])
                    tm = ctx.mark("t", itp)
                    ctx.xe_pb_free = ctx.op(
                        "s", lambda: nc.scalar.copy(
                            xe[:, fb * CAPL + t0 * 128:
                               fb * CAPL + (t0 + nt) * 128],
                            p4[:, 0:nt * 128]), tm, ctx.xe_guard.get(p))
                    ctx.xe_done[p] = ctx.xe_pb_free
                tasks.append(task)
        return tasks

    tgn = do_gathers(0)
    for t_ in mk_xe_tasks(0, tgn):
        t_()
    pend = []
    at_free = None
    for e in range(NE):
        gt = gt2[e % 2]
        xe = xe2[e % 2]
        tgall = tgn
        txe = ctx.xe_done[e % 2]
        le = e + 2 if e + 2 < NE else None
        if le is not None:
            lb_part1(le, tcols)
        if e + 1 < NE:
            tgn = do_gathers(e + 1)
            pend = mk_xe_tasks(e + 1, tgn)
        w1_last = w3_last = None
        for wi, w0 in ((0, 0), (1, 512)):
            W = WS1[wi]
            for hb in range(MHB):
                bank = pb[hb % 2]
                ctx.wait("t", ctx.tok("t"), tw1_tok[e], txe,
                         pbg2.get(hb % 2))
                for fb in range(FB):
                    im = nc.tensor.matmul(
                        bank[:, 0:W],
                        w1[:, fb * MH + hb * 128: fb * MH + hb * 128 + 128],
                        xe[:, fb * CAPL + w0: fb * CAPL + w0 + W],
                        start=(fb == 0), stop=(fb == FB - 1))
                tm = ctx.mark("t", im)
                w1_last = tm
                tse = ctx.op("s", lambda hb=hb, bank=bank, w0=w0, W=W:
                             nc.scalar.activation(
                                 at[:, hb * CAPL + w0: hb * CAPL + w0 + W],
                                 bank[:, 0:W], AF.Silu), tm,
                             pbg2.get(("a", hb, w0)), at_free)
                pbg2[hb % 2] = tse
                pbg2[("a", hb, w0)] = tse
                if le is not None and wi == 0 and hb == 6:
                    lb_base(5)
                if le is not None and wi == 1 and hb == 4:
                    lb_lrow_b(5)
                if le is not None and wi == 1 and hb == 10:
                    lb_cols(le, 5)
            if le is not None and wi == 0:
                lb_staccs(5)
            for hb in range(MHB):
                bank = pb[2 + hb % 2]
                ctx.wait("t", ctx.tok("t"), tw3_tok[e], pbg2.get(2 + hb % 2),
                         pbg2.get(("a", hb, w0)))
                for fb in range(FB):
                    im = nc.tensor.matmul(
                        bank[:, 0:W],
                        w3[:, fb * MH + hb * 128: fb * MH + hb * 128 + 128],
                        xe[:, fb * CAPL + w0: fb * CAPL + w0 + W],
                        start=(fb == 0), stop=(fb == FB - 1))
                tm = ctx.mark("t", im)
                w3_last = tm
                the = ctx.op("v", lambda hb=hb, bank=bank, w0=w0, W=W:
                             nc.vector.tensor_mul(
                                 at[:, hb * CAPL + w0: hb * CAPL + w0 + W],
                                 at[:, hb * CAPL + w0: hb * CAPL + w0 + W],
                                 bank[:, 0:W]), tm, pbg2.get(("h", hb, w0)))
                pbg2[2 + hb % 2] = the
                pbg2[("h", hb, w0)] = the
                if hb % 2 == 1 and pend:
                    pend.pop(0)()
            if le is not None and wi == 0:
                lb_lrow_a(5)
        if e + 1 < NE:
            load_w1(e + 1, w1_last, ctx.pcw_tok.get(e + 1))
            load_w3(e + 1, w3_last)
        w2_last = None
        for tt in range(CAPT):
            for nb in range(2):
                bank = pb[6 + nb]
                ctx.wait("t", ctx.tok("t"), tw2_tok[e], pbg2.get(6 + nb))
                for hb in range(MHB):
                    w0 = 0 if tt < 4 else 512
                    ctx.wait("t", pbg2[("h", hb, w0)])
                    im = nc.tensor.matmul(
                        bank[:, 0:512],
                        at[:, hb * CAPL + tt * 128: hb * CAPL + tt * 128 + 128],
                        w2[:, hb * C + nb * 512: hb * C + (nb + 1) * 512],
                        start=(hb == 0), stop=(hb == MHB - 1))
                tm = ctx.mark("t", im)
                w2_last = tm
                if nb == 0:
                    sgf = ctx.op("v", lambda tt=tt, e=e: nc.vector.tensor_add(
                        sgt, gt[tt][:, 1024 + e:1025 + e],
                        gt[tt][:, 1032 + e:1033 + e]), tgall, ctx.moe_sg)
                tye = ctx.op("v", lambda bank=bank, nb=nb:
                             nc.vector.tensor_scalar(
                                 ysb[:, nb * 512:(nb + 1) * 512],
                                 bank[:, 0:512], sgt,
                                 None, ALU.mult), tm, sc_guard, sgf)
                ctx.moe_sg = tye
                pbg2[6 + nb] = tye
                ctx.moe_ye[e % 2] = tye
            ctx.xe_guard[e % 2] = ctx.tok("t")
            ctx.wait("g", tye, ctx.tok("ms0"))
            sc_guard = ctx.dma("g", "ms0", dlt.ap(), ysb[:, :],
                               out_offset=IndirectOffsetOnAxis(
                                   Lint[:, e * CAPT + tt:
                                        e * CAPT + tt + 1], 0),
                               compute_op=ALU.add)
        if e + 1 < NE:
            load_w2(e + 1, w2_last)
        at_free = w2_last
    A.pop()
    ctx.barrier()

    A.push()
    identf = A.f32(128)
    dl = A.f32(1024)
    xts = [A.f32(CH) for _ in range(FB)]
    t2 = ctx.op("v", lambda: nc.vector.memset(identf[:, :], 1.0))
    tidf = ctx.op("g", lambda: nc.gpsimd.affine_select(
        identf[:, :], identf[:, :], pattern=[[-1, 128]], base=0,
        channel_multiplier=1, compare_op=ALU.is_equal, fill=0.0), t2)
    pbg3 = {}
    stg = []
    dl2 = [dl, A.f32(1024)]
    vtok2 = {}
    dstr = ("ms1", "cc0")
    for seq in range(NSEQ):
        for ch in range(NCH):
            lts = [None, None]
            for fb in range(FB):
                lts[fb % 2] = ctx.dma(
                    "q", "xld" + str(fb % 2), xts[fb],
                    xs.ap()[seq][fb * 128:(fb + 1) * 128,
                                 ch * CH:(ch + 1) * CH],
                    deps=tuple(stg[-1:]) if fb < 2 else ())
            stg = []
            vtok = None
            for tt in range(4):
                p = tt % 2
                r0 = seq * T + ch * CH + tt * 128
                dld = ctx.dma("q", dstr[p], dl2[p][:, :],
                              dlt.ap()[r0:r0 + 128, :],
                              deps=(vtok2.get(p), ctx.tok(dstr[p])))
                for fb in range(FB):
                    ctx.wait("t", ctx.tok("t"), dld, tidf, pbg3.get(fb % 2))
                    itp = nc.tensor.transpose(
                        pb[fb % 2][:, 0:128],
                        dl2[p][:, fb * 128:(fb + 1) * 128], identf[:, :])
                    ttp = ctx.mark("t", itp)
                    vtok = ctx.op("v", lambda fb=fb, tt=tt, p=p:
                                  nc.vector.tensor_add(
                                      xts[fb][:, tt * 128:(tt + 1) * 128],
                                      xts[fb][:, tt * 128:(tt + 1) * 128],
                                      pb[fb % 2][:, 0:128]), ttp, lts[fb % 2])
                    pbg3[fb % 2] = vtok
                vtok2[p] = vtok
            for fb in range(FB):
                stg.append(ctx.dma(
                    "q", "xst", xs.ap()[seq][fb * 128:(fb + 1) * 128,
                                             ch * CH:(ch + 1) * CH],
                    xts[fb], deps=(vtok,)))
    ctx.stream_ready = (ctx.tok("xst"), ctx.tok("ms3"))
    A.pop()
    A.pop()


def build_program(stages=ALL_STAGES):
    nc = bass.Bass("TRN2", target_bir_lowering=False)
    global _LAST_CTX
    dt = {}

    def din(name, shape, dtype=F32):
        dt[name] = nc.dram_tensor(name, list(shape), dtype, kind="ExternalInput")

    din("x", [NSEQ, C, T])
    din("wfc", [C, HID]); din("wpr", [HID, C])
    din("wk", [C, C]); din("wv", [C, C]); din("wr", [C, C]); din("wo", [C, C])
    din("fwk", [C, HID]); din("fwr", [C, C]); din("fwv", [HID, C])
    for nm in ("ln1g", "ln1b", "ln2g", "ln2b", "tdec", "tfirst",
               "tmk", "tmv", "tmr", "ftmk", "ftmr"):
        din(nm, [128, FB])
    din("wqkv", [C, 3 * C]); din("bqkv", [128, 24]); din("bqkvr", [1, 3 * C]); din("boutr", [1, C])
    din("wout", [C, C]); din("bout", [128, FB])
    din("wg", [C, NE])
    din("w1", [NE, C, MH]); din("w3", [NE, C, MH]); din("w2", [NE, MH, C])

    y_out = nc.dram_tensor("y", [NSEQ, C, T], F32, kind="ExternalOutput")
    xs = nc.dram_tensor("xs", [NSEQ, C, T], F32, kind="Internal")
    xs2 = nc.dram_tensor("xs2", [NSEQ, C, T], F32, kind="Internal")
    aux = {
        "xnfm": nc.dram_tensor("xnfm", [NSEQ, C, T], BF16, kind="Internal"),
        "cc_in": nc.dram_tensor("cc_in", [NE, NTOK], F32, kind="Internal"),
        "tab": nc.dram_tensor("tab", [TROWS, 1040], BF16, kind="Internal"),
        "dlt": nc.dram_tensor("dlt", [TROWS, C], F32, kind="Internal"),
        "cc_out": [nc.dram_tensor(f"cc_out{i}", [NCORES, NE, NTOK], F32,
                                  kind="Internal", addr_space="Shared")
                   for i in range(2)],
        "w1b": nc.dram_tensor("w1b", [NE, C, MH], BF16, kind="Internal"),
        "w3b": nc.dram_tensor("w3b", [NE, C, MH], BF16, kind="Internal"),
        "w2b": nc.dram_tensor("w2b", [NE, MH, C], BF16, kind="Internal"),
    }

    ctx = Ctx(nc)
    with contextlib.ExitStack() as stack:
        ctx.init_sems(stack)
        big = stack.enter_context(nc.sbuf_tensor("big", [128, BIG_N], BF16))
        sm = stack.enter_context(nc.sbuf_tensor("sm", [128, 512], F32))
        ctx.pb = [stack.enter_context(nc.psum_tensor(f"pb{i}", [128, 512], F32))
                  for i in range(8)]
        ctx.arena = Arena(big)
        stack.enter_context(nc.Block())

        smal = {}
        off = 0
        names = ("ln1g", "ln1b", "ln2g", "ln2b", "tdec", "tfirst",
                 "tmk", "tmv", "tmr", "ftmk", "ftmr")
        for nm in names:
            smal[nm] = sm[:, off:off + FB]; off += FB
            ctx.dma("q", "ini", smal[nm], dt[nm].ap())
        smal["bqkv"] = sm[:, off:off + 24]; off += 24
        ctx.dma("q", "ini", smal["bqkv"], dt["bqkv"].ap())
        smal["bout"] = sm[:, off:off + FB]; off += FB
        ctx.dma("q", "ini", smal["bout"], dt["bout"].ap())
        tok_small = ctx.tok("ini")
        ctx.c_rsum = sm[:, off:off + 1]; off += 1
        ctx.c_ones = sm[:, off:off + 1]; off += 1
        ctx.c_brow = sm[0:1, off:off + 128]; off += 128
        ctx.c_eps = sm[:, off:off + 1]; off += 1
        ctx.c_o128 = sm[:, off:off + 1]; off += 1
        ctx.c_ones16 = sm.bitcast(F16)[:, 2 * off:2 * off + 1]; off += 1
        nc.vector.memset(ctx.c_rsum, 1.0 / 1024.0)
        nc.vector.memset(ctx.c_ones, 1.0)
        nc.vector.memset(ctx.c_eps, 1e-5)
        nc.vector.memset(ctx.c_o128, 1.0 / 128.0)
        nc.vector.memset(ctx.c_ones16, 1.0)
        tc = ctx.mark("v", nc.vector.memset(ctx.c_brow, 1.0))
        ctx.wait("t", tc)
        ctx.ln_prior = []
        ctx.lnsq_guard = None
        ctx.moe_g_guard = None
        ctx.moe_x_guard = None
        ctx.moe_pb1 = {}
        drv = {"wdec": sm[:, off:off + FB], "eu": sm[:, off + FB:off + 2 * FB]}
        off += 2 * FB
        t1 = ctx.op("s", lambda: nc.scalar.activation(
            drv["wdec"], smal["tdec"], AF.Exp), tok_small)
        t2 = ctx.op("s", lambda: nc.scalar.activation(
            drv["wdec"], drv["wdec"], AF.Exp, scale=-1.0), t1)
        ts = ctx.op("s", lambda: nc.scalar.activation(
            drv["eu"], smal["tfirst"], AF.Exp), tok_small)
        ctx.drv_tok = ts
        ctx.smal = smal
        ctx.drv = drv

        for s_ in range(NSEQ):
            tin = ctx.dma("g", "ms0", xs.ap()[s_], dt["x"].ap()[s_])
        ctx.stream_ready = (tin,)

        ctx.pcw_tok = {}
        if "moe" in stages:
            for e in range(NE):
                ctx.dma("g", "pcw", aux["w1b"].ap()[e], dt["w1"].ap()[e])
                ctx.dma("g", "pcw", aux["w3b"].ap()[e], dt["w3"].ap()[e])
                ctx.dma("g", "pcw", aux["w2b"].ap()[e], dt["w2"].ap()[e])
            for e in range(NE):
                ctx.pcw_tok[e] = ctx.tok("pcw")

        moe_i = 0
        for st in stages:
            if st == "f0":
                emit_ffn(ctx, dt, xs)
            elif st == "rwkv":
                emit_rwkv(ctx, dt, xs, xs2)
            elif st == "attn":
                emit_attn(ctx, dt, xs)
            elif st == "moe":
                emit_moe(ctx, dt, xs, {**aux, "cc": aux["cc_out"][moe_i]})
                moe_i += 1

        _LAST_CTX = ctx
        ctx.barrier()
        ctx.wait("g", ctx.tok("xst"))
        for s_ in range(NSEQ):
            tout = ctx.dma("g", "fin", y_out.ap()[s_], xs.ap()[s_])
        ctx.wait("g", tout)
    return nc


def _prep_inputs(inputs):
    f32 = np.float32
    g = lambda k: np.asarray(inputs[k], dtype=f32)

    def vec128(v):
        return np.ascontiguousarray(np.asarray(v, f32).reshape(-1)
                                    .reshape(FB, 128).T)

    com = {
        "wfc": np.ascontiguousarray(g("L0_ffn_W_fc").T),
        "wpr": np.ascontiguousarray(g("L0_ffn_W_proj").T),
        "wk": np.ascontiguousarray(g("L0_attn_Wk").T),
        "wv": np.ascontiguousarray(g("L0_attn_Wv").T),
        "wr": np.ascontiguousarray(g("L0_attn_Wr").T),
        "wo": np.ascontiguousarray(g("L0_attn_Wo").T),
        "fwk": np.ascontiguousarray(g("L0_attn_fWk").T),
        "fwr": np.ascontiguousarray(g("L0_attn_fWr").T),
        "fwv": np.ascontiguousarray(g("L0_attn_fWv").T),
        "ln1g": vec128(g("L0_attn_ln1_g")), "ln1b": vec128(g("L0_attn_ln1_b")),
        "ln2g": vec128(g("L0_attn_ln2_g")), "ln2b": vec128(g("L0_attn_ln2_b")),
        "tdec": vec128(g("L0_attn_time_decay")),
        "tfirst": vec128(g("L0_attn_time_first")),
        "tmk": vec128(g("L0_attn_tmk")), "tmv": vec128(g("L0_attn_tmv")),
        "tmr": vec128(g("L0_attn_tmr")),
        "ftmk": vec128(g("L0_attn_f_tmk")), "ftmr": vec128(g("L0_attn_f_tmr")),
        "wqkv": np.ascontiguousarray(g("L1_attn_Wqkv")),
        "bqkv": np.ascontiguousarray(g("L1_attn_bqkv").reshape(24, 128).T),
        "wout": np.ascontiguousarray(g("L1_attn_Wout")),
        "bqkvr": np.ascontiguousarray(g("L1_attn_bqkv").reshape(1, 3 * C)),
        "boutr": np.ascontiguousarray(g("L1_attn_bout").reshape(1, C)),
        "bout": vec128(g("L1_attn_bout")),
        "wg": np.ascontiguousarray(g("L1_ffn_Wg").T),
        "w1": np.ascontiguousarray(g("L1_ffn_W1").transpose(0, 2, 1)),
        "w3": np.ascontiguousarray(g("L1_ffn_W3").transpose(0, 2, 1)),
        "w2": np.ascontiguousarray(g("L1_ffn_W2").transpose(0, 2, 1)),
    }
    x = g("L0_attn_x")
    in_maps = []
    for c in range(NCORES):
        m = dict(com)
        m["x"] = np.ascontiguousarray(x[c * NSEQ:(c + 1) * NSEQ].transpose(0, 2, 1))
        in_maps.append(m)
    return in_maps


_LAST_CTX = None
_PROG_CACHE = {}


def kernel(**inputs):
    key = tuple(ALL_STAGES)
    if key not in _PROG_CACHE:
        _PROG_CACHE[key] = build_program(ALL_STAGES)
    nc = _PROG_CACHE[key]
    in_maps = _prep_inputs(inputs)
    res = run_bass_kernel_spmd(nc, in_maps, core_ids=list(range(NCORES)))
    outs = [r["y"].transpose(0, 2, 1) for r in res.results]
    return np.ascontiguousarray(np.concatenate(outs, axis=0)).astype(np.float32)


# version 16 (final)
# speedup vs baseline: 1.0487x; 1.0056x over previous
import contextlib
import numpy as np
import concourse.bass as bass
import concourse.mybir as mybir
from concourse.bass_core_rust import IndirectOffsetOnAxis
from concourse.bass_utils import run_bass_kernel_spmd

F32 = mybir.dt.float32
F16 = mybir.dt.float16
BF16 = mybir.dt.bfloat16
I32 = mybir.dt.int32
AF = mybir.ActivationFunctionType
ALU = mybir.AluOpType

NCORES = 8
B, T, C = 16, 2048, 1024
NSEQ = B // NCORES
FB = C // 128
HID = 4096
HB = HID // 128
NE = 8
MH = 2048
CH = 512
NCH = T // CH
NTOK = NSEQ * T

SNN_B = (0.24419428096993973, 0.4639691338428854, 0.6861859295255306,
         0.9059607823984763, 1.0839784132255623)

ALL_STAGES = ("f0", "rwkv", "f0", "moe", "attn", "moe")
BIG_N = 100 * 1024


class Ctx:
    def __init__(self, nc):
        self.nc = nc
        self.E = {"t": nc.tensor, "v": nc.vector, "s": nc.scalar,
                  "g": nc.gpsimd, "q": nc.sync}
        self.sem = {}
        self.cnt = {}
        self.waited = {}

    DMA_STREAMS = ("ini", "xld0", "xld1", "xst", "wt0", "wt1", "wA0", "wA1",
                   "wB0", "wB1", "wC0", "wC1", "ms0", "ms1", "ms2", "ms3",
                   "ms4", "cc0", "cc1", "pcw", "wD0", "wD1", "fin")

    def init_sems(self, stack):
        for name in ("t", "v", "s", "g", "q") + self.DMA_STREAMS:
            self.sem[name] = stack.enter_context(self.nc.semaphore("sem_" + name))
            self.cnt[name] = 0

    def tok(self, name):
        return (name, self.cnt[name])

    def mark(self, name, inst, n=1):
        self.cnt[name] += n
        inst.then_inc(self.sem[name], n)
        return (name, self.cnt[name])

    def wait(self, waiter, *tokens):
        for tk in tokens:
            if tk is None:
                continue
            sname, v = tk
            if v <= 0:
                continue
            key = (waiter, sname)
            if self.waited.get(key, 0) >= v:
                continue
            self.waited[key] = v
            self.E[waiter].wait_ge(self.sem[sname], v)

    def op(self, eng, emit, *deps):
        self.wait(eng, self.tok(eng), *deps)
        return self.mark(eng, emit())

    def dma(self, qname, stream, out, in_, deps=(), transpose=False,
            in_offset=None, out_offset=None, **kw):
        eng = self.E[qname]
        self.wait(qname, *deps)
        if transpose:
            inst = eng.dma_start_transpose(out, in_)
        elif in_offset is not None or out_offset is not None:
            inst = eng.indirect_dma_start(out=out, out_offset=out_offset,
                                          in_=in_, in_offset=in_offset, **kw)
        else:
            inst = eng.dma_start(out=out, in_=in_, **kw)
        return self.mark(stream, inst, 16)

    def barrier(self):
        toks = [self.tok(s) for s in self.DMA_STREAMS]
        for name in ("t", "v", "s", "g", "q"):
            self.wait(name, self.tok(name), *toks)
            toks.append(self.mark(name, self.E[name].nop()))
        for name in ("t", "v", "s", "g", "q"):
            self.wait(name, *toks)


class Arena:

    def __init__(self, big):
        self.big = big
        self.off = 0
        self._marks = []

    def push(self):
        self._marks.append(self.off)

    def pop(self):
        self.off = self._marks.pop()

    def bf(self, n):
        o = self._align(2)
        self.off = o + 2 * n
        assert self.off <= 2 * BIG_N, f"SBUF arena overflow: {self.off}"
        return self.big[:, o // 2: o // 2 + n]

    def f32(self, n):
        o = self._align(4)
        self.off = o + 4 * n
        assert self.off <= 2 * BIG_N, f"SBUF arena overflow: {self.off}"
        return self.big.bitcast(F32)[:, o // 4: o // 4 + n]

    def f16(self, n):
        o = self._align(4)
        self.off = o + 2 * n
        assert self.off <= 2 * BIG_N, f"SBUF arena overflow: {self.off}"
        return self.big.bitcast(F16)[:, o // 2: o // 2 + n]

    def i32(self, n):
        o = self._align(4)
        self.off = o + 4 * n
        assert self.off <= 2 * BIG_N, f"SBUF arena overflow: {self.off}"
        return self.big.bitcast(I32)[:, o // 4: o // 4 + n]

    def _align(self, a):
        return (self.off + a - 1) // a * a


def ln_chunk(ctx, seq, ch, xs, xts, xnts, gain=None, bias=None, src=None,
             xts_free=(), apply_deps=()):
    nc, pb = ctx.nc, ctx.pb
    t0 = ch * CH
    if src is None:
        lts = [None, None]
        for fb in range(FB):
            lts[fb % 2] = ctx.dma(
                "q", "xld" + str(fb % 2), xts[fb],
                xs.ap()[seq][fb * 128:(fb + 1) * 128, t0:t0 + CH],
                deps=xts_free if fb < 2 else ())
        ldtoks = [lts[fb % 2] for fb in range(FB)]
    else:
        ldtoks = [src] * FB
    sqt = [None, None]
    tsqs = []
    for fb in range(FB):
        sq = ctx.lnsq[:, (fb % 2) * CH:(fb % 2) * CH + CH]
        tsq = ctx.op("s", lambda sq=sq, fb=fb: nc.scalar.activation(
            sq, xts[fb], AF.Square), ldtoks[fb], sqt[fb % 2], ctx.lnsq_guard,
            *ctx.ln_prior)
        tsqs.append(tsq)
        ctx.wait("t", ctx.tok("t"), ldtoks[fb], tsq,
                 *(ctx.ln_prior if fb == 0 else ()))
        nc.tensor.matmul(pb[4][0:1, 0:CH], ctx.c_rsum, xts[fb],
                         start=(fb == 0), stop=(fb == FB - 1))
        i2 = nc.tensor.matmul(pb[5][0:1, 0:CH], ctx.c_rsum, sq,
                              start=(fb == 0), stop=(fb == FB - 1))
        sqt[fb % 2] = ctx.mark("t", i2)
    tstat = ctx.tok("t")
    mu2 = ctx.lnrows[0:1, 0:CH]
    rstd = ctx.lnrows[0:1, CH:2 * CH]
    mrs = ctx.lnrows[0:1, 2 * CH:3 * CH]
    ts0 = ctx.op("s", lambda: nc.scalar.activation(mu2, pb[4][0:1, 0:CH],
                                                   AF.Square), tstat, *ctx.ln_prior)
    tv = ctx.op("v", lambda: nc.vector.tensor_sub(mu2, pb[5][0:1, 0:CH], mu2),
                ts0, tstat)
    ts = ctx.op("s", lambda: nc.scalar.activation(rstd, mu2, AF.Ln,
                                                  bias=ctx.c_eps[0:1, :]), tv)
    tr = ctx.op("s", lambda: nc.scalar.activation(rstd, rstd, AF.Exp,
                                                  scale=-0.5), ts)
    tv2 = ctx.op("v", lambda: nc.vector.tensor_mul(mrs, pb[4][0:1, 0:CH], rstd), tr)
    ctx.wait("t", ctx.tok("t"), tv2, ts0)
    ib = None
    for j in range(0, CH, 512):
        w = min(512, CH - j)
        nc.tensor.matmul(pb[4][:, j:j + w], ctx.c_brow,
                         rstd[0:1, j:j + w], start=True, stop=True)
        ib = nc.tensor.matmul(pb[5][:, j:j + w], ctx.c_brow,
                              mrs[0:1, j:j + w], start=True, stop=True)
    tb = ctx.mark("t", ib)
    outs = []
    t1tok = [tsqs[-2], tsqs[-1]]
    for fb in range(FB):
        t1 = ctx.lnsq[:, 0:CH] if fb % 2 == 0 else ctx.lnsq[:, CH:2 * CH]
        tm1 = ctx.op("v", lambda t1=t1, fb=fb: nc.vector.tensor_mul(
            t1, xts[fb], pb[4][:, 0:CH]), tb, t1tok[fb % 2],
            *(apply_deps if fb == 0 else ()))
        if gain is None:
            iv = ctx.op("v", lambda t1=t1, fb=fb: nc.vector.tensor_sub(
                xnts[fb], t1, pb[5][:, 0:CH]), tm1)
            outs.append(iv)
            t1tok[fb % 2] = iv
        else:
            tv3 = ctx.op("v", lambda t1=t1: nc.vector.tensor_sub(
                t1, t1, pb[5][:, 0:CH]), tm1)
            si = ctx.op("v", lambda t1=t1, fb=fb: nc.vector.tensor_scalar(
                xnts[fb], t1, gain[:, fb:fb + 1], bias[:, fb:fb + 1],
                ALU.mult, ALU.add), tv3)
            outs.append(si)
            t1tok[fb % 2] = si
    ctx.ln_prior = [ctx.tok("v"), ctx.tok("s")]
    return outs, ldtoks


def emit_ffn(ctx, dt, xs):
    nc, pb = ctx.nc, ctx.pb
    ctx.barrier()
    A = ctx.arena
    A.push()
    wpr = A.bf(HB * C)
    wfcq = [A.bf(FB * 1024) for _ in range(2)]
    h = A.bf(HB * CH)
    xn = [A.bf(CH) for _ in range(FB)]
    rt = [A.bf(CH) for _ in range(2)]
    xts = [A.f32(CH) for _ in range(FB)]
    yt = A.f32(CH)
    acc = A.f32(CH)
    ctx.lnsq = A.f32(2 * CH)
    ctx.lnrows = A.f32(3 * CH)

    twpr = None
    for hb in range(HB):
        twpr = ctx.dma("g", "wt0", wpr[:, hb * C:(hb + 1) * C],
                       dt["wpr"].ap()[hb * 128:(hb + 1) * 128, :])
    ctx.wait("q", *ctx.stream_ready)

    nq_total = NSEQ * NCH * 4

    def load_wfcq(qg, dep):
        buf = wfcq[qg % 2]
        q = qg % 4
        tq = None
        for fb in range(FB):
            tq = ctx.dma("g", "wA" + str(qg % 2),
                         buf[:, fb * 1024:(fb + 1) * 1024],
                         dt["wfc"].ap()[fb * 128:(fb + 1) * 128,
                                        q * 1024:(q + 1) * 1024],
                         deps=(dep,) if fb == 0 else ())
        return tq

    wfct = {0: load_wfcq(0, None), 1: load_wfcq(1, None)}
    qcons = {}
    evt = {}
    ebt = {}
    ybt = {}
    rtt = [None, None]
    resid_tok = []
    for seq in range(NSEQ):
        for ch in range(NCH):
            cidx = seq * NCH + ch
            xn_toks, _ = ln_chunk(ctx, seq, ch, xs, xts, xn,
                                  xts_free=resid_tok[-1:])
            resid_tok = []
            for q in range(4):
                qg = cidx * 4 + q
                ctx.wait("t", wfct[qg], *(xn_toks if q == 0 else ()))
                for k8 in range(8):
                    hb = q * 8 + k8
                    bank = pb[hb % 4]
                    ctx.wait("t", ctx.tok("t"), ebt.get(hb % 4))
                    for fb in range(FB):
                        im = nc.tensor.matmul(
                            bank[:, 0:CH],
                            wfcq[qg % 2][:, fb * 1024 + k8 * 128:
                                         fb * 1024 + k8 * 128 + 128],
                            xn[fb], start=(fb == 0), stop=(fb == FB - 1))
                    tm = ctx.mark("t", im)
                    tmx = ctx.op("v", lambda p=hb % 2, bank=bank:
                                 nc.vector.tensor_scalar_max(
                                     rt[p], bank[:, 0:CH], 0.0),
                                 tm, rtt[hb % 2])
                    tie = ctx.op("s", lambda p=hb % 2, hb=hb:
                                 nc.scalar.activation(
                                     h[:, hb * CH:(hb + 1) * CH], rt[p],
                                     AF.Square), tmx, evt.get(hb))
                    evt[hb] = ebt[hb % 4] = rtt[hb % 2] = tie
                qcons[qg] = ctx.tok("t")
                if qg + 2 < nq_total:
                    wfct[qg + 2] = load_wfcq(qg + 2, qcons[qg])
            for cb in range(FB):
                ybank = pb[6 + cb % 2]
                ctx.wait("t", ctx.tok("t"), twpr, ybt.get(cb % 2))
                for hb in range(HB):
                    ctx.wait("t", evt[hb])
                    im = nc.tensor.matmul(
                        ybank[:, 0:CH],
                        wpr[:, hb * C + cb * 128: hb * C + cb * 128 + 128],
                        h[:, hb * CH:(hb + 1) * CH],
                        start=(hb == 0), stop=(hb == HB - 1))
                tm = ctx.mark("t", im)
                tv = ctx.op("v", lambda ybank=ybank: nc.vector.tensor_copy(
                    yt, ybank[:, 0:CH]), tm)
                ybt[cb % 2] = tv
                ta = ctx.op("v", lambda: nc.vector.tensor_scalar(
                    acc, yt, SNN_B[0], None, ALU.is_gt), tv)
                for i in (1, 2, 3, 4):
                    ta = ctx.op("v", lambda i=i: nc.vector.scalar_tensor_tensor(
                        acc, yt, SNN_B[i], acc, ALU.is_gt, ALU.add), ta)
                tvx = ctx.op("v", lambda cb=cb: nc.vector.scalar_tensor_tensor(
                    xts[cb], acc, 0.1, xts[cb], ALU.mult, ALU.add), ta)
                resid_tok.append(ctx.dma(
                    "q", "xst", xs.ap()[seq][cb * 128:(cb + 1) * 128,
                                             ch * CH:(ch + 1) * CH], xts[cb],
                    deps=(tvx,)))
    ctx.stream_ready = (ctx.tok("xst"), ctx.tok("ms3"))
    A.pop()


def emit_rwkv(ctx, dt, xs, xs2):
    nc, pb = ctx.nc, ctx.pb
    ctx.barrier()
    A = ctx.arena
    sm = ctx.smal
    drv = ctx.drv

    A.push()
    wk = A.bf(FB * C); wv = A.bf(FB * C); wr = A.bf(FB * C); wo = A.bf(FB * C)
    xts = [A.f32(CH) for _ in range(FB)]
    xp = xts
    xn1 = [A.bf(CH + 1) for _ in range(FB)]
    xk = [A.bf(CH) for _ in range(FB)]
    xv = [A.bf(CH) for _ in range(FB)]
    xr = [A.bf(CH) for _ in range(FB)]
    kt = [A.f32(CH) for _ in range(FB)]
    vt = [A.f32(CH) for _ in range(FB)]
    sr = [A.bf(CH) for _ in range(FB)]
    rw = [A.bf(CH) for _ in range(FB)]
    At = A.f32(CH + 1); Bt = A.f32(CH + 1)
    ab_st = A.f32(2 * FB)
    ctx.lnsq = A.f32(2 * CH)
    ctx.lnrows = A.f32(3 * CH)

    tw = None
    for w_ap, nm in ((wk, "wk"), (wv, "wv"), (wr, "wr"), (wo, "wo")):
        for fb in range(FB):
            tw = ctx.dma("g", "wt0", w_ap[:, fb * C:(fb + 1) * C],
                         dt[nm].ap()[fb * 128:(fb + 1) * 128, :])
    ctx.wait("q", *ctx.stream_ready)

    ebt = {}
    obt = {}
    scan_dep = []
    r1_resid = []
    for seq in range(NSEQ):
        tz = ctx.op("v", lambda: nc.vector.memset(ab_st[:, 0:2 * FB], 0.0),
                    *scan_dep)
        zcols = []
        for fb in range(FB):
            zcols.append(ctx.op("v", lambda fb=fb: nc.vector.memset(
                xn1[fb][:, 0:1], 0.0)))
        for ch in range(NCH):
            xp_toks, _ = ln_chunk(ctx, seq, ch, xs, xts, xp,
                                  xts_free=r1_resid[-1:])
            if ch > 0:
                for fb in range(FB):
                    ctx.op("v", lambda fb=fb: nc.vector.tensor_copy(
                        xn1[fb][:, 0:1], xn1[fb][:, CH:CH + 1]))
            xn_views = [xn1[fb][:, 1:1 + CH] for fb in range(FB)]
            t_src = xp_toks[-1]
            xn_toks, _ = ln_chunk(ctx, seq, ch, xs, xp, xn_views,
                                  gain=sm["ln1g"], bias=sm["ln1b"], src=t_src)
            mix_toks = []
            for fb in range(FB):
                td = ctx.op("v", lambda fb=fb: nc.vector.tensor_sub(
                    rw[fb], xn1[fb][:, 1:1 + CH], xn1[fb][:, 0:CH]), xn_toks[fb])
                ctx.op("v", lambda fb=fb: nc.vector.scalar_tensor_tensor(
                    xk[fb], rw[fb], sm["tmk"][:, fb:fb + 1], xn1[fb][:, 0:CH],
                    ALU.mult, ALU.add), td)
                ctx.op("v", lambda fb=fb: nc.vector.scalar_tensor_tensor(
                    xv[fb], rw[fb], sm["tmv"][:, fb:fb + 1], xn1[fb][:, 0:CH],
                    ALU.mult, ALU.add))
                tm3 = ctx.op("v", lambda fb=fb: nc.vector.scalar_tensor_tensor(
                    xr[fb], rw[fb], sm["tmr"][:, fb:fb + 1], xn1[fb][:, 0:CH],
                    ALU.mult, ALU.add))
                mix_toks.append(tm3)
            proj_ev = []
            for w_ap, xin, which in ((wk, xk, "k"), (wv, xv, "v"), (wr, xr, "r")):
                for ob in range(FB):
                    bank = pb[(ob % 4)]
                    ctx.wait("t", ctx.tok("t"), tw, ebt.get(ob % 4),
                             mix_toks[-1])
                    for fb in range(FB):
                        im = nc.tensor.matmul(
                            bank[:, 0:CH],
                            w_ap[:, fb * C + ob * 128: fb * C + ob * 128 + 128],
                            xin[fb], start=(fb == 0), stop=(fb == FB - 1))
                    tm = ctx.mark("t", im)
                    if which == "k":
                        ev = ctx.op("s", lambda ob=ob, bank=bank:
                                    nc.scalar.copy(kt[ob], bank[:, 0:CH]),
                                    tm)
                    elif which == "v":
                        ev = ctx.op("v", lambda ob=ob, bank=bank:
                                    nc.vector.tensor_copy(vt[ob], bank[:, 0:CH]),
                                    tm)
                    else:
                        ev = ctx.op("s", lambda ob=ob, bank=bank:
                                    nc.scalar.activation(sr[ob], bank[:, 0:CH],
                                                         AF.Sigmoid), tm)
                    ebt[ob % 4] = ev
                    proj_ev.append(ev)
            Et = ctx.lnsq[:, 0:CH]
            EVt = ctx.lnsq[:, CH:2 * CH]
            rw_toks = []
            for fb in range(FB):
                te = ctx.op("s", lambda fb=fb: nc.scalar.activation(
                    Et, kt[fb], AF.Exp), proj_ev[fb], ctx.ln_prior[0],
                    *( [rw_toks[-1]] if fb else []))
                tev = ctx.op("v", lambda fb=fb: nc.vector.tensor_mul(
                    EVt, Et, vt[fb]), te, proj_ev[FB + fb])
                ca = ctx.op("v", lambda fb=fb: nc.vector.tensor_copy(
                    At[:, 0:1], ab_st[:, fb:fb + 1]), tz)
                ta = ctx.op("v", lambda fb=fb: nc.vector.tensor_tensor_scan(
                    At[:, 1:1 + CH], drv["wdec"][:, fb:fb + 1].to_broadcast([128, CH]),
                    EVt, ab_st[:, fb:fb + 1], ALU.mult, ALU.add), tev, ca,
                    ctx.drv_tok)
                cb_ = ctx.op("v", lambda fb=fb: nc.vector.tensor_copy(
                    Bt[:, 0:1], ab_st[:, FB + fb:FB + fb + 1]))
                tb_ = ctx.op("v", lambda fb=fb: nc.vector.tensor_tensor_scan(
                    Bt[:, 1:1 + CH], drv["wdec"][:, fb:fb + 1].to_broadcast([128, CH]),
                    Et, ab_st[:, FB + fb:FB + fb + 1], ALU.mult, ALU.add), cb_)
                ctx.op("v", lambda fb=fb: nc.vector.tensor_copy(
                    ab_st[:, fb:fb + 1], At[:, CH:CH + 1]), ta)
                ctx.op("v", lambda fb=fb: nc.vector.tensor_copy(
                    ab_st[:, FB + fb:FB + fb + 1], Bt[:, CH:CH + 1]), tb_)
                tn = ctx.op("v", lambda fb=fb: nc.vector.scalar_tensor_tensor(
                    EVt, EVt, drv["eu"][:, fb:fb + 1], At[:, 0:CH],
                    ALU.mult, ALU.add), ta)
                tdn = ctx.op("v", lambda fb=fb: nc.vector.scalar_tensor_tensor(
                    Et, Et, drv["eu"][:, fb:fb + 1], Bt[:, 0:CH],
                    ALU.mult, ALU.add), tb_)
                trl = ctx.op("s", lambda: nc.scalar.activation(
                    Et, Et, AF.Ln), tdn)
                tr_ = ctx.op("s", lambda: nc.scalar.activation(
                    Et, Et, AF.Exp, scale=-1.0), trl)
                ty = ctx.op("v", lambda fb=fb: nc.vector.tensor_mul(
                    kt[fb], EVt, Et), tr_, tn)
                trw = ctx.op("v", lambda fb=fb: nc.vector.tensor_mul(
                    rw[fb], kt[fb], sr[fb]), ty, proj_ev[2 * FB + fb])
                rw_toks.append(trw)
            ctx.lnsq_guard = ctx.tok("v")
            for ob in range(FB):
                bank = pb[6 + ob % 2]
                ctx.wait("t", ctx.tok("t"), obt.get(ob % 2), rw_toks[-1])
                for fb in range(FB):
                    im = nc.tensor.matmul(
                        bank[:, 0:CH],
                        wo[:, fb * C + ob * 128: fb * C + ob * 128 + 128],
                        rw[fb], start=(fb == 0), stop=(fb == FB - 1))
                tm = ctx.mark("t", im)
                tv = ctx.op("v", lambda ob=ob, bank=bank: nc.vector.tensor_add(
                    xp[ob], xp[ob], bank[:, 0:CH]), tm)
                obt[ob % 2] = tv
                r1_resid.append(ctx.dma(
                    "q", "xst",
                    xs2.ap()[seq][ob * 128:(ob + 1) * 128,
                                  ch * CH:(ch + 1) * CH], xp[ob],
                    deps=(tv,)))
            scan_dep = [ctx.tok("xst")]
    A.pop()
    ctx.barrier()
    r1_done = (ctx.tok("xst"),)

    A.push()
    fwv = A.bf(16 * C * 2)
    fwkq = [A.bf(FB * 1024) for _ in range(2)]
    fwr = A.bf(FB * C)
    kk = A.bf(16 * CH * 2)
    xn2 = [A.bf(CH + 1) for _ in range(FB)]
    xk2 = [A.bf(CH) for _ in range(FB)]
    xr2 = [kk[:, fb * CH:(fb + 1) * CH] for fb in range(FB)]
    sr2 = [A.bf(CH) for _ in range(FB)]
    rt = [A.bf(CH) for _ in range(2)]
    xts2 = [A.f32(CH) for _ in range(FB)]
    ctx.lnsq = A.f32(2 * CH)
    ctx.lnrows = A.f32(3 * CH)
    tmp = ctx.lnsq[:, 0:CH]
    xo = ctx.lnsq[:, CH:2 * CH]

    tfv = None
    for hb in range(32):
        tfv = ctx.dma("g", "wt1", fwv[:, hb * C:(hb + 1) * C],
                      dt["fwv"].ap()[hb * 128:(hb + 1) * 128, :])
    for fb in range(FB):
        tfr = ctx.dma("g", "wt1", fwr[:, fb * C:(fb + 1) * C],
                      dt["fwr"].ap()[fb * 128:(fb + 1) * 128, :])
    tfr = ctx.tok("wt1")
    ctx.wait("q", *r1_done)

    nq_total = NSEQ * NCH * 4

    def load_fwkq(qg, dep):
        buf = fwkq[qg % 2]
        q = qg % 4
        tq = None
        for fb in range(FB):
            tq = ctx.dma("g", "wA" + str(qg % 2),
                         buf[:, fb * 1024:(fb + 1) * 1024],
                         dt["fwk"].ap()[fb * 128:(fb + 1) * 128,
                                        q * 1024:(q + 1) * 1024],
                         deps=(dep,) if fb == 0 else ())
        return tq

    fwkt = {0: load_fwkq(0, None), 1: load_fwkq(1, None)}
    ctx.xo_guard = None
    qcons = {}
    evt = {}
    ebt = {}
    ybt = {}
    rtt = [None, None]
    resid_tok = []
    for seq in range(NSEQ):
        zc = []
        for fb in range(FB):
            zc.append(ctx.op("v", lambda fb=fb: nc.vector.memset(
                xn2[fb][:, 0:1], 0.0), *resid_tok[-1:]))
        for ch in range(NCH):
            cidx = seq * NCH + ch
            xn_toks, _ = ln_chunk(ctx, seq, ch, xs2, xts2,
                                  [xn2[fb][:, 1:1 + CH] for fb in range(FB)],
                                  gain=sm["ln2g"], bias=sm["ln2b"],
                                  xts_free=resid_tok[-1:])
            mix_toks = []
            for fb in range(FB):
                td = ctx.op("v", lambda fb=fb: nc.vector.tensor_sub(
                    rt[0], xn2[fb][:, 1:1 + CH], xn2[fb][:, 0:CH]), xn_toks[fb],
                    rtt[0])
                ctx.op("v", lambda fb=fb: nc.vector.scalar_tensor_tensor(
                    xk2[fb], rt[0], sm["ftmk"][:, fb:fb + 1], xn2[fb][:, 0:CH],
                    ALU.mult, ALU.add), td)
                tm2 = ctx.op("v", lambda fb=fb: nc.vector.scalar_tensor_tensor(
                    xr2[fb], rt[0], sm["ftmr"][:, fb:fb + 1], xn2[fb][:, 0:CH],
                    ALU.mult, ALU.add))
                rtt[0] = tm2
                mix_toks.append(tm2)
            for fb in range(FB):
                ctx.op("v", lambda fb=fb: nc.vector.tensor_copy(
                    xn2[fb][:, 0:1], xn2[fb][:, CH:CH + 1]), mix_toks[-1])
            sh_tok = ctx.tok("v")
            sr_toks = []
            for ob in range(FB):
                bank = pb[6 + ob % 2]
                ctx.wait("t", ctx.tok("t"), tfr, ybt.get(ob % 2), mix_toks[-1])
                for fb in range(FB):
                    im = nc.tensor.matmul(
                        bank[:, 0:CH],
                        fwr[:, fb * C + ob * 128: fb * C + ob * 128 + 128],
                        xr2[fb], start=(fb == 0), stop=(fb == FB - 1))
                tm = ctx.mark("t", im)
                ts_ = ctx.op("s", lambda ob=ob, bank=bank: nc.scalar.activation(
                    sr2[ob], bank[:, 0:CH], AF.Sigmoid), tm)
                ybt[ob % 2] = ts_
                sr_toks.append(ts_)
            for q in range(4):
                qg = cidx * 4 + q
                ctx.wait("t", fwkt[qg])
                for k8 in range(8):
                    hb = q * 8 + k8
                    bank = pb[hb % 4]
                    ctx.wait("t", ctx.tok("t"), ebt.get(hb % 4))
                    for fb in range(FB):
                        im = nc.tensor.matmul(
                            bank[:, 0:CH],
                            fwkq[qg % 2][:, fb * 1024 + k8 * 128:
                                         fb * 1024 + k8 * 128 + 128],
                            xk2[fb], start=(fb == 0), stop=(fb == FB - 1))
                    tm = ctx.mark("t", im)
                    tmx = ctx.op("v", lambda p=hb % 2, bank=bank:
                                 nc.vector.tensor_scalar_max(
                                     rt[p], bank[:, 0:CH], 0.0),
                                 tm, rtt[hb % 2])
                    tie = ctx.op("s", lambda p=hb % 2, hb=hb:
                                 nc.scalar.activation(
                                     kk[:, hb * CH:(hb + 1) * CH], rt[p],
                                     AF.Square), tmx, evt.get(hb))
                    evt[hb] = ebt[hb % 4] = rtt[hb % 2] = tie
                qcons[qg] = ctx.tok("t")
                if qg + 2 < nq_total:
                    fwkt[qg + 2] = load_fwkq(qg + 2, qcons[qg])
            for cb in range(FB):
                ybank = pb[6 + cb % 2]
                tld = ctx.dma("q", "ms1", xo,
                              xs.ap()[seq][cb * 128:(cb + 1) * 128,
                                           ch * CH:(ch + 1) * CH],
                              deps=(ctx.xo_guard, *ctx.ln_prior))
                ctx.wait("t", ctx.tok("t"), tfv, ybt.get(cb % 2))
                for hb in range(32):
                    ctx.wait("t", evt[hb])
                    im = nc.tensor.matmul(
                        ybank[:, 0:CH],
                        fwv[:, hb * C + cb * 128: hb * C + cb * 128 + 128],
                        kk[:, hb * CH:(hb + 1) * CH],
                        start=(hb == 0), stop=(hb == 31))
                tm = ctx.mark("t", im)
                tv1 = ctx.op("v", lambda cb=cb, ybank=ybank: nc.vector.tensor_mul(
                    tmp, ybank[:, 0:CH], sr2[cb]), tm, sr_toks[cb])
                ybt[cb % 2] = tv1
                tv2 = ctx.op("v", lambda cb=cb: nc.vector.tensor_add(
                    xts2[cb], xts2[cb], tmp), tv1)
                tv3 = ctx.op("v", lambda cb=cb: nc.vector.tensor_add(
                    xts2[cb], xts2[cb], xo), tv2, tld)
                ctx.xo_guard = tv3
                ctx.lnsq_guard = tv3
                resid_tok.append(ctx.dma(
                    "q", "xst", xs.ap()[seq][cb * 128:(cb + 1) * 128,
                                             ch * CH:(ch + 1) * CH], xts2[cb],
                    deps=(tv3,)))
    ctx.stream_ready = (ctx.tok("xst"), ctx.tok("ms3"))
    A.pop()


def emit_attn(ctx, dt, xs):
    nc, pb = ctx.nc, ctx.pb
    ctx.barrier()
    A = ctx.arena
    A.push()
    wqkv = A.bf(FB * 3 * C)
    qa = A.bf(FB * T)
    ka = A.bf(FB * T)
    va = A.bf(16 * C)
    mask = A.bf(384)
    ident = A.bf(128)
    onesr = A.bf(128)
    vbias = A.bf(C)
    boutr = A.bf(C)

    tz1 = ctx.op("v", lambda: nc.vector.memset(mask[:, :], 0.0))
    tz2 = ctx.op("v", lambda: nc.vector.memset(ident[:, :], 1.0))
    tz3 = ctx.op("v", lambda: nc.vector.memset(onesr[0:1, :], 1.0))
    ti = ctx.op("g", lambda: nc.gpsimd.affine_select(
        ident[:, :], ident[:, :], pattern=[[-1, 128]], base=0,
        channel_multiplier=1, compare_op=ALU.is_equal, fill=0.0), tz2)
    tm1 = ctx.op("g", lambda: nc.gpsimd.affine_select(
        mask[:, :], mask[:, :], pattern=[[1, 384]], base=-64,
        channel_multiplier=-1, compare_op=ALU.is_ge, fill=-240000.0), tz1)
    tm2 = ctx.op("g", lambda: nc.gpsimd.affine_select(
        mask[:, :], mask[:, :], pattern=[[-1, 384]], base=192,
        channel_multiplier=1, compare_op=ALU.is_ge, fill=-240000.0), tm1)
    ctx.dma("g", "ms2", vbias[0:1, :], dt["bqkvr"].ap()[0:1, 2 * C:3 * C])
    tb1 = tb2 = ctx.dma("g", "ms2", boutr[0:1, :], dt["boutr"].ap())
    ctx.dbg = dict(qa=qa, ka=ka, va=va)
    ctx.dbg2 = {}
    tw = None
    for fb in range(FB):
        tw = ctx.dma("g", "wt0", wqkv[:, fb * 3 * C:(fb + 1) * 3 * C],
                     dt["wqkv"].ap()[fb * 128:(fb + 1) * 128, :])
    ctx.wait("q", *ctx.stream_ready)

    for seq in range(NSEQ):
        A.push()
        xn = [A.bf(CH) for _ in range(FB)]
        xts = [A.f32(CH) for _ in range(FB)]
        ctx.lnsq = A.f32(2 * CH)
        ctx.lnrows = A.f32(3 * CH)
        ebt = {}
        for ch in range(NCH):
            xn_toks, _ = ln_chunk(ctx, seq, ch, xs, xts, xn,
                                  xts_free=tuple(ctx.ln_prior))
            t0 = ch * CH
            for which, colo, dst in (("q", 0, qa), ("k", C, ka)):
                for ob in range(FB):
                    bank = pb[ob % 4]
                    ctx.wait("t", ctx.tok("t"), tw, ebt.get(ob % 4), xn_toks[-1])
                    for fb in range(FB):
                        im = nc.tensor.matmul(
                            bank[:, 0:CH],
                            wqkv[:, fb * 3 * C + colo + ob * 128:
                                 fb * 3 * C + colo + ob * 128 + 128],
                            xn[fb], start=(fb == 0), stop=(fb == FB - 1))
                    tm = ctx.mark("t", im)
                    if which == "q":
                        ev = ctx.op("v", lambda ob=ob, bank=bank, t0=t0:
                                    nc.vector.tensor_scalar(
                                        qa[:, ob * T + t0: ob * T + t0 + CH],
                                        bank[:, 0:CH],
                                        ctx.smal["bqkv"][:, ob:ob + 1], None,
                                        ALU.add), tm)
                    else:
                        ev = ctx.op("v", lambda ob=ob, bank=bank, t0=t0:
                                    nc.vector.tensor_scalar(
                                        ka[:, ob * T + t0: ob * T + t0 + CH],
                                        bank[:, 0:CH],
                                        ctx.smal["bqkv"][:, 8 + ob:9 + ob], None,
                                        ALU.add), tm)
                    ebt[ob % 4] = ev
            for st_ in range(4):
                tt = t0 + st_ * 128
                for nb in range(2):
                    bank = pb[6 + nb]
                    ctx.wait("t", ctx.tok("t"), ebt.get(("v", nb)), tb1, tz3)
                    for fb in range(FB):
                        nc.tensor.matmul(
                            bank[:, 0:CH],
                            xn[fb][:, st_ * 128:(st_ + 1) * 128],
                            wqkv[:, fb * 3 * C + 2 * C + nb * CH:
                                 fb * 3 * C + 2 * C + (nb + 1) * CH],
                            start=(fb == 0), stop=False)
                    im = nc.tensor.matmul(
                        bank[:, 0:CH], onesr[0:1, 0:128],
                        vbias[0:1, nb * CH:(nb + 1) * CH],
                        start=False, stop=True)
                    tm = ctx.mark("t", im)
                    ev = ctx.op("v", lambda tt=tt, nb=nb, bank=bank:
                                nc.vector.tensor_copy(
                                    va[:, (tt // 128) * C + nb * CH:
                                       (tt // 128) * C + (nb + 1) * CH],
                                    bank[:, 0:CH]), tm)
                    ebt[("v", nb)] = ev
        A.pop()
        ctx.barrier()
        A.push()
        wout = A.bf(FB * C)
        asb = A.bf(384)
        asb_b = A.bf(384)
        atb = A.bf(384)
        atb_b = A.bf(384)
        dz = A.bf(128)
        dz_b = A.bf(128)
        scb2 = [A.f32(384), A.f32(384)]
        avb2 = [A.bf(FB * 128) for _ in range(2)]
        zrow = A.f32(4)
        xrt2 = [A.f32(128) for _ in range(2)]
        two = None
        for fb in range(FB):
            two = ctx.dma("g", "wt1", wout[:, fb * C:(fb + 1) * C],
                          dt["wout"].ap()[fb * 128:(fb + 1) * 128, :],
                          deps=(ctx.tok("wt1"),) if fb == 0 else ())
        nih = T // 128
        ebt2 = {}
        ctx.xo_guards = {}
        asb2 = [asb, asb_b]
        atb2 = [atb, atb_b]
        dz2 = [dz, dz_b]
        def mk_outproj(ih, avt):
            def emit():
                i0 = ih * 128
                avb = avb2[ih % 2]
                for ob in range(FB):
                    bank = pb[6 + ob % 2]
                    xrt = xrt2[ob % 2]
                    tld = ctx.dma("q", "ms1", xrt,
                                  xs.ap()[seq][ob * 128:(ob + 1) * 128,
                                               i0:i0 + 128],
                                  deps=(ctx.xo_guards.get(ob % 2),))
                    ctx.wait("t", ctx.tok("t"), ebt2.get(("o", ob % 2)), avt,
                             two)
                    for fb in range(FB):
                        im = nc.tensor.matmul(
                            bank[:, 0:128],
                            wout[:, fb * C + ob * 128: fb * C + ob * 128 + 128],
                            avb[:, fb * 128:(fb + 1) * 128],
                            start=(fb == 0), stop=(fb == FB - 1))
                    tm = ctx.mark("t", im)
                    ta = ctx.op("v", lambda bank=bank, xrt=xrt, ob=ob:
                                nc.vector.scalar_tensor_tensor(
                                    xrt, bank[:, 0:128],
                                    ctx.smal["bout"][:, ob:ob + 1], xrt,
                                    ALU.add, ALU.add), tm, tld)
                    ebt2[("o", ob % 2)] = ta
                    tst = ctx.dma("q", "xst" if ob % 2 == 0 else "ms3",
                                  xs.ap()[seq][ob * 128:(ob + 1) * 128,
                                               i0:i0 + 128],
                                  xrt, deps=(ta,))
                    ctx.xo_guards[ob % 2] = tst
                ctx.avb_guard[ih % 2] = ctx.tok("t")
            return emit

        pending_op = None
        ctx.avb_guard = {}
        for ih in range(nih):
            i0 = ih * 128
            jb0 = max(0, ih - 1)
            jb1 = min(nih, ih + 2)
            kw = (jb1 - jb0) * 128
            moff = 128 * (1 - (ih - jb0))
            avt = None
            st = {}

            def stage_scores(h):
                p = h % 2
                fbq = h // 2
                po = (h % 2) * 64
                sb_ = pb[p]
                ctx.wait("t", ctx.tok("t"), ebt2.get(("s", p)), ti, tm2)
                for jj in range(jb0, jb1):
                    nc.tensor.matmul(
                        sb_[:, (jj - jb0) * 128:(jj - jb0 + 1) * 128],
                        qa[po:po + 64, fbq * T + i0: fbq * T + i0 + 128],
                        ka[po:po + 64, fbq * T + jj * 128: fbq * T + (jj + 1) * 128],
                        start=(jj == jb0), stop=False)
                im = nc.tensor.matmul(
                    sb_[:, 0:kw], ident[:, 0:128],
                    mask[:, moff:moff + kw], start=False, stop=True)
                return ctx.mark("t", im)

            def stage_soft(h, tsc):
                p = h % 2
                sb_ = pb[p]
                scb = scb2[p]
                zr = zrow[:, 2 * p:2 * p + 2]
                tev = ctx.op("s", lambda: nc.scalar.copy(
                    scb[:, 0:kw], sb_[:, 0:kw]), tsc, st.get(("sc", p)))
                ebt2[("s", p)] = tev
                tmx = ctx.op("v", lambda: nc.vector.tensor_reduce(
                    zr[:, 0:1], scb[:, 0:kw], mybir.AxisListType.X,
                    ALU.max), tev, st.get(("zr", p)))
                tsk = ctx.op("v", lambda: nc.vector.tensor_scalar(
                    zr[:, 0:1], zr[:, 0:1], -0.125, None, ALU.mult), tmx)
                tex = ctx.op("s", lambda: nc.scalar.activation(
                    asb2[p][:, 0:kw], scb[:, 0:kw], AF.Exp,
                    bias=zr[:, 0:1], scale=0.125,
                    accum_out=zr[:, 1:2]), tsk, st.get(("as", p)))
                st[("sc", p)] = tex
                tzi = ctx.op("v", lambda: nc.vector.reciprocal(
                    zr[:, 1:2], zr[:, 1:2]), tex)
                tdz = ctx.op("v", lambda: nc.vector.tensor_scalar(
                    dz2[p][:, :], ident[:, :], zr[:, 1:2], None, ALU.mult),
                    tzi, ti, st.get(("dz", p)))
                st[("zr", p)] = tdz
                return tex, tdz

            def stage_trans(h, tex, tdz):
                p = h % 2
                atp = pb[2 + p]
                ctx.wait("t", ctx.tok("t"), ebt2.get(("at", p)), tex, tdz)
                im = None
                for jj in range(jb0, jb1):
                    im = nc.tensor.matmul(
                        atp[:, (jj - jb0) * 128:(jj - jb0 + 1) * 128],
                        asb2[p][:, (jj - jb0) * 128:(jj - jb0 + 1) * 128],
                        dz2[p][:, :], start=(jj == jb0), stop=(jj == jb1 - 1))
                tat = ctx.mark("t", im)
                st[("as", p)] = st[("dz", p)] = tat
                tae = ctx.op("s", lambda: nc.scalar.activation(
                    atb2[p][:, 0:kw], atp[:, 0:kw], AF.Copy), tat,
                    st.get(("ab", p)))
                ebt2[("at", p)] = tae
                return tae

            def stage_av(h, tae):
                p = h % 2
                avp = pb[4 + p]
                ctx.wait("t", ctx.tok("t"), ebt2.get(("av", p)), tae)
                im = None
                for jj in range(jb0, jb1):
                    im = nc.tensor.matmul(
                        avp[0:64, 0:128],
                        va[:, jj * C + h * 64: jj * C + h * 64 + 64],
                        atb2[p][:, (jj - jb0) * 128:(jj - jb0 + 1) * 128],
                        start=(jj == jb0), stop=(jj == jb1 - 1))
                tv_ = ctx.mark("t", im)
                st[("ab", p)] = tv_
                tve = ctx.op("v", lambda h=h, avp=avp, avb=avb2[ih % 2]:
                             nc.vector.tensor_copy(
                                 avb[(h % 2) * 64:(h % 2) * 64 + 64,
                                     (h // 2) * 128:(h // 2) * 128 + 128],
                                 avp[0:64, 0:128]), tv_,
                             ctx.avb_guard.get(ih % 2) if h == 0 else None)
                ebt2[("av", p)] = tve
                return tve

            tscs = {0: stage_scores(0)}
            softs = {}
            taes = {}
            for h in range(16):
                if h + 1 < 16:
                    tscs[h + 1] = stage_scores(h + 1)
                if h == 2 and pending_op is not None:
                    pending_op()
                    pending_op = None
                if h >= 2:
                    taes[h - 2] = stage_trans(h - 2, *softs[h - 2])
                if h >= 3:
                    avt = stage_av(h - 3, taes[h - 3])
                softs[h] = stage_soft(h, tscs[h])
            taes[14] = stage_trans(14, *softs[14])
            avt = stage_av(13, taes[13])
            taes[15] = stage_trans(15, *softs[15])
            avt = stage_av(14, taes[14])
            avt = stage_av(15, taes[15])
            pending_op = mk_outproj(ih, avt)
        if pending_op is not None:
            pending_op()
        A.pop()
        ctx.barrier()
    ctx.stream_ready = (ctx.tok("xst"), ctx.tok("ms3"))
    A.pop()


CAPG = 4096
CAPL = 768
CAPT = CAPL // 128
TROWS = 4096 + NE * CAPT + 16


def emit_moe(ctx, dt, xs, aux):
    nc, pb = ctx.nc, ctx.pb
    ctx.barrier()
    A = ctx.arena
    xnfm, cc_in, cc = aux["xnfm"], aux["cc_in"], aux["cc"]
    tab, dlt = aux["tab"], aux["dlt"]
    NIT = 32
    A.push()
    ctx.moe_lint = A.i32(NE * CAPT)
    MHB = MH // 128
    w1 = A.bf(FB * MH)
    w3 = A.bf(FB * MH)
    w2 = A.bf(MHB * C)
    w1b, w3b, w2b = aux["w1b"], aux["w3b"], aux["w2b"]
    tw1_tok = {}; tw3_tok = {}; tw2_tok = {}

    def load_w1(e, *deps):
        for fb in range(FB):
            ctx.dma("q", "wD" + str(e % 2), w1[:, fb * MH:(fb + 1) * MH],
                    w1b.ap()[e][fb * 128:(fb + 1) * 128, :],
                    deps=deps if fb == 0 else ())
        tw1_tok[e] = ctx.tok("wD" + str(e % 2))

    def load_w3(e, *deps):
        for fb in range(FB):
            ctx.dma("q", "wD" + str(e % 2), w3[:, fb * MH:(fb + 1) * MH],
                    w3b.ap()[e][fb * 128:(fb + 1) * 128, :],
                    deps=deps + (ctx.tok("wD" + str(e % 2)),)
                    if fb == 0 else ())
        tw3_tok[e] = ctx.tok("wD" + str(e % 2))

    def load_w2(e, *deps):
        for hb in range(MHB):
            ctx.dma("q", "wD" + str(e % 2), w2[:, hb * C:(hb + 1) * C],
                    w2b.ap()[e][hb * 128:(hb + 1) * 128, :],
                    deps=deps + (ctx.tok("wD" + str(e % 2)),)
                    if hb == 0 else ())
        tw2_tok[e] = ctx.tok("wD" + str(e % 2))

    identb = A.bf(128)
    slt = A.bf(128)
    jrow = A.f16(CAPL)
    ones16 = A.f16(2)
    Sacc = A.f16(CAPL)
    lrowf = A.f32(CAPL)
    sc32 = A.f32(32); m16 = A.f16(32); c16 = A.f16(32); rtb = A.bf(2)
    jkl = A.f32(16)
    lo = A.f32(NE)
    ctx.identb = identb
    tzi = ctx.op("v", lambda: nc.vector.memset(identb[:, :], 1.0))
    tidb = ctx.op("g", lambda: nc.gpsimd.affine_select(
        identb[:, :], identb[:, :], pattern=[[-1, 128]], base=0,
        channel_multiplier=1, compare_op=ALU.is_equal, fill=0.0), tzi)
    t1s = ctx.op("v", lambda: nc.vector.memset(slt[:, :], 1.0))
    tslt = ctx.op("g", lambda: nc.gpsimd.affine_select(
        slt[:, :], slt[:, :], pattern=[[1, 128]], base=-1,
        channel_multiplier=-1, compare_op=ALU.is_ge, fill=0.0), t1s)
    tjr = ctx.op("g", lambda: nc.gpsimd.iota(
        jrow[:, 0:CAPL], pattern=[[1, CAPL]], base=0, channel_multiplier=0,
        allow_small_or_imprecise_dtypes=True))
    to16 = ctx.op("v", lambda: nc.vector.memset(ones16[:, 0:1], 1.0))

    lb = {"grd": None, "base": None, "lrow": None, "sacc": None, "rtbT": None}

    def lb_part1(e, *deps):
        tsm = ctx.op("v", lambda: nc.vector.memset(Sacc[:, 0:CAPL], 0.0),
                     lb["sacc"], *deps)
        tl = ctx.dma("g", "ms4", sc32[:, :],
                     cc_in.ap()[e].rearrange("(p u) -> p u", p=128),
                     deps=(tsm,))
        tm_ = ctx.op("v", lambda e=e: nc.vector.tensor_scalar(
            m16[:, :], sc32[:, :], lo[:, e:e + 1], None, ALU.is_gt), tl)
        tsc = ctx.op("v", lambda: nc.vector.tensor_tensor_scan(
            c16[:, :], ones16[:, 0:1].to_broadcast([128, 32]), m16[:, :], 0.0,
            ALU.mult, ALU.add), tm_, to16)
        lb["p1"] = ctx.op("v", lambda: nc.vector.tensor_copy(
            rtb[:, 0:1], c16[:, 31:32]), tsc, lb["rtbT"])

    def lb_base(bank):
        ctx.wait("t", ctx.tok("t"), lb["p1"], tslt, lb["grd"])
        ib = nc.tensor.matmul(pb[bank][:, 6:7], slt[:, :], rtb[:, 0:1],
                              start=True, stop=True)
        lb["rtbT"] = lb["base"] = ctx.mark("t", ib)

    def lb_staccs(bank):
        tcf = ctx.op("v", lambda: nc.vector.tensor_scalar(
            c16[:, :], c16[:, :], pb[bank][:, 6:7], None, ALU.add),
            lb["base"])
        lb["grd"] = tcf
        tsa = tcf
        for u in range(32):
            tsa = ctx.op("v", lambda u=u: nc.vector.scalar_tensor_tensor(
                Sacc[:, 0:CAPL], jrow[:, 0:CAPL], c16[:, u:u + 1],
                Sacc[:, 0:CAPL], ALU.is_ge, ALU.add), tsa, tjr)
        lb["stacc"] = tsa

    def lb_lrow_a(bank):
        ctx.wait("t", ctx.tok("t"), lb["stacc"], lb["grd"])
        il = nc.tensor.matmul(pb[bank][0:1, 0:512], ctx.c_ones16,
                              Sacc[:, 0:512], start=True, stop=True)
        tlr = ctx.mark("t", il)
        lb["grd"] = ctx.op("v", lambda: nc.vector.tensor_copy(
            lrowf[0:1, 0:512], pb[bank][0:1, 0:512]), tlr)

    def lb_lrow_b(bank):
        ctx.wait("t", ctx.tok("t"), lb["grd"])
        il = nc.tensor.matmul(pb[bank][0:1, 0:256], ctx.c_ones16,
                              Sacc[:, 512:CAPL], start=True, stop=True)
        tlr = ctx.mark("t", il)
        lb["sacc"] = lb["grd"] = ctx.op(
            "v", lambda: nc.vector.tensor_copy(
                lrowf[0:1, 512:CAPL], pb[bank][0:1, 0:256]), tlr)

    def lb_cols(e, bank):
        Lint = ctx.moe_lint
        ctx.wait("t", ctx.tok("t"), lb["grd"])
        ic = None
        for tt in range(CAPT):
            ic = nc.tensor.matmul(pb[bank][:, tt:tt + 1],
                                  lrowf[0:1, tt * 128:(tt + 1) * 128],
                                  ctx.c_ones[0:1, 0:1], start=True, stop=True)
        tic = ctx.mark("t", ic)
        tov = ctx.op("v", lambda bank=bank: nc.vector.tensor_scalar(
            jkl[:, 0:CAPT], pb[bank][:, 0:CAPT], 4095.5, None, ALU.is_gt),
            tic)
        tof = None
        for tt in range(CAPT):
            tof = ctx.op("v", lambda tt=tt, e=e, bank=bank:
                         nc.vector.scalar_tensor_tensor(
                             jkl[:, 8 + tt:9 + tt], jkl[:, tt:tt + 1],
                             float(e * CAPT + tt), pb[bank][:, tt:tt + 1],
                             ALU.mult, ALU.add), tov)
        lb["grd"] = tof
        lg = ctx.op("v", lambda e=e: nc.vector.tensor_copy(
            Lint[:, e * CAPT:(e + 1) * CAPT], jkl[:, 8:8 + CAPT]), tof)
        ctx.moe_lg_e[e] = lg

    A.push()
    wg = A.bf(FB * NE)
    xn = [A.bf(CH) for _ in range(FB)]
    xts = [A.f32(CH) for _ in range(FB)]
    scrow = A.f32(CH)
    zt = A.bf(1040)
    ztf = A.f32(1024)
    ctx.lnsq = A.f32(2 * CH)
    ctx.lnrows = A.f32(3 * CH)
    twg = None
    for fb in range(FB):
        twg = ctx.dma("g", "ms2", wg[:, fb * NE:(fb + 1) * NE],
                      dt["wg"].ap()[fb * 128:(fb + 1) * 128, :],
                      deps=(ctx.tok("ms2"),) if fb == 0 else ())
    tz0 = ctx.op("v", lambda: nc.vector.memset(zt[:, :], 0.0))
    tz1 = ctx.op("v", lambda: nc.vector.memset(ztf[:, :], 0.0))
    ctx.dma("q", "ms1", tab.ap()[4096:4096 + 64, :], zt[0:64, :], deps=(tz0,))
    for r in range(0, TROWS, 128):
        n = min(128, TROWS - r)
        ctx.dma("q", "ms1", dlt.ap()[r:r + n, :], ztf[0:n, :], deps=(tz1,))
    sct = None
    for seq in range(NSEQ):
        for ch in range(NCH):
            xn_toks, _ = ln_chunk(ctx, seq, ch, xs, xts, xn,
                                  xts_free=tuple(ctx.ln_prior),
                                  apply_deps=(ctx.tok("xst"), ctx.tok("wC0"),
                                              ctx.tok("wC1")))
            for fb in range(FB):
                ctx.dma("q", "xst",
                        xnfm.ap()[seq][fb * 128:(fb + 1) * 128,
                                       ch * CH:(ch + 1) * CH],
                        xn[fb], deps=(xn_toks[fb],))
            ctx.wait("t", ctx.tok("t"), twg, sct, xn_toks[-1])
            for fb in range(FB):
                im = nc.tensor.matmul(pb[0][0:NE, 0:CH],
                                      wg[:, fb * NE:(fb + 1) * NE], xn[fb],
                                      start=(fb == 0), stop=(fb == FB - 1))
            tm = ctx.mark("t", im)
            sct = ctx.op("v", lambda: nc.vector.tensor_copy(
                scrow[0:NE, :], pb[0][0:NE, 0:CH]), tm, ctx.tok("ms1"))
            ctx.dma("q", "ms1", cc_in.ap()[0:NE, seq * T + ch * CH:
                                           seq * T + (ch + 1) * CH],
                    scrow[0:NE, :], deps=(sct,))
    A.pop()
    ctx.barrier()

    load_w1(0, ctx.pcw_tok.get(0))
    load_w3(0, ctx.pcw_tok.get(0))
    load_w2(0, ctx.pcw_tok.get(0))
    A.push()
    scg = A.f32(2048)
    junk = A.f32(2048)
    hic = A.f32(1); midc = A.f32(1); loc = A.f32(1)
    cntc = A.f32(1); mskc = A.i32(1); msk2c = A.i32(1)
    negc = A.f32(1); zc = A.f32(1); zhc = A.f32(1)
    thcol = A.f32(1); nthcol = A.f32(1); zhcol = A.f32(1)
    sloc = A.f32(CH); gloc = A.f32(CH); e1loc = A.f32(CH)
    gtmb = A.bf(4 * NE)
    identf = A.f32(128)
    onesf = A.f32(128)
    g8 = A.f32(128)
    gt16 = A.f32(NE)
    BDs = A.f32(128)
    Lint = ctx.moe_lint
    t2 = ctx.op("v", lambda: nc.vector.memset(identf[:, :], 1.0))
    tidf = ctx.op("g", lambda: nc.gpsimd.affine_select(
        identf[:, :], identf[:, :], pattern=[[-1, 128]], base=0,
        channel_multiplier=1, compare_op=ALU.is_equal, fill=0.0), t2)
    tonf = ctx.op("v", lambda: nc.vector.memset(onesf[:, :], 1.0))
    tg1 = ctx.op("v", lambda: nc.vector.memset(g8[0:NE, :], 1.0))
    tg2 = ctx.op("g", lambda: nc.gpsimd.affine_select(
        g8[0:NE, :], g8[0:NE, :], pattern=[[1, 128]], base=0,
        channel_multiplier=-16, compare_op=ALU.is_ge, fill=0.0), tg1)
    tg3 = ctx.op("g", lambda: nc.gpsimd.affine_select(
        g8[0:NE, :], g8[0:NE, :], pattern=[[-1, 128]], base=15,
        channel_multiplier=16, compare_op=ALU.is_ge, fill=0.0), tg2)
    tg4 = ctx.op("v", lambda: nc.vector.memset(gt16[:, :], 1.0 / 16.0))
    tg5 = ctx.op("g", lambda: nc.gpsimd.affine_select(
        gt16[:, :], gt16[:, :], pattern=[[-16, NE]], base=0,
        channel_multiplier=1, compare_op=ALU.is_ge, fill=0.0), tg4)
    tg6 = ctx.op("g", lambda: nc.gpsimd.affine_select(
        gt16[:, :], gt16[:, :], pattern=[[16, NE]], base=15,
        channel_multiplier=-1, compare_op=ALU.is_ge, fill=0.0), tg5)
    ctx.wait("t", ctx.tok("t"), tg3)
    ibd = nc.tensor.matmul(pb[1][:, 0:128], g8[0:NE, :], g8[0:NE, :],
                           start=True, stop=True)
    tbd0 = ctx.mark("t", ibd)
    tbd = ctx.op("v", lambda: nc.vector.tensor_copy(
        BDs[:, :], pb[1][:, 0:128]), tbd0)
    tcc = ctx.op("g", lambda: nc.gpsimd.collective_compute(
        "AllGather", ALU.bypass, replica_groups=[list(range(NCORES))],
        ins=[cc_in.ap()], outs=[cc.ap()]), ctx.tok("ms1"))
    tld = None
    for e in range(NE):
        for sl in range(NCORES):
            tld = ctx.dma("g", "ms0", scg[e * 16:(e + 1) * 16,
                                          sl * 256:(sl + 1) * 256],
                          cc.ap()[sl][e].rearrange("(p u) -> p u", p=16),
                          deps=(tcc,))
    xnb2 = [[A.bf(CH) for _ in range(FB)] for _ in range(2)]
    xtk2 = [A.bf(1024) for _ in range(2)]
    xnb_guard = {}
    xtk_guard = {}
    tlb = {}
    pbB = {"free": None}

    def pb_load(ci):
        p = ci % 2
        seq, ch = divmod(ci, NCH)
        tl = None
        for fb in range(FB):
            tl = ctx.dma("q", "wB" + str(p), xnb2[p][fb],
                         xnfm.ap()[seq][fb * 128:(fb + 1) * 128,
                                        ch * CH:(ch + 1) * CH],
                         deps=(xnb_guard.get(p),) if fb == 0 else ())
        tlb[ci] = tl

    def pb_unit(ci, sub):
        p = ci % 2
        seq, ch = divmod(ci, NCH)
        sp = sub % 2
        r0 = seq * T + ch * CH + sub * 128
        pA = pb[6].bitcast(BF16)
        pB = pb[7].bitcast(BF16)
        ctx.wait("t", ctx.tok("t"), tlb[ci], tidb, pbB["free"])
        for fb in range(FB):
            bank = pA if fb < 4 else pB
            itp = nc.tensor.transpose(
                bank[:, (fb % 4) * 128:(fb % 4) * 128 + 128],
                xnb2[p][fb][:, sub * 128:(sub + 1) * 128], identb[:, :])
        tm = ctx.mark("t", itp)
        if sub == 3:
            xnb_guard[p] = tm
            if ci + 2 < NSEQ * NCH:
                pb_load(ci + 2)
        ctx.op("s", lambda: nc.scalar.copy(
            xtk2[sp][:, 0:512], pA[:, 0:512]), tm, xtk_guard.get(sp))
        te2 = ctx.op("s", lambda: nc.scalar.copy(
            xtk2[sp][:, 512:1024], pB[:, 0:512]))
        pbB["free"] = te2
        xtk_guard[sp] = ctx.dma("q", "wC" + str(sp),
                                tab.ap()[r0:r0 + 128, 0:1024],
                                xtk2[sp][:, :], deps=(te2,))

    pb_load(0)
    pb_load(1)
    punits = [(ci, sub) for ci in range(NSEQ * NCH) for sub in range(4)]
    pui = [0]

    def pb_drain(n):
        while n > 0 and pui[0] < len(punits):
            ci, sub = punits[pui[0]]
            pb_unit(ci, sub)
            pui[0] += 1
            n -= 1

    ctx.op("v", lambda: nc.vector.memset(loc, -1000.0))
    t0v = ctx.op("v", lambda: nc.vector.memset(hic, 1000.0))
    pbg = {1: tbd}
    for it in range(NIT):
        ctx.op("v", lambda: nc.vector.tensor_add(midc, loc, hic), t0v)
        tmid = ctx.op("v", lambda: nc.vector.tensor_scalar(
            midc, midc, 0.5, None, ALU.mult))
        tc = ctx.op("v", lambda: nc.vector.tensor_scalar(
            junk, scg[:, :], midc, 0.0, ALU.is_gt, ALU.add,
            accum_out=cntc), tmid, tld)
        ctx.wait("t", ctx.tok("t"), tc, tbd, pbg.get(2))
        i3 = nc.tensor.matmul(pb[2][:, 0:1], BDs[:, :], cntc,
                              start=True, stop=True)
        t3 = ctx.mark("t", i3)
        tmk = ctx.op("v", lambda: nc.vector.tensor_scalar(
            mskc, pb[2][:, 0:1], float(CAPG), None, ALU.is_ge), t3)
        tmk2 = ctx.op("v", lambda: nc.vector.tensor_scalar(
            msk2c, pb[2][:, 0:1], float(CAPG), None, ALU.is_lt), t3)
        pbg[2] = tmk2
        ctx.op("v", lambda: nc.vector.select(loc, mskc, midc, loc,
                                             add_drain=True), tmk)
        t0v = ctx.op("v", lambda: nc.vector.select(hic, msk2c, midc, hic,
                                                   add_drain=True), tmk2)
        pb_drain(2)
    pb_drain(64)

    tng = ctx.op("v", lambda: nc.vector.tensor_scalar(
        negc, loc, -1.0, None, ALU.mult), t0v)
    te1 = ctx.op("s", lambda: nc.scalar.activation(
        junk, scg[:, :], AF.Exp, bias=negc), tng)
    tzp = ctx.op("v", lambda: nc.vector.scalar_tensor_tensor(
        junk, scg[:, :], loc, junk, ALU.is_gt, ALU.mult,
        accum_out=zc), te1)
    ctx.wait("t", ctx.tok("t"), tzp, pbg.get(1))
    iz = nc.tensor.matmul(pb[1][:, 0:1], BDs[:, :], zc, start=True, stop=True)
    tzm = ctx.mark("t", iz)
    tzv = ctx.op("v", lambda: nc.vector.tensor_copy(zhc, pb[1][:, 0:1]), tzm)
    tzl = ctx.op("s", lambda: nc.scalar.activation(zhc, zhc, AF.Ln), tzv)
    tze = ctx.op("s", lambda: nc.scalar.activation(zhc, zhc, AF.Exp,
                                                   scale=-1.0), tzl)
    tzh = ctx.op("v", lambda: nc.vector.tensor_scalar(
        zhc, zhc, 0.5, None, ALU.mult), tze)
    ctx.wait("t", ctx.tok("t"), tzh, tng, tg6, pbg.get(3))
    nc.tensor.matmul(pb[3][0:NE, 0:1], gt16[:, :], loc, start=True, stop=True)
    ic2 = nc.tensor.matmul(pb[3][0:NE, 1:2], gt16[:, :], zhc,
                           start=True, stop=True)
    tcm = ctx.mark("t", ic2)
    tthc = ctx.op("v", lambda: nc.vector.tensor_copy(
        thcol[0:NE, :], pb[3][0:NE, 0:1]), tcm)
    tnthc = ctx.op("v", lambda: nc.vector.tensor_scalar(
        nthcol[0:NE, :], thcol[0:NE, :], -1.0, None, ALU.mult), tthc)
    tzhc = ctx.op("v", lambda: nc.vector.tensor_copy(
        zhcol[0:NE, :], pb[3][0:NE, 1:2]))
    ttmp = ctx.op("v", lambda: nc.vector.tensor_scalar(
        junk[:, 0:NE], gt16[:, :], loc, None, ALU.mult), tzhc)
    ctx.wait("t", ctx.tok("t"), ttmp, tonf, pbg.get(2))
    ilo = nc.tensor.matmul(pb[2][:, 0:NE], onesf, junk[:, 0:NE],
                           start=True, stop=True)
    tlo = ctx.mark("t", ilo)
    tcols = ctx.op("v", lambda: nc.vector.tensor_copy(
        lo[:, :], pb[2][:, 0:NE]), tlo)
    pbg[3] = tcols
    tgl = None
    gldg = {}
    ctx.moe_ttg = None
    for seq in range(NSEQ):
        for ch in range(NCH):
            o = seq * T + ch * CH
            tl = ctx.dma("q", "ms1", sloc[0:NE, :], cc_in.ap()[0:NE, o:o + CH],
                         deps=(tgl,))
            te1 = ctx.op("s", lambda: nc.scalar.activation(
                e1loc[0:NE, :], sloc[0:NE, :], AF.Exp,
                bias=nthcol[0:NE, :]), tl, tcols, tgl)
            tg1 = ctx.op("v", lambda: nc.vector.scalar_tensor_tensor(
                gloc[0:NE, :], sloc[0:NE, :], thcol[0:NE, :], e1loc[0:NE, :],
                ALU.is_gt, ALU.mult), te1, ctx.moe_ttg)
            tg2 = ctx.op("v", lambda: nc.vector.tensor_scalar(
                gloc[0:NE, :], gloc[0:NE, :], zhcol[0:NE, :], None, ALU.mult),
                tg1)
            tgl = tg2
            for sub in range(4):
                p = sub % 2
                ctx.wait("t", ctx.tok("t"), tg2, pbg.get(4), tidf)
                it_ = nc.tensor.transpose(
                    pb[4][0:128, 0:NE],
                    gloc[0:NE, sub * 128:(sub + 1) * 128],
                    identf[0:NE, 0:NE])
                tt_ = ctx.mark("t", it_)
                ctx.moe_ttg = tt_
                ghl = gtmb[:, p * 2 * NE:(p + 1) * 2 * NE]
                ctx.op("v", lambda ghl=ghl: nc.vector.tensor_copy(
                    ghl[:, 0:NE], pb[4][:, 0:NE]), tt_, gldg.get(p))
                tgv = ctx.op("v", lambda ghl=ghl: nc.vector.tensor_sub(
                    ghl[:, NE:2 * NE], pb[4][:, 0:NE], ghl[:, 0:NE]))
                pbg[4] = tgv
                gldg[p] = ctx.dma(
                    "q", "wC" + str(p),
                    tab.ap()[o + sub * 128: o + sub * 128 + 128, 1024:1040],
                    ghl, deps=(tgv, ctx.tok("wC" + str(p))))
    ctx.moe_lg_e = {}
    lb["grd"] = pbg.get(4)
    for e in (0, 1):
        lb_part1(e, tcols)
        lb_base(4 + e)
        lb_staccs(4 + e)
        lb_lrow_a(4 + e)
        lb_lrow_b(4 + e)
        lb_cols(e, 4 + e)
    A.pop()
    ctx.barrier()

    A.push()
    WS1 = (512, 256)
    at = A.bf(MHB * CAPL)
    gt2 = [[A.bf(1040) for _ in range(CAPT)] for _ in range(2)]
    xe2 = [A.bf(FB * CAPL) for _ in range(2)]
    ysb = A.f32(1024)
    sgt = A.f32(1)
    ctx.moe_sg = None
    Lint = ctx.moe_lint
    pbg2 = {}
    sc_guard = None
    ctx.moe_ye = {}

    def do_gathers(e):
        p = e % 2
        for tt in range(CAPT):
            ctx.dma("g", "ms2", gt2[p][tt][:, :], tab.ap(),
                    in_offset=IndirectOffsetOnAxis(
                        Lint[:, e * CAPT + tt: e * CAPT + tt + 1], 0),
                    deps=(ctx.moe_lg_e[e], ctx.moe_ye.get(p),
                          ctx.tok("ms0"), ctx.tok("ms2")) if tt == 0 else ())
        return ctx.tok("ms2")

    ctx.xe_guard = {}
    ctx.xe_done = {}
    ctx.xe_pb_free = None

    def mk_xe_tasks(e, tgall):
        p = e % 2
        gt = gt2[p]
        xe = xe2[p]
        tasks = []
        for fb in range(FB):
            for t0, nt in ((0, 4), (4, 2)):
                def task(fb=fb, t0=t0, nt=nt):
                    p4 = pb[4].bitcast(BF16)
                    ctx.wait("t", ctx.tok("t"), tgall, ctx.xe_pb_free, tidb)
                    for i in range(nt):
                        itp = nc.tensor.transpose(
                            p4[:, i * 128:(i + 1) * 128],
                            gt[t0 + i][:, fb * 128:(fb + 1) * 128],
                            identb[:, :])
                    tm = ctx.mark("t", itp)
                    ctx.xe_pb_free = ctx.op(
                        "s", lambda: nc.scalar.copy(
                            xe[:, fb * CAPL + t0 * 128:
                               fb * CAPL + (t0 + nt) * 128],
                            p4[:, 0:nt * 128]), tm, ctx.xe_guard.get(p))
                    ctx.xe_done[p] = ctx.xe_pb_free
                tasks.append(task)
        return tasks

    tgn = do_gathers(0)
    for t_ in mk_xe_tasks(0, tgn):
        t_()
    pend = []
    at_free = None
    for e in range(NE):
        gt = gt2[e % 2]
        xe = xe2[e % 2]
        tgall = tgn
        txe = ctx.xe_done[e % 2]
        le = e + 2 if e + 2 < NE else None
        if le is not None:
            lb_part1(le, tcols)
        if e + 1 < NE:
            tgn = do_gathers(e + 1)
            pend = mk_xe_tasks(e + 1, tgn)
        w1_last = w3_last = None
        for wi, w0 in ((0, 0), (1, 512)):
            W = WS1[wi]
            for hb in range(MHB):
                bank = pb[hb % 2]
                ctx.wait("t", ctx.tok("t"), tw1_tok[e], txe,
                         pbg2.get(hb % 2))
                for fb in range(FB):
                    im = nc.tensor.matmul(
                        bank[:, 0:W],
                        w1[:, fb * MH + hb * 128: fb * MH + hb * 128 + 128],
                        xe[:, fb * CAPL + w0: fb * CAPL + w0 + W],
                        start=(fb == 0), stop=(fb == FB - 1))
                tm = ctx.mark("t", im)
                w1_last = tm
                tse = ctx.op("s", lambda hb=hb, bank=bank, w0=w0, W=W:
                             nc.scalar.activation(
                                 at[:, hb * CAPL + w0: hb * CAPL + w0 + W],
                                 bank[:, 0:W], AF.Silu), tm,
                             pbg2.get(("a", hb, w0)), at_free)
                pbg2[hb % 2] = tse
                pbg2[("a", hb, w0)] = tse
                if le is not None and wi == 0 and hb == 6:
                    lb_base(5)
                if le is not None and wi == 1 and hb == 4:
                    lb_lrow_b(5)
                if le is not None and wi == 1 and hb == 10:
                    lb_cols(le, 5)
            if le is not None and wi == 0:
                lb_staccs(5)
            for hb in range(MHB):
                bank = pb[2 + hb % 2]
                ctx.wait("t", ctx.tok("t"), tw3_tok[e], pbg2.get(2 + hb % 2),
                         pbg2.get(("a", hb, w0)))
                for fb in range(FB):
                    im = nc.tensor.matmul(
                        bank[:, 0:W],
                        w3[:, fb * MH + hb * 128: fb * MH + hb * 128 + 128],
                        xe[:, fb * CAPL + w0: fb * CAPL + w0 + W],
                        start=(fb == 0), stop=(fb == FB - 1))
                tm = ctx.mark("t", im)
                w3_last = tm
                the = ctx.op("v", lambda hb=hb, bank=bank, w0=w0, W=W:
                             nc.vector.tensor_mul(
                                 at[:, hb * CAPL + w0: hb * CAPL + w0 + W],
                                 at[:, hb * CAPL + w0: hb * CAPL + w0 + W],
                                 bank[:, 0:W]), tm, pbg2.get(("h", hb, w0)))
                pbg2[2 + hb % 2] = the
                pbg2[("h", hb, w0)] = the
                if hb % 2 == 1 and pend:
                    pend.pop(0)()
            if le is not None and wi == 0:
                lb_lrow_a(5)
        if e + 1 < NE:
            load_w1(e + 1, w1_last, ctx.pcw_tok.get(e + 1))
            load_w3(e + 1, w3_last)
        w2_last = None
        for tt in range(CAPT):
            for nb in range(2):
                bank = pb[6 + nb]
                ctx.wait("t", ctx.tok("t"), tw2_tok[e], pbg2.get(6 + nb))
                for hb in range(MHB):
                    w0 = 0 if tt < 4 else 512
                    ctx.wait("t", pbg2[("h", hb, w0)])
                    im = nc.tensor.matmul(
                        bank[:, 0:512],
                        at[:, hb * CAPL + tt * 128: hb * CAPL + tt * 128 + 128],
                        w2[:, hb * C + nb * 512: hb * C + (nb + 1) * 512],
                        start=(hb == 0), stop=(hb == MHB - 1))
                tm = ctx.mark("t", im)
                w2_last = tm
                if nb == 0:
                    sgf = ctx.op("v", lambda tt=tt, e=e: nc.vector.tensor_add(
                        sgt, gt[tt][:, 1024 + e:1025 + e],
                        gt[tt][:, 1032 + e:1033 + e]), tgall, ctx.moe_sg)
                tye = ctx.op("v", lambda bank=bank, nb=nb:
                             nc.vector.tensor_scalar(
                                 ysb[:, nb * 512:(nb + 1) * 512],
                                 bank[:, 0:512], sgt,
                                 None, ALU.mult), tm, sc_guard, sgf)
                ctx.moe_sg = tye
                pbg2[6 + nb] = tye
                ctx.moe_ye[e % 2] = tye
            ctx.xe_guard[e % 2] = ctx.tok("t")
            ctx.wait("g", tye, ctx.tok("ms0"))
            sc_guard = ctx.dma("g", "ms0", dlt.ap(), ysb[:, :],
                               out_offset=IndirectOffsetOnAxis(
                                   Lint[:, e * CAPT + tt:
                                        e * CAPT + tt + 1], 0),
                               compute_op=ALU.add)
        if e + 1 < NE:
            load_w2(e + 1, w2_last)
        at_free = w2_last
    A.pop()
    ctx.barrier()

    A.push()
    identf = A.f32(128)
    dl = A.f32(1024)
    xts = [A.f32(CH) for _ in range(FB)]
    t2 = ctx.op("v", lambda: nc.vector.memset(identf[:, :], 1.0))
    tidf = ctx.op("g", lambda: nc.gpsimd.affine_select(
        identf[:, :], identf[:, :], pattern=[[-1, 128]], base=0,
        channel_multiplier=1, compare_op=ALU.is_equal, fill=0.0), t2)
    pbg3 = {}
    stg = []
    dl2 = [dl, A.f32(1024)]
    vtok2 = {}
    dstr = ("ms1", "cc0")
    for seq in range(NSEQ):
        for ch in range(NCH):
            lts = [None, None]
            for fb in range(FB):
                lts[fb % 2] = ctx.dma(
                    "q", "xld" + str(fb % 2), xts[fb],
                    xs.ap()[seq][fb * 128:(fb + 1) * 128,
                                 ch * CH:(ch + 1) * CH],
                    deps=tuple(stg[-1:]) if fb < 2 else ())
            stg = []
            vtok = None
            for tt in range(4):
                p = tt % 2
                r0 = seq * T + ch * CH + tt * 128
                dld = ctx.dma("q", dstr[p], dl2[p][:, :],
                              dlt.ap()[r0:r0 + 128, :],
                              deps=(vtok2.get(p), ctx.tok(dstr[p])))
                for fb in range(FB):
                    ctx.wait("t", ctx.tok("t"), dld, tidf, pbg3.get(fb % 2))
                    itp = nc.tensor.transpose(
                        pb[fb % 2][:, 0:128],
                        dl2[p][:, fb * 128:(fb + 1) * 128], identf[:, :])
                    ttp = ctx.mark("t", itp)
                    vtok = ctx.op("v", lambda fb=fb, tt=tt, p=p:
                                  nc.vector.tensor_add(
                                      xts[fb][:, tt * 128:(tt + 1) * 128],
                                      xts[fb][:, tt * 128:(tt + 1) * 128],
                                      pb[fb % 2][:, 0:128]), ttp, lts[fb % 2])
                    pbg3[fb % 2] = vtok
                vtok2[p] = vtok
            for fb in range(FB):
                stg.append(ctx.dma(
                    "q", "xst", xs.ap()[seq][fb * 128:(fb + 1) * 128,
                                             ch * CH:(ch + 1) * CH],
                    xts[fb], deps=(vtok,)))
    ctx.stream_ready = (ctx.tok("xst"), ctx.tok("ms3"))
    A.pop()
    A.pop()


def build_program(stages=ALL_STAGES):
    nc = bass.Bass("TRN2", target_bir_lowering=False)
    global _LAST_CTX
    dt = {}

    def din(name, shape, dtype=F32):
        dt[name] = nc.dram_tensor(name, list(shape), dtype, kind="ExternalInput")

    din("x", [NSEQ, C, T])
    din("wfc", [C, HID]); din("wpr", [HID, C])
    din("wk", [C, C]); din("wv", [C, C]); din("wr", [C, C]); din("wo", [C, C])
    din("fwk", [C, HID]); din("fwr", [C, C]); din("fwv", [HID, C])
    for nm in ("ln1g", "ln1b", "ln2g", "ln2b", "tdec", "tfirst",
               "tmk", "tmv", "tmr", "ftmk", "ftmr"):
        din(nm, [128, FB])
    din("wqkv", [C, 3 * C]); din("bqkv", [128, 24]); din("bqkvr", [1, 3 * C]); din("boutr", [1, C])
    din("wout", [C, C]); din("bout", [128, FB])
    din("wg", [C, NE])
    din("w1", [NE, C, MH]); din("w3", [NE, C, MH]); din("w2", [NE, MH, C])

    y_out = nc.dram_tensor("y", [NSEQ, C, T], F32, kind="ExternalOutput")
    xs = nc.dram_tensor("xs", [NSEQ, C, T], F32, kind="Internal")
    xs2 = nc.dram_tensor("xs2", [NSEQ, C, T], F32, kind="Internal")
    aux = {
        "xnfm": nc.dram_tensor("xnfm", [NSEQ, C, T], BF16, kind="Internal"),
        "cc_in": nc.dram_tensor("cc_in", [NE, NTOK], F32, kind="Internal"),
        "tab": nc.dram_tensor("tab", [TROWS, 1040], BF16, kind="Internal"),
        "dlt": nc.dram_tensor("dlt", [TROWS, C], F32, kind="Internal"),
        "cc_out": [nc.dram_tensor(f"cc_out{i}", [NCORES, NE, NTOK], F32,
                                  kind="Internal", addr_space="Shared")
                   for i in range(2)],
        "w1b": nc.dram_tensor("w1b", [NE, C, MH], BF16, kind="Internal"),
        "w3b": nc.dram_tensor("w3b", [NE, C, MH], BF16, kind="Internal"),
        "w2b": nc.dram_tensor("w2b", [NE, MH, C], BF16, kind="Internal"),
    }

    ctx = Ctx(nc)
    with contextlib.ExitStack() as stack:
        ctx.init_sems(stack)
        big = stack.enter_context(nc.sbuf_tensor("big", [128, BIG_N], BF16))
        sm = stack.enter_context(nc.sbuf_tensor("sm", [128, 512], F32))
        ctx.pb = [stack.enter_context(nc.psum_tensor(f"pb{i}", [128, 512], F32))
                  for i in range(8)]
        ctx.arena = Arena(big)
        stack.enter_context(nc.Block())

        smal = {}
        off = 0
        names = ("ln1g", "ln1b", "ln2g", "ln2b", "tdec", "tfirst",
                 "tmk", "tmv", "tmr", "ftmk", "ftmr")
        for nm in names:
            smal[nm] = sm[:, off:off + FB]; off += FB
            ctx.dma("q", "ini", smal[nm], dt[nm].ap())
        smal["bqkv"] = sm[:, off:off + 24]; off += 24
        ctx.dma("q", "ini", smal["bqkv"], dt["bqkv"].ap())
        smal["bout"] = sm[:, off:off + FB]; off += FB
        ctx.dma("q", "ini", smal["bout"], dt["bout"].ap())
        tok_small = ctx.tok("ini")
        ctx.c_rsum = sm[:, off:off + 1]; off += 1
        ctx.c_ones = sm[:, off:off + 1]; off += 1
        ctx.c_brow = sm[0:1, off:off + 128]; off += 128
        ctx.c_eps = sm[:, off:off + 1]; off += 1
        ctx.c_o128 = sm[:, off:off + 1]; off += 1
        ctx.c_ones16 = sm.bitcast(F16)[:, 2 * off:2 * off + 1]; off += 1
        nc.vector.memset(ctx.c_rsum, 1.0 / 1024.0)
        nc.vector.memset(ctx.c_ones, 1.0)
        nc.vector.memset(ctx.c_eps, 1e-5)
        nc.vector.memset(ctx.c_o128, 1.0 / 128.0)
        nc.vector.memset(ctx.c_ones16, 1.0)
        tc = ctx.mark("v", nc.vector.memset(ctx.c_brow, 1.0))
        ctx.wait("t", tc)
        ctx.ln_prior = []
        ctx.lnsq_guard = None
        ctx.moe_g_guard = None
        ctx.moe_x_guard = None
        ctx.moe_pb1 = {}
        drv = {"wdec": sm[:, off:off + FB], "eu": sm[:, off + FB:off + 2 * FB]}
        off += 2 * FB
        t1 = ctx.op("s", lambda: nc.scalar.activation(
            drv["wdec"], smal["tdec"], AF.Exp), tok_small)
        t2 = ctx.op("s", lambda: nc.scalar.activation(
            drv["wdec"], drv["wdec"], AF.Exp, scale=-1.0), t1)
        ts = ctx.op("s", lambda: nc.scalar.activation(
            drv["eu"], smal["tfirst"], AF.Exp), tok_small)
        ctx.drv_tok = ts
        ctx.smal = smal
        ctx.drv = drv

        for s_ in range(NSEQ):
            tin = ctx.dma("g", "ms0", xs.ap()[s_], dt["x"].ap()[s_])
        ctx.stream_ready = (tin,)

        ctx.pcw_tok = {}
        if "moe" in stages:
            for e in range(NE):
                ctx.dma("g", "pcw", aux["w1b"].ap()[e], dt["w1"].ap()[e])
                ctx.dma("g", "pcw", aux["w3b"].ap()[e], dt["w3"].ap()[e])
                ctx.dma("g", "pcw", aux["w2b"].ap()[e], dt["w2"].ap()[e])
            for e in range(NE):
                ctx.pcw_tok[e] = ctx.tok("pcw")

        moe_i = 0
        for st in stages:
            if st == "f0":
                emit_ffn(ctx, dt, xs)
            elif st == "rwkv":
                emit_rwkv(ctx, dt, xs, xs2)
            elif st == "attn":
                emit_attn(ctx, dt, xs)
            elif st == "moe":
                emit_moe(ctx, dt, xs, {**aux, "cc": aux["cc_out"][moe_i]})
                moe_i += 1

        _LAST_CTX = ctx
        ctx.barrier()
        ctx.wait("g", ctx.tok("xst"))
        for s_ in range(NSEQ):
            tout = ctx.dma("g", "fin", y_out.ap()[s_], xs.ap()[s_])
        ctx.wait("g", tout)
    return nc


def _prep_inputs(inputs):
    f32 = np.float32
    g = lambda k: np.asarray(inputs[k], dtype=f32)

    def vec128(v):
        return np.ascontiguousarray(np.asarray(v, f32).reshape(-1)
                                    .reshape(FB, 128).T)

    com = {
        "wfc": np.ascontiguousarray(g("L0_ffn_W_fc").T),
        "wpr": np.ascontiguousarray(g("L0_ffn_W_proj").T),
        "wk": np.ascontiguousarray(g("L0_attn_Wk").T),
        "wv": np.ascontiguousarray(g("L0_attn_Wv").T),
        "wr": np.ascontiguousarray(g("L0_attn_Wr").T),
        "wo": np.ascontiguousarray(g("L0_attn_Wo").T),
        "fwk": np.ascontiguousarray(g("L0_attn_fWk").T),
        "fwr": np.ascontiguousarray(g("L0_attn_fWr").T),
        "fwv": np.ascontiguousarray(g("L0_attn_fWv").T),
        "ln1g": vec128(g("L0_attn_ln1_g")), "ln1b": vec128(g("L0_attn_ln1_b")),
        "ln2g": vec128(g("L0_attn_ln2_g")), "ln2b": vec128(g("L0_attn_ln2_b")),
        "tdec": vec128(g("L0_attn_time_decay")),
        "tfirst": vec128(g("L0_attn_time_first")),
        "tmk": vec128(g("L0_attn_tmk")), "tmv": vec128(g("L0_attn_tmv")),
        "tmr": vec128(g("L0_attn_tmr")),
        "ftmk": vec128(g("L0_attn_f_tmk")), "ftmr": vec128(g("L0_attn_f_tmr")),
        "wqkv": np.ascontiguousarray(g("L1_attn_Wqkv")),
        "bqkv": np.ascontiguousarray(g("L1_attn_bqkv").reshape(24, 128).T),
        "wout": np.ascontiguousarray(g("L1_attn_Wout")),
        "bqkvr": np.ascontiguousarray(g("L1_attn_bqkv").reshape(1, 3 * C)),
        "boutr": np.ascontiguousarray(g("L1_attn_bout").reshape(1, C)),
        "bout": vec128(g("L1_attn_bout")),
        "wg": np.ascontiguousarray(g("L1_ffn_Wg").T),
        "w1": np.ascontiguousarray(g("L1_ffn_W1").transpose(0, 2, 1)),
        "w3": np.ascontiguousarray(g("L1_ffn_W3").transpose(0, 2, 1)),
        "w2": np.ascontiguousarray(g("L1_ffn_W2").transpose(0, 2, 1)),
    }
    x = g("L0_attn_x")
    in_maps = []
    for c in range(NCORES):
        m = dict(com)
        m["x"] = np.ascontiguousarray(x[c * NSEQ:(c + 1) * NSEQ].transpose(0, 2, 1))
        in_maps.append(m)
    return in_maps


_LAST_CTX = None
_PROG_CACHE = {}


def kernel(**inputs):
    key = tuple(ALL_STAGES)
    if key not in _PROG_CACHE:
        _PROG_CACHE[key] = build_program(ALL_STAGES)
    nc = _PROG_CACHE[key]
    in_maps = _prep_inputs(inputs)
    res = run_bass_kernel_spmd(nc, in_maps, core_ids=list(range(NCORES)))
    outs = [r["y"].transpose(0, 2, 1) for r in res.results]
    return np.ascontiguousarray(np.concatenate(outs, axis=0)).astype(np.float32)
